# version 08
# speedup vs baseline: 1.4067x; 1.1495x over previous
import contextlib
import numpy as np
import concourse.bass as bass
import concourse.mybir as mybir
from concourse.bass_utils import run_bass_kernel_spmd
from concourse.aot_env import allow_ici_all_to_all

F32 = mybir.dt.float32
BF16 = mybir.dt.bfloat16
AF = mybir.ActivationFunctionType
ALU = mybir.AluOpType
AXX = mybir.AxisListType.X

NCORES = 8
D = 1024
TOK = 2048
T_FULL = 8192
BATCH = 2
NHEAD = 16
DK = 64
D_IN = 2048
N_SSM = 16
DT_RANK = 64
NEXP = 16
H_EXP = 256
HSH = 2048
EPS = 1e-5
NT = TOK // 128
NCB = D // 128

ENGINES = ("tensor", "scalar", "vector", "gpsimd", "sync")
DMA_SEM = {"sync": "dmaS", "gpsimd": "dmaG", "scalar": "dmaA"}



def build_constants(core):
    c = {}
    c["ident"] = np.eye(128, dtype=np.float32)
    k = np.arange(D, dtype=np.float64)
    ang = 2.0 * np.pi * np.outer(k, k) / D
    c["dftC"] = np.cos(ang).astype(np.float32)
    c["dftS"] = np.sin(ang).astype(np.float32)
    c["dftSneg"] = (-np.sin(ang)).astype(np.float32)
    pos0 = (core % 4) * TOK
    t = np.arange(pos0, pos0 + TOK, dtype=np.float64)
    freqs = 1.0 / (10000.0 ** (np.arange(0, DK, 2, dtype=np.float64) / DK))
    angs = np.outer(freqs, t)
    cos2 = np.repeat(np.cos(angs), 2, axis=0)
    sin2 = np.repeat(np.sin(angs), 2, axis=0)
    sgn = np.tile(np.array([-1.0, 1.0]), DK // 2)[:, None]
    c["ropeCos"] = np.tile(cos2, (2, 1)).astype(np.float32)
    c["ropeSin"] = np.tile(sin2 * sgn, (2, 1)).astype(np.float32)
    for j in range(4):
        kk = np.arange(128)[:, None] + 128 * j
        qq = np.arange(512)[None, :]
        c[f"cmask{j}"] = (kk <= qq).astype(np.float32)
    c["ones1x128"] = np.ones((1, 128), dtype=np.float32)
    c["epscol"] = np.full((128, 1), EPS, dtype=np.float32)
    c["onecol"] = np.ones((128, 1), dtype=np.float32)
    r0 = core * 128
    c["dftC_r"] = np.ascontiguousarray(c["dftC"][:, r0:r0 + 128])
    c["dftS_r"] = np.ascontiguousarray(c["dftS"][:, r0:r0 + 128])
    c["dftSn_r"] = np.ascontiguousarray(c["dftSneg"][:, r0:r0 + 128])
    return c

CONST_SPECS = [
    ("ident", [128, 128], F32),
    ("dftC", [D, D], F32),
    ("dftS", [D, D], F32),
    ("dftSneg", [D, D], F32),
    ("ropeCos", [128, TOK], F32),
    ("ropeSin", [128, TOK], F32),
    ("cmask0", [128, 512], F32),
    ("cmask1", [128, 512], F32),
    ("cmask2", [128, 512], F32),
    ("cmask3", [128, 512], F32),
    ("ones1x128", [1, 128], F32),
    ("epscol", [128, 1], F32),
    ("onecol", [128, 1], F32),
    ("dftC_r", [D, 128], F32),
    ("dftS_r", [D, 128], F32),
    ("dftSn_r", [D, 128], F32),
]



class Buf:
    __slots__ = ("ten", "wr", "rds", "name")

    def __init__(self, ten, name):
        self.ten = ten
        self.name = name
        self.wr = None
        self.rds = []

    def __getitem__(self, idx):
        return self.ten[idx]

    @property
    def ap(self):
        return self.ten


class Sched:

    def __init__(self, nc):
        self.nc = nc
        self.steps = {e: [] for e in ENGINES}
        self.sem_handles = {}
        self.counts = {}
        self._ctx = contextlib.ExitStack()

    def sem(self, name):
        if name not in self.sem_handles:
            self.sem_handles[name] = self._ctx.enter_context(self.nc.semaphore(name))
            self.counts[name] = 0
        return self.sem_handles[name]

    fence = None

    def all_tokens(self):
        return [(s, v) for s, v in self.counts.items() if v > 0]

    def set_fence(self):
        self.fence = self.all_tokens()

    def _record(self, engine, fn, waits, sem, inc):
        wmap = {}
        for w in waits:
            if w is None:
                continue
            s, v = w
            if s not in wmap or wmap[s] < v:
                wmap[s] = v
        wl = sorted(wmap.items())
        tok = None
        if sem is not None:
            self.sem(sem)
            self.counts[sem] += inc
            tok = (sem, self.counts[sem])
        self.steps[engine].append((wl, fn, tok, inc if sem else 0))
        return tok

    def step(self, engine, fn, reads=(), writes=(), waits=(), sem="x", inc=1):
        if sem == "x":
            sem = {"tensor": "pe", "scalar": "se", "vector": "ve",
                   "gpsimd": "ge", "sync": "sy"}[engine]
        wl = list(waits)
        for b in reads:
            if b.wr is not None:
                wl.append(b.wr)
        for b in writes:
            if b.wr is None and b.rds == [] and self.fence:
                wl.extend(self.fence)
            if b.wr is not None:
                wl.append(b.wr)
            wl.extend(b.rds)
        tok = self._record(engine, fn, wl, sem, inc)
        for b in reads:
            if tok is not None:
                b.rds.append(tok)
        for b in writes:
            b.wr = tok
            b.rds = []
        return tok

    DMA_LANES = 4

    def dma(self, queue, out, in_, reads=(), writes=(), waits=()):
        if not hasattr(self, "dma_lane"):
            self.dma_lane = {}
        lane = self.dma_lane.get(queue, 0)
        self.dma_lane[queue] = (lane + 1) % self.DMA_LANES
        sname = DMA_SEM[queue] + str(lane)
        self.sem(sname)
        prev = self.counts[sname]
        wl = list(waits) + ([(sname, prev)] if prev else [])
        return self.step(
            queue, lambda e, o=out, i=in_: e.dma_start(out=o, in_=i),
            reads=reads, writes=writes, waits=wl, sem=sname, inc=16)

    def dma_barrier(self):
        toks = []
        for q in DMA_SEM.values():
            for lane in range(self.DMA_LANES):
                sname = q + str(lane)
                if self.counts.get(sname):
                    toks.append((sname, self.counts[sname]))
        return toks

    def mm_group(self, items, out_buf, reads=(), waits=(), psum_slice=None):
        n = len(items)
        wl = list(waits)
        for b in reads:
            if b.wr is not None:
                wl.append(b.wr)
        if out_buf.wr is None and out_buf.rds == [] and self.fence:
            wl.extend(self.fence)
        if out_buf.wr is not None:
            wl.append(out_buf.wr)
        wl.extend(out_buf.rds)
        tok = None
        for i, (lhs, rhs) in enumerate(items):
            dst = out_buf.ten[psum_slice] if psum_slice is not None else out_buf.ten[:, :]
            tok = self._record(
                "tensor",
                lambda e, l=lhs, r=rhs, d=dst, s=(i == 0), p=(i == n - 1): e.matmul(
                    d, l, r, start=s, stop=p),
                wl if i == 0 else (), "pe" if i == n - 1 else None, 1)
        for b in reads:
            b.rds.append(tok)
        out_buf.wr = tok
        out_buf.rds = []
        return tok

    def barrier_token(self, name):
        self.sem(name)
        return (name, self.counts[name])

    def build(self, block):
        for engine in ENGINES:
            steps = self.steps[engine]
            if not steps:
                continue
            dec = getattr(block, engine)

            def _make(_steps):
                def _body(eng):
                    for wl, fn, tok, inc in _steps:
                        for (sname, val) in wl:
                            eng.wait_ge(self.sem_handles[sname], val)
                        ins = fn(eng)
                        if tok is not None:
                            ins.then_inc(self.sem_handles[tok[0]], inc)
                return _body
            dec(_make(steps))

    def close(self):
        self._ctx.close()


class Mem:

    def __init__(self, nc, ctx):
        self.nc = nc
        self.ctx = ctx

    def sb(self, name, shape, dt=F32):
        return Buf(self.ctx.enter_context(self.nc.sbuf_tensor(name, shape, dt)), name)

    def ps(self, name, shape, dt=F32):
        return Buf(self.ctx.enter_context(self.nc.psum_tensor(name, shape, dt)), name)

    @contextlib.contextmanager
    def scope(self, S=None):
        with contextlib.ExitStack() as es:
            yield Mem(self.nc, es)
        if S is not None:
            S.set_fence()


def ln_emit(S, mem, x_ap, out_ap, stat, sq, epscol, reads, writes, extra_waits=(), rows=128):
    st = stat.ten[0:rows, :]
    S.step("scalar", lambda e: e.activation(sq.ten[0:rows, :], x_ap, AF.Square),
           reads=reads, writes=[sq], waits=extra_waits)
    S.step("vector", lambda e: e.reduce_sum(st[:, 0:1], x_ap, axis=AXX),
           reads=reads, writes=[stat], waits=extra_waits)
    S.step("vector", lambda e: e.reduce_sum(st[:, 1:2], sq.ten[0:rows, :], axis=AXX),
           reads=[sq], writes=[stat])
    S.step("vector", lambda e: e.tensor_scalar(st[:, 2:4], st[:, 0:2], 1.0 / D, None, ALU.mult),
           reads=[stat], writes=[stat])
    S.step("vector", lambda e: e.tensor_mul(st[:, 4:5], st[:, 2:3], st[:, 2:3]),
           reads=[stat], writes=[stat])
    S.step("vector", lambda e: e.tensor_sub(st[:, 5:6], st[:, 3:4], st[:, 4:5]),
           reads=[stat], writes=[stat])
    S.step("scalar", lambda e: e.activation(st[:, 6:7], st[:, 5:6], AF.Sqrt, bias=epscol.ten[0:rows, 0:1]),
           reads=[stat, epscol], writes=[stat])
    S.step("vector", lambda e: e.reciprocal(st[:, 7:8], st[:, 6:7]),
           reads=[stat], writes=[stat])
    return S.step("vector", lambda e: e.tensor_scalar(
        out_ap, x_ap, st[:, 2:3], st[:, 7:8], ALU.subtract, ALU.mult),
        reads=reads + [stat], writes=writes)


def transpose_tiles(S, mem, src_buf, src_col0, dst_buf, dst_col0, nrow_tiles,
                    ncol_tiles, src_stride, dst_stride, ident, pst, dt_bytes=2):
    for r in range(nrow_tiles):
        for c in range(ncol_tiles):
            p = pst[(r * ncol_tiles + c) % len(pst)]
            s_ap = src_buf[:, src_col0 + r * src_stride + c * 128:
                           src_col0 + r * src_stride + c * 128 + 128]
            d_ap = dst_buf[:, dst_col0 + c * dst_stride + r * 128:
                           dst_col0 + c * dst_stride + r * 128 + 128]
            S.step("tensor", lambda e, s=s_ap, p=p: e.transpose(p.ten[:, :], s, ident.ten[:, :]),
                   reads=[src_buf, ident], writes=[p])
            S.step("scalar", lambda e, d=d_ap, p=p: e.copy(d, p.ten[:, :]),
                   reads=[p], writes=[dst_buf])



def emit_p0(S, mem, ios):
    cst = ios["const"]
    with mem.scope(S) as m1:
        xlnT = m1.sb("p0_xlnT", [128, NCB * TOK], BF16)
        x_t = [m1.sb(f"p0_x{i}", [128, D]) for i in range(2)]
        sq = m1.sb("p0_sq", [128, D])
        xln_f = m1.sb("p0_xlnf", [128, D])
        xln_bf = m1.sb("p0_xlnbf", [128, D], BF16)
        stat = m1.sb("p0_stat", [128, 8])
        rope_c = m1.sb("p0_ropec", [128, TOK], BF16)
        rope_s = m1.sb("p0_ropes", [128, TOK], BF16)
        w_s = m1.sb("p0_w", [128, NCB * D], BF16)
        wT = m1.sb("p0_wT", [128, NCB * D], BF16)
        rot_t = [m1.sb(f"p0_rot{i}", [128, TOK], BF16) for i in range(2)]
        v_t = [m1.sb(f"p0_v{i}", [128, D], BF16) for i in range(2)]
        qtmp = m1.sb("p0_qtmp", [128, 512], BF16)
        qsw = m1.sb("p0_qsw", [128, 512], BF16)
        qm1 = m1.sb("p0_qm1", [128, 512], BF16)
        ident_b = m1.sb("p0_identb", [128, 128], BF16)
        psA = m1.ps("p0_psA", [128, 512])
        psB = m1.ps("p0_psB", [128, 512])
        psT = [m1.ps("p0_psT1", [128, 128], BF16), m1.ps("p0_psT2", [128, 128], BF16)]

        S.dma("gpsimd", ident_b[:, :], cst["ident"], writes=[ident_b])
        epscol = m1.sb("p0_eps", [128, 1])
        S.dma("gpsimd", epscol[:, :], cst["epscol"], writes=[epscol])
        S.dma("gpsimd", rope_c[:, :], cst["ropeCos"], writes=[rope_c])
        S.dma("gpsimd", rope_s[:, :], cst["ropeSin"], writes=[rope_s])

        for ti in range(NT):
            xt = x_t[ti % 2]
            S.dma("sync", xt[:, :], ios["x"][ti * 128:(ti + 1) * 128, :], writes=[xt])
            ln_emit(S, m1, xt[:, :], xln_f[:, :], stat, sq, epscol, reads=[xt], writes=[xln_f])
            S.step("vector", lambda e: e.tensor_copy(xln_bf.ten[:, :], xln_f.ten[:, :]),
                   reads=[xln_f], writes=[xln_bf])
            for cb in range(NCB):
                p = psT[cb % 2]
                S.step("tensor", lambda e, cb=cb, p=p: e.transpose(
                    p.ten[:, :], xln_bf.ten[:, cb * 128:(cb + 1) * 128], ident_b.ten[:, :]),
                    reads=[xln_bf, ident_b], writes=[p])
                dst = xlnT[:, cb * TOK + ti * 128: cb * TOK + (ti + 1) * 128]
                S.step("scalar", lambda e, dst=dst, p=p: e.copy(dst, p.ten[:, :]),
                       reads=[p], writes=[xlnT])

        for wi_, wname in enumerate(("wq", "wk", "wv")):
            S.dma("gpsimd", w_s.ten.ap().rearrange("p (n c) -> p n c", n=NCB),
                  ios[wname].rearrange("(n p) c -> p n c", p=128), writes=[w_s])
            transpose_tiles(S, m1, w_s, 0, wT, 0, NCB, NCB, D, D, ident_b, psT)
            if wname != "wv":
                rot_out = ios["q_a2a"] if wname == "wq" else ios["k_a2a"]
                for dqb in range(NCB):
                    rt = rot_t[dqb % 2]
                    for tc in range(4):
                        psum = psA if tc % 2 == 0 else psB
                        items = [(wT[:, cb * D + dqb * 128: cb * D + (dqb + 1) * 128],
                                  xlnT[:, cb * TOK + tc * 512: cb * TOK + (tc + 1) * 512])
                                 for cb in range(NCB)]
                        S.mm_group(items, psum, reads=[wT, xlnT])
                        sl = slice(tc * 512, (tc + 1) * 512)
                        S.step("scalar", lambda e, psum=psum: e.copy(qtmp.ten[:, :], psum.ten[:, :]),
                               reads=[psum], writes=[qtmp])
                        S.step("vector", lambda e: e.stream_shuffle(
                            qsw.ten[:, :], qtmp.ten[:, :], [i ^ 1 for i in range(32)]),
                            reads=[qtmp], writes=[qsw])
                        S.step("vector", lambda e, sl=sl: e.tensor_mul(
                            qm1.ten[:, :], qtmp.ten[:, :], rope_c.ten[:, sl]),
                            reads=[qtmp, rope_c], writes=[qm1])
                        S.step("vector", lambda e, sl=sl: e.tensor_mul(
                            qsw.ten[:, :], qsw.ten[:, :], rope_s.ten[:, sl]),
                            reads=[qsw, rope_s], writes=[qsw])
                        S.step("vector", lambda e, rt=rt, sl=sl: e.tensor_add(
                            rt.ten[:, sl], qm1.ten[:, :], qsw.ten[:, :]),
                            reads=[qm1, qsw], writes=[rt])
                    S.dma("sync", rot_out[dqb, :, :], rt[:, :], reads=[rt])
            else:
                for ti in range(NT):
                    vt = v_t[ti % 2]
                    for hc in range(2):
                        psum = psA if hc == 0 else psB
                        items = [(xlnT[:, cb * TOK + ti * 128: cb * TOK + (ti + 1) * 128],
                                  wT[:, cb * D + hc * 512: cb * D + (hc + 1) * 512])
                                 for cb in range(NCB)]
                        S.mm_group(items, psum, reads=[xlnT, wT])
                        dst = vt[:, hc * 512:(hc + 1) * 512]
                        S.step("scalar", lambda e, dst=dst, psum=psum: e.copy(dst, psum.ten[:, :]),
                               reads=[psum], writes=[vt])
                    S.dma("sync",
                          ios["v_a2a"][:, ti * 128:(ti + 1) * 128, :].rearrange("j p c -> p j c"),
                          vt.ten.ap().rearrange("p (j c) -> p j c", j=NCORES), reads=[vt])
        t_qkv_done = S.dma_barrier()

    with mem.scope(S) as m2:
        Cs = m2.sb("p0_C", [128, NCB * D])
        Sn = m2.sb("p0_Sn", [128, NCB * D])
        Crs = m2.sb("p0_Cr", [128, NCB * 128])
        Srs = m2.sb("p0_Sr", [128, NCB * 128])
        Snrs = m2.sb("p0_Snr", [128, NCB * 128])
        wr_s = m2.sb("p0_wr", [128, NCB * D])
        wi_s = m2.sb("p0_wi", [128, NCB * D])
        PTs = m2.sb("p0_PT", [128, NCB * 128])
        QTs = m2.sb("p0_QT", [128, NCB * 128])
        m_out = m2.sb("p0_m", [128, D])
        psA = m2.ps("p0_psA2", [128, 512])
        psB = m2.ps("p0_psB2", [128, 512])
        S.dma("sync", Cs.ten.ap().rearrange("p (n o) -> p n o", n=NCB),
              cst["dftC"].rearrange("(n p) o -> p n o", p=128),
              writes=[Cs], waits=t_qkv_done)
        S.dma("sync", Sn.ten.ap().rearrange("p (n o) -> p n o", n=NCB),
              cst["dftSneg"].rearrange("(n p) o -> p n o", p=128),
              writes=[Sn], waits=t_qkv_done)
        S.dma("sync", Crs.ten.ap().rearrange("p (n o) -> p n o", n=NCB),
              cst["dftC_r"].rearrange("(n p) o -> p n o", p=128),
              writes=[Crs], waits=t_qkv_done)
        S.dma("sync", Srs.ten.ap().rearrange("p (n o) -> p n o", n=NCB),
              cst["dftS_r"].rearrange("(n p) o -> p n o", p=128),
              writes=[Srs], waits=t_qkv_done)
        S.dma("sync", Snrs.ten.ap().rearrange("p (n o) -> p n o", n=NCB),
              cst["dftSn_r"].rearrange("(n p) o -> p n o", p=128),
              writes=[Snrs], waits=t_qkv_done)
        S.dma("gpsimd", wr_s.ten.ap().rearrange("p (n o) -> p n o", n=NCB),
              ios["w_freq_real"].rearrange("(n p) o -> p n o", p=128),
              writes=[wr_s], waits=t_qkv_done)
        S.dma("gpsimd", wi_s.ten.ap().rearrange("p (n o) -> p n o", n=NCB),
              ios["w_freq_imag"].rearrange("(n p) o -> p n o", p=128),
              writes=[wi_s], waits=t_qkv_done)
        for dstT, wa, wb, sb_b in ((PTs, wr_s, wi_s, Srs), (QTs, wi_s, wr_s, Snrs)):
            for kb in range(NCB):
                psum = psA if kb % 2 == 0 else psB
                items = []
                for db in range(NCB):
                    items.append((wa[:, db * D + kb * 128: db * D + (kb + 1) * 128],
                                  Crs[:, db * 128:(db + 1) * 128]))
                    items.append((wb[:, db * D + kb * 128: db * D + (kb + 1) * 128],
                                  sb_b[:, db * 128:(db + 1) * 128]))
                S.mm_group(items, psum, reads=[wa, wb, Crs, sb_b],
                           psum_slice=(slice(None), slice(0, 128)))
                S.step("scalar", lambda e, kb=kb, psum=psum, dstT=dstT: e.copy(
                    dstT[:, kb * 128:(kb + 1) * 128], psum.ten[:, 0:128]),
                    reads=[psum], writes=[dstT])
        for oc in range(2):
            psum = psA if oc == 0 else psB
            items = []
            for kb in range(NCB):
                items.append((PTs[:, kb * 128:(kb + 1) * 128],
                              Cs[:, kb * D + oc * 512: kb * D + (oc + 1) * 512]))
                items.append((QTs[:, kb * 128:(kb + 1) * 128],
                              Sn[:, kb * D + oc * 512: kb * D + (oc + 1) * 512]))
            S.mm_group(items, psum, reads=[PTs, QTs, Cs, Sn])
            S.step("scalar", lambda e, oc=oc, psum=psum: e.mul(
                m_out[:, oc * 512:(oc + 1) * 512], psum.ten[:, :], 1.0 / D),
                reads=[psum], writes=[m_out])
        S.dma("sync", ios["m_slice"], m_out[:, :], reads=[m_out])




def build_p0():
    nc = bass.Bass("TRN2", target_bir_lowering=False, debug=False)
    ios = {"const": {}}
    for name, shape, dt in CONST_SPECS:
        ios["const"][name] = nc.dram_tensor("c_" + name, shape, dt, kind="ExternalInput").ap()
    ios["x"] = nc.dram_tensor("x", [TOK, D], F32, kind="ExternalInput").ap()
    for w in ("wq", "wk", "wv"):
        ios[w] = nc.dram_tensor(w, [D, D], F32, kind="ExternalInput").ap()
    ios["w_freq_real"] = nc.dram_tensor("w_freq_real", [D, D], F32, kind="ExternalInput").ap()
    ios["w_freq_imag"] = nc.dram_tensor("w_freq_imag", [D, D], F32, kind="ExternalInput").ap()
    ios["q_a2a"] = nc.dram_tensor("q_a2a", [NCORES, 128, TOK], BF16, kind="ExternalOutput").ap()
    ios["k_a2a"] = nc.dram_tensor("k_a2a", [NCORES, 128, TOK], BF16, kind="ExternalOutput").ap()
    ios["v_a2a"] = nc.dram_tensor("v_a2a", [NCORES, TOK, 128], BF16, kind="ExternalOutput").ap()
    ios["m_slice"] = nc.dram_tensor("m_slice", [128, D], F32, kind="ExternalOutput").ap()
    with contextlib.ExitStack() as ctx:
        mem = Mem(nc, ctx)
        S = Sched(nc)
        emit_p0(S, mem, ios)
        end_tok = S.dma_barrier()
        S.step("gpsimd", lambda e: e.nop(), waits=end_tok, sem=None)
        block = ctx.enter_context(nc.Block())
        S.build(block)
    S.close()
    return nc





def emit_p1(S, mem, ios, units=4, qcs=16):
    cst = ios["const"]
    NKT = T_FULL // 128
    nbatch = max(1, units // 2)
    with mem.scope(S) as m1:
        qt_h = [m1.sb(f"p1_qt{h}", [64, T_FULL], BF16) for h in range(2)]
        kt_h = [m1.sb(f"p1_kt{h}", [64, T_FULL], BF16) for h in range(2)]
        vs_h = [m1.sb(f"p1_vs{h}", [128, NKT * 65], BF16) for h in range(2)]
        pt = [m1.sb(f"p1_pt{i}", [128, 512], BF16) for i in range(4)]
        masks = [m1.sb(f"p1_mask{j}", [128, 512], BF16) for j in range(4)]
        ones_s = m1.sb("p1_ones", [128, 64])
        den_s = m1.sb("p1_den", [128, 512])
        rec_sb = [m1.sb(f"p1_rec{h}", [64, 512]) for h in range(2)]
        outT = [m1.sb(f"p1_outT{h}", [64, 512], BF16) for h in range(2)]
        psS = [m1.ps(f"p1_psS{i}", [128, 512]) for i in range(4)]
        psO = [m1.ps(f"p1_psO{h}", [128, 512]) for h in range(2)]
        psR = [m1.ps(f"p1_psR{h}", [128, 512]) for h in range(2)]

        for j in range(4):
            S.dma("gpsimd", masks[j][:, :], cst[f"cmask{j}"], writes=[masks[j]])
        S.step("vector", lambda e: e.memset(ones_s.ten[:, :], 1.0), writes=[ones_s])

        for b in range(nbatch):
            for hl in range(2):
                qt = qt_h[hl]
                kt = kt_h[hl]
                S.dma("sync", qt.ten.ap().rearrange("p (s t) -> p s t", s=4),
                      ios["qg"][4 * b:4 * b + 4, 64 * hl:64 * hl + 64, :]
                          .rearrange("s p t -> p s t"),
                      writes=[qt])
                S.dma("sync", kt.ten.ap().rearrange("p (s t) -> p s t", s=4),
                      ios["kg"][4 * b:4 * b + 4, 64 * hl:64 * hl + 64, :]
                          .rearrange("s p t -> p s t"),
                      writes=[kt])
                vs = vs_h[hl]
                S.dma("sync",
                      vs.ten.ap().rearrange("p (n f) -> p n f", f=65)[:, :, 0:64]
                          .rearrange("p (s n) c -> p s n c", s=4),
                      ios["vg"][4 * b:4 * b + 4, :, 64 * hl:64 * hl + 64]
                          .rearrange("s (n p) c -> p s n c", p=128),
                      writes=[vs])
                S.step("vector", lambda e, vs=vs: e.memset(
                    vs.ten.ap().rearrange("p (n f) -> p n f", f=65)[:, :, 64:65], 1.0),
                    reads=[ones_s], writes=[vs])

            for qc in range(qcs):
                q0 = qc * 512
                nk = (q0 + 512) // 128
                first_pv = [True, True]

                def pvemit(kti):
                    for hl in range(2):
                        vs = vs_h[hl]
                        po = psO[hl]
                        p = pt[2 * hl + kti % 2]
                        wl = []
                        if first_pv[hl]:
                            if po.wr is not None:
                                wl.append(po.wr)
                            wl.extend(po.rds)
                            if vs.wr is not None:
                                wl.append(vs.wr)
                        if p.wr is not None:
                            wl.append(p.wr)
                        tokpv = S._record(
                            "tensor",
                            lambda e, kti=kti, p=p, po=po, vs=vs, s=first_pv[hl],
                                   last=(kti == nk - 1): e.matmul(
                                po.ten[0:65, :], vs.ten[:, kti * 65:kti * 65 + 65],
                                p.ten[:, :], start=s, stop=last, skip_group_check=True),
                            wl, "pe", 1)
                        if kti == nk - 1:
                            po.wr = tokpv
                            po.rds = []
                        p.rds.append(tokpv)
                        vs.rds.append(tokpv)
                        first_pv[hl] = False

                for kti in range(nk):
                    for hl in range(2):
                        qt = qt_h[hl]
                        kt = kt_h[hl]
                        ss = psS[2 * hl + kti % 2]
                        S.mm_group([(kt[:, kti * 128:(kti + 1) * 128],
                                     qt[:, q0:q0 + 512])],
                                   ss, reads=[kt, qt])
                    for hl in range(2):
                        ss = psS[2 * hl + kti % 2]
                        p = pt[2 * hl + kti % 2]
                        S.step("scalar", lambda e, ss=ss, p=p: e.activation(
                            p.ten[:, :], ss.ten[:, :], AF.Exp, scale=0.125),
                            reads=[ss], writes=[p])
                        j = kti - q0 // 128
                        if j >= 0:
                            S.step("vector", lambda e, p=p, j=j: e.tensor_mul(
                                p.ten[:, :], p.ten[:, :], masks[j].ten[:, :]),
                                reads=[p, masks[j]], writes=[p])
                    if kti > 0:
                        pvemit(kti - 1)
                pvemit(nk - 1)
                for hl in range(2):
                    po = psO[hl]
                    rec = rec_sb[hl]
                    ot = outT[hl]
                    dr = 64 * hl
                    S.step("scalar", lambda e, po=po, dr=dr: e.copy(
                        den_s.ten[dr + 0:dr + 1, :], po.ten[64:65, :]),
                        reads=[po], writes=[den_s])
                    S.step("vector", lambda e, dr=dr: e.reciprocal(
                        den_s.ten[dr + 0:dr + 1, :], den_s.ten[dr + 0:dr + 1, :]),
                        reads=[den_s], writes=[den_s])
                    S.mm_group([(ones_s[dr:dr + 1, 0:64], den_s[dr:dr + 1, :])],
                               psR[hl], reads=[ones_s, den_s],
                               psum_slice=(slice(0, 64), slice(None)))
                    S.step("scalar", lambda e, rec=rec, hl=hl: e.copy(
                        rec.ten[:, :], psR[hl].ten[0:64, :]),
                        reads=[psR[hl]], writes=[rec])
                    S.step("vector", lambda e, ot=ot, po=po, rec=rec: e.tensor_mul(
                        ot.ten[:, :], po.ten[0:64, :], rec.ten[:, :]),
                        reads=[po, rec], writes=[ot])
                    blk = 4 * b + q0 // TOK
                    S.dma("sync",
                          ios["ao_a2a"][blk, 64 * hl:64 * hl + 64,
                                        (q0 % TOK):(q0 % TOK) + 512],
                          ot[:, :], reads=[ot])


def build_p1(units=4, qcs=16):
    nc = bass.Bass("TRN2", target_bir_lowering=False, debug=False)
    ios = {"const": {}}
    for name, shape, dt in CONST_SPECS:
        ios["const"][name] = nc.dram_tensor("c_" + name, shape, dt, kind="ExternalInput").ap()
    ios["qg"] = nc.dram_tensor("qg", [NCORES, 128, TOK], BF16, kind="ExternalInput").ap()
    ios["kg"] = nc.dram_tensor("kg", [NCORES, 128, TOK], BF16, kind="ExternalInput").ap()
    ios["vg"] = nc.dram_tensor("vg", [NCORES, TOK, 128], BF16, kind="ExternalInput").ap()
    ios["ao_a2a"] = nc.dram_tensor("ao_a2a", [NCORES, 128, TOK], BF16, kind="ExternalOutput").ap()
    with contextlib.ExitStack() as ctx:
        mem = Mem(nc, ctx)
        S = Sched(nc)
        emit_p1(S, mem, ios, units, qcs)
        end_tok = S.dma_barrier()
        S.step("gpsimd", lambda e: e.nop(), waits=end_tok, sem=None)
        block = ctx.enter_context(nc.Block())
        S.build(block)
    S.close()
    return nc



def emit_p2(S, mem, ios):
    cst = ios["const"]
    with mem.scope(S) as m1:
        attnT = m1.sb("p2_attnT", [128, NCB * TOK], BF16)
        w_s = m1.sb("p2_w", [128, NCB * D], BF16)
        woT = m1.sb("p2_woT", [128, NCB * D], BF16)
        m_s = m1.sb("p2_m", [128, NCB * D], BF16)
        bias_s = m1.sb("p2_bias", [1, D], BF16)
        ones_s = m1.sb("p2_ones", [1, 128], BF16)
        ident_b = m1.sb("p2_identb", [128, 128], BF16)
        epscol = m1.sb("p2_eps", [128, 1])
        x_t = [m1.sb(f"p2_x{i}", [128, D]) for i in range(2)]
        x1_t = [m1.sb(f"p2_x1{i}", [128, D]) for i in range(2)]
        x2_t = [m1.sb(f"p2_x2{i}", [128, D]) for i in range(2)]
        xln_f = m1.sb("p2_xlnf", [128, D])
        xln_bf = m1.sb("p2_xlnbf", [128, D], BF16)
        xlnT_t = [m1.sb(f"p2_xlnT{i}", [128, NCB * 128], BF16) for i in range(2)]
        sq = m1.sb("p2_sq", [128, D])
        stat = m1.sb("p2_stat", [128, 8])
        psA = m1.ps("p2_psA", [128, 512])
        psB = m1.ps("p2_psB", [128, 512])
        psT = [m1.ps("p2_psT1", [128, 128], BF16), m1.ps("p2_psT2", [128, 128], BF16)]

        S.dma("gpsimd", ident_b[:, :], cst["ident"], writes=[ident_b])
        S.dma("gpsimd", epscol[:, :], cst["epscol"], writes=[epscol])
        S.dma("gpsimd", ones_s[:, :], cst["ones1x128"], writes=[ones_s])
        S.dma("sync", attnT.ten.ap().rearrange("p (j t) -> p j t", j=NCORES),
              ios["ao"].rearrange("j p t -> p j t"), writes=[attnT])
        S.dma("gpsimd", w_s.ten.ap().rearrange("p (n c) -> p n c", n=NCB),
              ios["wo"].rearrange("(n p) c -> p n c", p=128), writes=[w_s])
        transpose_tiles(S, m1, w_s, 0, woT, 0, NCB, NCB, D, D, ident_b, psT)
        S.dma("gpsimd", m_s.ten.ap().rearrange("p (n c) -> p n c", n=NCB),
              ios["M"].rearrange("(n p) c -> p n c", p=128), writes=[m_s])
        S.dma("gpsimd", bias_s[:, :], ios["ffn_bias"].rearrange("(x c) -> x c", x=1), writes=[bias_s])

        for ti in range(NT):
            xt = x_t[ti % 2]
            x1 = x1_t[ti % 2]
            x2 = x2_t[ti % 2]
            xlnT = xlnT_t[ti % 2]
            S.dma("sync", xt[:, :], ios["x"][ti * 128:(ti + 1) * 128, :], writes=[xt])
            for oc in range(2):
                psum = psA if oc == 0 else psB
                items = [(attnT[:, cb * TOK + ti * 128: cb * TOK + (ti + 1) * 128],
                          woT[:, cb * D + oc * 512: cb * D + (oc + 1) * 512])
                         for cb in range(NCB)]
                S.mm_group(items, psum, reads=[attnT, woT])
                S.step("vector", lambda e, psum=psum, oc=oc, x1=x1, xt=xt: e.tensor_add(
                    x1.ten[:, oc * 512:(oc + 1) * 512], psum.ten[:, :],
                    xt.ten[:, oc * 512:(oc + 1) * 512]),
                    reads=[psum, xt], writes=[x1])
            ln_emit(S, m1, x1[:, :], xln_f[:, :], stat, sq, epscol,
                    reads=[x1], writes=[xln_f])
            S.step("vector", lambda e: e.tensor_copy(xln_bf.ten[:, :], xln_f.ten[:, :]),
                   reads=[xln_f], writes=[xln_bf])
            for cb in range(NCB):
                p = psT[cb % 2]
                S.step("tensor", lambda e, cb=cb, p=p: e.transpose(
                    p.ten[:, :], xln_bf.ten[:, cb * 128:(cb + 1) * 128], ident_b.ten[:, :]),
                    reads=[xln_bf, ident_b], writes=[p])
                S.step("scalar", lambda e, cb=cb, p=p, xlnT=xlnT: e.copy(
                    xlnT[:, cb * 128:(cb + 1) * 128], p.ten[:, :]),
                    reads=[p], writes=[xlnT])
            for oc in range(2):
                psum = psA if oc == 0 else psB
                items = [(xlnT[:, cb * 128:(cb + 1) * 128],
                          m_s[:, cb * D + oc * 512: cb * D + (oc + 1) * 512])
                         for cb in range(NCB)]
                items.append((ones_s[0:1, :], bias_s[0:1, oc * 512:(oc + 1) * 512]))
                S.mm_group(items, psum, reads=[xlnT, m_s, ones_s, bias_s])
                S.step("vector", lambda e, psum=psum, oc=oc, x1=x1, x2=x2: e.tensor_add(
                    x2.ten[:, oc * 512:(oc + 1) * 512], psum.ten[:, :],
                    x1.ten[:, oc * 512:(oc + 1) * 512]),
                    reads=[psum, x1], writes=[x2])
            S.dma("sync", ios["x2"][ti * 128:(ti + 1) * 128, :], x2[:, :], reads=[x2])
            if ti == NT - 1:
                S.dma("sync", ios["x2tail"], x2[125:128, :], reads=[x2])



def emit_p3(S, mem, ios):
    cst = ios["const"]
    NDB = D_IN // 128
    EXT = TOK + 3
    with mem.scope(S) as m0:
        ident_b = m0.sb("p3_identb", [128, 128], BF16)
        epscol = m0.sb("p3_eps", [128, 1])
        onecol = m0.sb("p3_one", [128, 1])
        u_preT = m0.sb("p3_upre", [128, NDB * EXT], BF16)
        S.dma("gpsimd", ident_b[:, :], cst["ident"], writes=[ident_b])
        S.dma("gpsimd", epscol[:, :], cst["epscol"], writes=[epscol])
        S.dma("gpsimd", onecol[:, :], cst["onecol"], writes=[onecol])

        with mem.scope(S) as m1:
            halo_s = m1.sb("p3_halo", [8, 3 * D])
            oneh = m1.sb("p3_oneh", [8, 1])
            xhf = m1.sb("p3_xhf", [1, 3 * D])
            xh = m1.sb("p3_xh", [3, D])
            xah = m1.sb("p3_xah", [3, D])
            xah_bf = m1.sb("p3_xahbf", [3, D], BF16)
            xaT_halo = m1.sb("p3_xaTh", [128, NCB * 3], BF16)
            stat = m1.sb("p3_stat", [128, 8])
            sqh = m1.sb("p3_sqh", [3, D])
            psH = m1.ps("p3_psH", [128, 512])
            psT = [m1.ps("p3_psT1", [128, 128], BF16), m1.ps("p3_psT2", [128, 128], BF16)]
            S.dma("sync", halo_s.ten.ap().rearrange("p (r c) -> p r c", r=3),
                  ios["halo_g"], writes=[halo_s])
            S.dma("sync", oneh[:, :], ios["onehot"], writes=[oneh])
            for k in range(6):
                S.mm_group([(oneh[0:8, 0:1], halo_s[:, k * 512:(k + 1) * 512])], psH,
                           reads=[oneh, halo_s], psum_slice=(slice(0, 1), slice(None)))
                S.step("scalar", lambda e, k=k: e.copy(xhf.ten[0:1, k * 512:(k + 1) * 512],
                                                       psH.ten[0:1, :]),
                       reads=[psH], writes=[xhf])
            tw = S.dma("sync", ios["halo_scr"].rearrange("(x r) c -> x r c", x=1),
                       xhf[0:1, :].rearrange("p (r c) -> p r c", r=3), reads=[xhf])
            S.dma("sync", xh[:, :], ios["halo_scr"], writes=[xh], waits=[tw])
            ln_emit(S, m1, xh[0:3, :], xah[0:3, :], stat, sqh, epscol,
                    reads=[xh], writes=[xah], rows=3)
            S.step("vector", lambda e: e.tensor_copy(xah_bf.ten[0:3, :], xah.ten[0:3, :]),
                   reads=[xah], writes=[xah_bf])
            for cb in range(NCB):
                p = psT[cb % 2]
                S.step("tensor", lambda e, cb=cb, p=p: e.transpose(
                    p.ten[0:128, 0:3], xah_bf.ten[0:3, cb * 128:(cb + 1) * 128],
                    ident_b.ten[0:3, 0:3]),
                    reads=[xah_bf, ident_b], writes=[p])
                S.step("scalar", lambda e, cb=cb, p=p: e.copy(
                    xaT_halo[:, cb * 3:(cb + 1) * 3], p.ten[0:128, 0:3]),
                    reads=[p], writes=[xaT_halo])

            xaT = m1.sb("p3_xaT", [128, NCB * TOK], BF16)
            x2t = [m1.sb(f"p3_x2{i}", [128, D]) for i in range(2)]
            xln_f = m1.sb("p3_xlnf", [128, D])
            xln_bf = m1.sb("p3_xlnbf", [128, D], BF16)
            sq = m1.sb("p3_sq", [128, D])
            for ti in range(NT):
                xt = x2t[ti % 2]
                S.dma("sync", xt[:, :], ios["x2"][ti * 128:(ti + 1) * 128, :], writes=[xt])
                ln_emit(S, m1, xt[:, :], xln_f[:, :], stat, sq, epscol,
                        reads=[xt], writes=[xln_f])
                S.step("vector", lambda e: e.tensor_copy(xln_bf.ten[:, :], xln_f.ten[:, :]),
                       reads=[xln_f], writes=[xln_bf])
                for cb in range(NCB):
                    p = psT[cb % 2]
                    S.step("tensor", lambda e, cb=cb, p=p: e.transpose(
                        p.ten[:, :], xln_bf.ten[:, cb * 128:(cb + 1) * 128], ident_b.ten[:, :]),
                        reads=[xln_bf, ident_b], writes=[p])
                    S.step("scalar", lambda e, cb=cb, ti=ti, p=p: e.copy(
                        xaT[:, cb * TOK + ti * 128: cb * TOK + (ti + 1) * 128], p.ten[:, :]),
                        reads=[p], writes=[xaT])

            win_s = m1.sb("p3_win", [128, NCB * D_IN], BF16)
            res_t = [m1.sb(f"p3_res{i}", [128, TOK], BF16) for i in range(2)]
            S.dma("gpsimd", win_s.ten.ap().rearrange("p (n c) -> p n c", n=NCB),
                  ios["W_in"][:, 0:D_IN].rearrange("(n p) c -> p n c", p=128),
                  writes=[win_s])
            for db in range(NDB):
                items = [(win_s[:, cb * D_IN + db * 128:cb * D_IN + (db + 1) * 128],
                          xaT_halo[:, cb * 3:(cb + 1) * 3]) for cb in range(NCB)]
                S.mm_group(items, psH, reads=[win_s, xaT_halo],
                           psum_slice=(slice(None), slice(0, 3)))
                S.step("scalar", lambda e, db=db: e.copy(
                    u_preT[:, db * EXT: db * EXT + 3], psH.ten[:, 0:3]),
                    reads=[psH], writes=[u_preT])
                for tc in range(4):
                    items = [(win_s[:, cb * D_IN + db * 128:cb * D_IN + (db + 1) * 128],
                              xaT[:, cb * TOK + tc * 512: cb * TOK + (tc + 1) * 512])
                             for cb in range(NCB)]
                    S.mm_group(items, psH, reads=[win_s, xaT])
                    S.step("scalar", lambda e, db=db, tc=tc: e.copy(
                        u_preT[:, db * EXT + 3 + tc * 512: db * EXT + 3 + (tc + 1) * 512],
                        psH.ten[:, :]),
                        reads=[psH], writes=[u_preT])
            S.dma("gpsimd", win_s.ten.ap().rearrange("p (n c) -> p n c", n=NCB),
                  ios["W_in"][:, D_IN:2 * D_IN].rearrange("(n p) c -> p n c", p=128),
                  writes=[win_s])
            for db in range(NDB):
                rt = res_t[db % 2]
                for tc in range(4):
                    items = [(win_s[:, cb * D_IN + db * 128:cb * D_IN + (db + 1) * 128],
                              xaT[:, cb * TOK + tc * 512: cb * TOK + (tc + 1) * 512])
                             for cb in range(NCB)]
                    S.mm_group(items, psH, reads=[win_s, xaT])
                    S.step("scalar", lambda e, tc=tc, rt=rt: e.activation(
                        rt.ten[:, tc * 512:(tc + 1) * 512], psH.ten[:, :], AF.Silu),
                        reads=[psH], writes=[rt])
                S.dma("sync", ios["sres"][db * 128:(db + 1) * 128, :], rt[:, :], reads=[rt])

        with mem.scope(S) as m2:
            cw_t = [m2.sb(f"p3_cw{i}", [128, 4 * NDB * 128], BF16) for i in range(2)]
            u_t = [m2.sb(f"p3_u{i}", [128, TOK], BF16) for i in range(2)]
            wx_s = m2.sb("p3_wx", [128, NDB * 96], BF16)
            cb_col = m2.sb("p3_cb", [128, NDB])
            psC = [m2.ps(f"p3_psC{i}", [128, 512]) for i in range(2)]
            psX = [m2.ps(f"p3_psX{i}", [128, 512]) for i in range(4)]
            S.dma("gpsimd", wx_s.ten.ap().rearrange("p (n c) -> p n c", n=NDB),
                  ios["W_x"].rearrange("(n p) c -> p n c", p=128), writes=[wx_s])
            ident_f = m2.sb("p3_identf", [128, 128])
            colr = m2.sb("p3_colr", [16, 128])
            psF = m2.ps("p3_psF", [128, 128])
            S.dma("gpsimd", ident_f[:, :], cst["ident"], writes=[ident_f])
            S.dma("gpsimd", colr[:, :], ios["conv_b"].rearrange("(n p) -> n p", n=NDB),
                  writes=[colr])
            S.step("tensor", lambda e: e.transpose(
                psF.ten[0:128, 0:16], colr.ten[0:16, :], ident_f.ten[0:16, 0:16]),
                reads=[colr, ident_f], writes=[psF])
            S.step("scalar", lambda e: e.copy(cb_col.ten[:, :], psF.ten[0:128, 0:16]),
                   reads=[psF], writes=[cb_col])
            xdbl_started = [False] * 4
            for ob in range(NDB):
                cw = cw_t[ob % 2]
                S.dma("gpsimd", cw.ten.ap().rearrange("p (j it o) -> p j it o", j=4, it=NDB),
                      ios["conv_w"][:, :, ob * 128:(ob + 1) * 128]
                          .rearrange("j (it p) o -> p j it o", p=128),
                      writes=[cw])
                ut = u_t[ob % 2]
                for tc in range(4):
                    psum = psC[tc % 2]
                    items = []
                    for j in range(4):
                        for it in range(NDB):
                            items.append((cw[:, (j * NDB + it) * 128:(j * NDB + it + 1) * 128],
                                          u_preT[:, it * EXT + tc * 512 + j:
                                                 it * EXT + tc * 512 + j + 512]))
                    S.mm_group(items, psum, reads=[cw, u_preT])
                    S.step("scalar", lambda e, tc=tc, ob=ob, psum=psum, ut=ut: e.activation(
                        ut.ten[:, tc * 512:(tc + 1) * 512], psum.ten[:, :], AF.Silu,
                        bias=cb_col.ten[:, ob:ob + 1]),
                        reads=[psum, cb_col], writes=[ut])
                    px = psX[tc]
                    wl = []
                    if not xdbl_started[tc]:
                        if px.wr is not None:
                            wl.append(px.wr)
                        wl.extend(px.rds)
                    wl.append(ut.wr)
                    wl.append(wx_s.wr)
                    tokx = S._record(
                        "tensor",
                        lambda e, ob=ob, tc=tc, px=px, ut=ut, s=(ob == 0): e.matmul(
                            px.ten[0:96, :], wx_s.ten[:, ob * 96:(ob + 1) * 96],
                            ut.ten[:, tc * 512:(tc + 1) * 512],
                            start=s, stop=(ob == NDB - 1), skip_group_check=True),
                        wl, "pe", 1)
                    xdbl_started[tc] = True
                    ut.rds.append(tokx)
                    wx_s.rds.append(tokx)
                    if ob == NDB - 1:
                        px.wr = tokx
                        px.rds = []
                S.dma("sync", ios["u_scr"][ob * 128:(ob + 1) * 128, :], ut[:, :], reads=[ut])

            dpre = m2.sb("p3_dpre", [64, TOK], BF16)
            bc_s = m2.sb("p3_bc", [32, TOK], BF16)
            for tc in range(4):
                S.step("scalar", lambda e, tc=tc: e.copy(
                    dpre.ten[0:64, tc * 512:(tc + 1) * 512], psX[tc].ten[0:64, :]),
                    reads=[psX[tc]], writes=[dpre])
                S.step("scalar", lambda e, tc=tc: e.copy(
                    bc_s.ten[0:32, tc * 512:(tc + 1) * 512], psX[tc].ten[64:96, :]),
                    reads=[psX[tc]], writes=[bc_s])
            ios["_bc_tok"] = [S.dma("sync", ios["bc"], bc_s[0:32, :], reads=[bc_s])]

            wdt_s = m2.sb("p3_wdt", [64, D_IN], BF16)
            bdt_col = m2.sb("p3_bdt", [128, NDB])
            dtmp = m2.sb("p3_dtmp", [128, 512])
            dbf_t = [m2.sb(f"p3_dbf{i}", [128, TOK], BF16) for i in range(2)]
            du_t = [m2.sb(f"p3_du{i}", [128, TOK], BF16) for i in range(2)]
            ur_t = [m2.sb(f"p3_ur{i}", [128, TOK], BF16) for i in range(2)]
            u_scr_done = S.dma_barrier()
            S.dma("gpsimd", wdt_s[:, :], ios["W_dt"], writes=[wdt_s])
            S.dma("gpsimd", colr[:, :], ios["b_dt"].rearrange("(n p) -> n p", n=NDB),
                  writes=[colr])
            S.step("tensor", lambda e: e.transpose(
                psF.ten[0:128, 0:16], colr.ten[0:16, :], ident_f.ten[0:16, 0:16]),
                reads=[colr, ident_f], writes=[psF])
            S.step("scalar", lambda e: e.copy(bdt_col.ten[:, :], psF.ten[0:128, 0:16]),
                   reads=[psF], writes=[bdt_col])
            for db in range(NDB):
                dbf = dbf_t[db % 2]
                du = du_t[db % 2]
                ur = ur_t[db % 2]
                S.dma("sync", ur[:, :], ios["u_scr"][db * 128:(db + 1) * 128, :],
                      writes=[ur], waits=u_scr_done)
                for tc in range(4):
                    psum = psC[tc % 2]
                    S.mm_group([(wdt_s[0:64, db * 128:(db + 1) * 128],
                                 dpre[0:64, tc * 512:(tc + 1) * 512])], psum,
                               reads=[wdt_s, dpre])
                    S.step("scalar", lambda e, db=db, psum=psum: e.activation(
                        dtmp.ten[:, :], psum.ten[:, :], AF.Exp,
                        bias=bdt_col.ten[:, db:db + 1]),
                        reads=[psum, bdt_col], writes=[dtmp])
                    S.step("scalar", lambda e, tc=tc, dbf=dbf: e.activation(
                        dbf.ten[:, tc * 512:(tc + 1) * 512], dtmp.ten[:, :], AF.Ln,
                        bias=onecol.ten[:, 0:1]),
                        reads=[dtmp, onecol], writes=[dbf])
                S.step("vector", lambda e, dbf=dbf, du=du, ur=ur: e.tensor_mul(
                    du.ten[:, :], dbf.ten[:, :], ur.ten[:, :]),
                    reads=[dbf, ur], writes=[du])
                t1 = S.dma("sync", ios["dlt"][db * 128:(db + 1) * 128, :], dbf[:, :], reads=[dbf])
                t2 = S.dma("sync", ios["du"][db * 128:(db + 1) * 128, :], du[:, :], reads=[du])
                ios.setdefault("_dltdu_tok", {})[db] = [t1, t2]


def build_p2():
    nc = bass.Bass("TRN2", target_bir_lowering=False, debug=False)
    ios = {"const": {}}
    for name, shape, dt in CONST_SPECS:
        ios["const"][name] = nc.dram_tensor("c_" + name, shape, dt, kind="ExternalInput").ap()
    ios["x"] = nc.dram_tensor("x", [TOK, D], F32, kind="ExternalInput").ap()
    ios["ao"] = nc.dram_tensor("ao", [NCORES, 128, TOK], BF16, kind="ExternalInput").ap()
    ios["wo"] = nc.dram_tensor("wo", [D, D], F32, kind="ExternalInput").ap()
    ios["M"] = nc.dram_tensor("M", [D, D], F32, kind="ExternalInput").ap()
    ios["ffn_bias"] = nc.dram_tensor("ffn_bias", [D], F32, kind="ExternalInput").ap()
    ios["x2"] = nc.dram_tensor("x2", [TOK, D], F32, kind="ExternalOutput").ap()
    ios["x2tail"] = nc.dram_tensor("x2tail", [3, D], F32, kind="ExternalOutput").ap()
    with contextlib.ExitStack() as ctx:
        mem = Mem(nc, ctx)
        S = Sched(nc)
        emit_p2(S, mem, ios)
        S.step("gpsimd", lambda e: e.nop(), waits=S.dma_barrier(), sem=None)
        block = ctx.enter_context(nc.Block())
        S.build(block)
    S.close()
    return nc


def build_p3():
    nc = bass.Bass("TRN2", target_bir_lowering=False, debug=False)
    ios = {"const": {}}
    for name, shape, dt in CONST_SPECS:
        ios["const"][name] = nc.dram_tensor("c_" + name, shape, dt, kind="ExternalInput").ap()
    ios["x2"] = nc.dram_tensor("x2", [TOK, D], F32, kind="ExternalInput").ap()
    ios["halo_g"] = nc.dram_tensor("halo_g", [NCORES, 3, D], F32, kind="ExternalInput").ap()
    ios["onehot"] = nc.dram_tensor("onehot", [NCORES, 1], F32, kind="ExternalInput").ap()
    ios["W_in"] = nc.dram_tensor("W_in", [D, 2 * D_IN], F32, kind="ExternalInput").ap()
    ios["conv_w"] = nc.dram_tensor("conv_w", [4, D_IN, D_IN], F32, kind="ExternalInput").ap()
    ios["conv_b"] = nc.dram_tensor("conv_b", [D_IN], F32, kind="ExternalInput").ap()
    ios["W_x"] = nc.dram_tensor("W_x", [D_IN, 96], F32, kind="ExternalInput").ap()
    ios["W_dt"] = nc.dram_tensor("W_dt", [DT_RANK, D_IN], F32, kind="ExternalInput").ap()
    ios["b_dt"] = nc.dram_tensor("b_dt", [D_IN], F32, kind="ExternalInput").ap()
    ios["dlt"] = nc.dram_tensor("dlt", [D_IN, TOK], BF16, kind="ExternalOutput").ap()
    ios["du"] = nc.dram_tensor("du", [D_IN, TOK], BF16, kind="ExternalOutput").ap()
    ios["sres"] = nc.dram_tensor("sres", [D_IN, TOK], BF16, kind="ExternalOutput").ap()
    ios["bc"] = nc.dram_tensor("bc", [32, TOK], BF16, kind="ExternalOutput").ap()
    ios["halo_scr"] = nc.dram_tensor("halo_scr", [3, D], F32, kind="Internal").ap()
    ios["u_scr"] = nc.dram_tensor("u_scr", [D_IN, TOK], BF16, kind="Internal").ap()
    with contextlib.ExitStack() as ctx:
        mem = Mem(nc, ctx)
        S = Sched(nc)
        emit_p3(S, mem, ios)
        S.step("gpsimd", lambda e: e.nop(), waits=S.dma_barrier(), sem=None)
        block = ctx.enter_context(nc.Block())
        S.build(block)
    S.close()
    return nc



def emit_p4(S, mem, ios):
    NG = 16
    F = 2048
    with mem.scope(S) as m1:
        e8 = m1.sb("p4_e8", [128, NG * 128], BF16)
        ws = m1.sb("p4_ws", [128, NG * 128], BF16)
        e16 = m1.sb("p4_e16", [32, 256], BF16)
        alog = m1.sb("p4_alog", [128, 32])
        alan = m1.sb("p4_alan", [128, 32])
        dcol = m1.sb("p4_dcol", [128, 2])
        dlt_b = m1.sb("p4_dlt", [128, 2 * T_FULL], BF16)
        du_b = m1.sb("p4_du", [128, 2 * T_FULL], BF16)
        sres_b = m1.sb("p4_sres", [128, 2 * T_FULL], BF16)
        y_sb = m1.sb("p4_y", [128, 2 * T_FULL], BF16)
        bc_b = m1.sb("p4_bc", [32, T_FULL], BF16)
        brep = m1.sb("p4_brep", [128, F])
        crep = m1.sb("p4_crep", [128, F])
        a_sb = m1.sb("p4_a", [128, F])
        b_sb = m1.sb("p4_b", [128, F])
        s_sb = m1.sb("p4_s", [128, F])
        sc_sb = m1.sb("p4_sc", [128, F], BF16)
        carry = m1.sb("p4_carry", [128, 64])
        resid = m1.sb("p4_resid", [128, 2])
        carrybf = m1.sb("p4_carrybf", [128, 64], BF16)
        psE = [m1.ps(f"p4_psE{i}", [128, 512]) for i in range(2)]
        psU = [m1.ps(f"p4_psU{i}", [128, 512]) for i in range(2)]
        psY = m1.ps("p4_psY", [128, 2048])
        cst = ios["const"]

        S.dma("gpsimd", e8[:, :], ios["e8sel"], writes=[e8])
        S.dma("gpsimd", ws[:, :], ios["wsel"], writes=[ws])
        S.dma("gpsimd", e16[:, :], ios["e16sel"], writes=[e16])
        S.dma("gpsimd", alog[:, :], ios["alog_lanes"], writes=[alog])
        S.dma("gpsimd", dcol[:, :], ios["d_col"], writes=[dcol])
        S.step("scalar", lambda e: e.activation(alan.ten[:, :], alog.ten[:, :], AF.Exp),
               reads=[alog], writes=[alan])
        S.step("vector", lambda e: e.tensor_scalar(alan.ten[:, :], alan.ten[:, :],
                                                   -1.0, None, ALU.mult),
               reads=[alan], writes=[alan])
        S.step("vector", lambda e: e.memset(carry.ten[:, :], 0.0), writes=[carry])

        for b in range(2):
            for t3, io3 in ((dlt_b, "dlt_g"), (du_b, "du_g"), (sres_b, "sres_g")):
                for dh in range(2):
                    S.dma("sync",
                          t3.ten[:, dh * T_FULL:(dh + 1) * T_FULL]
                              .rearrange("p (s t) -> p s t", s=4),
                          ios[io3][4 * b:4 * b + 4, dh * 128:(dh + 1) * 128, :]
                              .rearrange("s p t -> p s t"),
                          writes=[t3])
            S.dma("sync", bc_b.ten.ap().rearrange("p (s t) -> p s t", s=4),
                  ios["bc_g"][4 * b:4 * b + 4, :, :].rearrange("s p t -> p s t"),
                  writes=[bc_b])
            if b == 0:
                S.step("vector", lambda e: e.memset(carry.ten[:, :], 0.0),
                       reads=[], writes=[carry])
            else:
                S.step("vector", lambda e: e.memset(carry.ten[:, :], 0.0),
                       writes=[carry])
            for tq in range(4):
                t0 = tq * F
                for which, rep, r0 in ((0, brep, 0), (1, crep, 16)):
                    for sub in range(4):
                        ps = psE[sub % 2]
                        S.mm_group([(e16[0:32, r0 * 8:r0 * 8 + 128] if False else
                                     e16[0:32, which * 128:(which + 1) * 128],
                                     bc_b[0:32, t0 + sub * 512: t0 + (sub + 1) * 512])],
                                   ps, reads=[e16, bc_b])
                        S.step("scalar", lambda e, rep=rep, sub=sub, ps=ps: e.copy(
                            rep.ten[:, sub * 512:(sub + 1) * 512], ps.ten[:, :]),
                            reads=[ps], writes=[rep])
                ystart = [False] * 4
                for dh in range(2):
                    for g16 in range(NG):
                        gidx = dh * NG + g16
                        dsl = slice(dh * T_FULL + t0, dh * T_FULL + t0 + F)
                        for sub in range(4):
                            ps = psE[sub % 2]
                            S.mm_group([(e8[:, g16 * 128:(g16 + 1) * 128],
                                         dlt_b[:, dh * T_FULL + t0 + sub * 512:
                                               dh * T_FULL + t0 + (sub + 1) * 512])],
                                       ps, reads=[e8, dlt_b])
                            S.step("scalar", lambda e, sub=sub, ps=ps, gidx=gidx: e.activation(
                                a_sb.ten[:, sub * 512:(sub + 1) * 512], ps.ten[:, :],
                                AF.Exp, scale=alan.ten[:, gidx:gidx + 1]),
                                reads=[ps, alan], writes=[a_sb])
                        for sub in range(4):
                            ps = psU[sub % 2]
                            S.mm_group([(e8[:, g16 * 128:(g16 + 1) * 128],
                                         du_b[:, dh * T_FULL + t0 + sub * 512:
                                              dh * T_FULL + t0 + (sub + 1) * 512])],
                                       ps, reads=[e8, du_b])
                            S.step("vector", lambda e, sub=sub, ps=ps: e.tensor_mul(
                                b_sb.ten[:, sub * 512:(sub + 1) * 512], ps.ten[:, :],
                                brep.ten[:, sub * 512:(sub + 1) * 512]),
                                reads=[ps, brep], writes=[b_sb])
                        ccol = carry.ten[:, b * 0 + gidx + (0 if b == 0 else 0):gidx + 1]
                        S.step("vector", lambda e, gidx=gidx: e.tensor_tensor_scan(
                            s_sb.ten[:, :], a_sb.ten[:, :], b_sb.ten[:, :],
                            carry.ten[:, gidx:gidx + 1], ALU.mult, ALU.add),
                            reads=[a_sb, b_sb, carry], writes=[s_sb])
                        S.step("vector", lambda e, gidx=gidx: e.tensor_copy(
                            carry.ten[:, gidx:gidx + 1], s_sb.ten[:, F - 1:F]),
                            reads=[s_sb], writes=[carry])
                        S.step("gpsimd", lambda e: e.tensor_mul(
                            sc_sb.ten[:, :], s_sb.ten[:, :], crep.ten[:, :]),
                            reads=[s_sb, crep], writes=[sc_sb])
                        for sub in range(4):
                            wl = []
                            if not ystart[sub]:
                                if psY.wr is not None:
                                    wl.append(psY.wr)
                                wl.extend(psY.rds)
                            if sc_sb.wr is not None:
                                wl.append(sc_sb.wr)
                            if ws.wr is not None:
                                wl.append(ws.wr)
                            toky = S._record(
                                "tensor",
                                lambda e, g16=g16, sub=sub, s=(not ystart[sub]),
                                       last=(g16 == NG - 1): e.matmul(
                                    psY.ten[:, sub * 512:(sub + 1) * 512],
                                    ws.ten[:, g16 * 128:(g16 + 1) * 128],
                                    sc_sb.ten[:, sub * 512:(sub + 1) * 512],
                                    start=s, stop=last, skip_group_check=True),
                                wl, "pe", 1)
                            ystart[sub] = True
                            sc_sb.rds.append(toky)
                            ws.rds.append(toky)
                            if g16 == NG - 1:
                                if sub == 3:
                                    psY.wr = toky
                                    psY.rds = []
                    S.step("scalar", lambda e, dh=dh, t0=t0: e.copy(
                        y_sb.ten[:, dh * T_FULL + t0: dh * T_FULL + t0 + F],
                        psY.ten[:, :]),
                        reads=[psY], writes=[y_sb])
                    ystart = [False] * 4
            for dh in range(2):
                S.step("vector", lambda e, dh=dh: e.tensor_copy(
                    carrybf.ten[:, dh * NG:(dh + 1) * NG],
                    carry.ten[:, dh * NG:(dh + 1) * NG]),
                    reads=[carry], writes=[carrybf])
                items = []
                for g16 in range(NG):
                    gidx = dh * NG + g16
                    items.append((ws[:, g16 * 128:(g16 + 1) * 128],
                                  carrybf[:, gidx:gidx + 1]))
                S.mm_group(items, psE[0], reads=[ws, carrybf],
                           psum_slice=(slice(None), slice(0, 1)))
                S.step("vector", lambda e, dh=dh: e.tensor_mul(
                    resid.ten[:, dh:dh + 1], psE[0].ten[:, 0:1], dcol.ten[:, dh:dh + 1]),
                    reads=[psE[0], dcol], writes=[resid])
                S.step("scalar", lambda e, dh=dh: e.add(
                    y_sb.ten[:, dh * T_FULL:(dh + 1) * T_FULL],
                    y_sb.ten[:, dh * T_FULL:(dh + 1) * T_FULL],
                    resid.ten[:, dh:dh + 1]),
                    reads=[resid], writes=[y_sb])
                S.step("vector", lambda e, dh=dh: e.tensor_mul(
                    y_sb.ten[:, dh * T_FULL:(dh + 1) * T_FULL],
                    y_sb.ten[:, dh * T_FULL:(dh + 1) * T_FULL],
                    sres_b.ten[:, dh * T_FULL:(dh + 1) * T_FULL]),
                    reads=[sres_b], writes=[y_sb])
                for tq in range(4):
                    S.dma("sync", ios["ya2a"][4 * b + tq, dh * 128:(dh + 1) * 128, :],
                          y_sb[:, dh * T_FULL + tq * F: dh * T_FULL + (tq + 1) * F],
                          reads=[y_sb])


def build_p4():
    nc = bass.Bass("TRN2", target_bir_lowering=False, debug=False)
    ios = {"const": {}}
    for name, shape, dt in CONST_SPECS:
        ios["const"][name] = nc.dram_tensor("c_" + name, shape, dt, kind="ExternalInput").ap()
    for nm in ("dlt_g", "du_g", "sres_g"):
        ios[nm] = nc.dram_tensor(nm, [NCORES, 256, TOK], BF16, kind="ExternalInput").ap()
    ios["bc_g"] = nc.dram_tensor("bc_g", [NCORES, 32, TOK], BF16, kind="ExternalInput").ap()
    ios["alog_lanes"] = nc.dram_tensor("alog_lanes", [128, 32], F32, kind="ExternalInput").ap()
    ios["d_col"] = nc.dram_tensor("d_col", [128, 2], F32, kind="ExternalInput").ap()
    ios["e8sel"] = nc.dram_tensor("e8sel", [128, 16 * 128], BF16, kind="ExternalInput").ap()
    ios["wsel"] = nc.dram_tensor("wsel", [128, 16 * 128], BF16, kind="ExternalInput").ap()
    ios["e16sel"] = nc.dram_tensor("e16sel", [32, 256], BF16, kind="ExternalInput").ap()
    ios["ya2a"] = nc.dram_tensor("ya2a", [NCORES, 256, TOK], BF16, kind="ExternalOutput").ap()
    with contextlib.ExitStack() as ctx:
        mem = Mem(nc, ctx)
        S = Sched(nc)
        emit_p4(S, mem, ios)
        S.step("gpsimd", lambda e: e.nop(), waits=S.dma_barrier(), sem=None)
        block = ctx.enter_context(nc.Block())
        S.build(block)
    S.close()
    return nc


def p4_host_consts():
    e8 = np.zeros((128, 16 * 128), np.float32)
    for g in range(16):
        for l in range(128):
            e8[g * 8 + l // 16, g * 128 + l] = 1.0
    ws = np.zeros((128, 16 * 128), np.float32)
    for g in range(16):
        for l in range(128):
            ws[l, g * 128 + g * 8 + l // 16] = 1.0
    e16 = np.zeros((32, 256), np.float32)
    for l in range(128):
        e16[l % 16, l] = 1.0
        e16[16 + l % 16, 128 + l] = 1.0
    import ml_dtypes
    bf = ml_dtypes.bfloat16
    return e8.astype(bf), ws.astype(bf), e16.astype(bf)



def emit_p5(S, mem, ios):
    cst = ios["const"]
    NDB = D_IN // 128
    with mem.scope(S) as m0:
        ident_b = m0.sb("p5_identb", [128, 128], BF16)
        epscol = m0.sb("p5_eps", [128, 1])
        acc = m0.sb("p5_acc", [128, NT * D])
        xfT = m0.sb("p5_xfT", [128, NCB * TOK], BF16)
        combT = m0.sb("p5_combT", [16, TOK], BF16)
        stat = m0.sb("p5_stat", [128, 8])
        psA = m0.ps("p5_psA", [128, 512])
        psB = m0.ps("p5_psB", [128, 512])
        psC = m0.ps("p5_psC", [128, 512])
        psT = [m0.ps("p5_psT1", [128, 128], BF16), m0.ps("p5_psT2", [128, 128], BF16)]
        S.dma("gpsimd", ident_b[:, :], cst["ident"], writes=[ident_b])
        S.dma("gpsimd", epscol[:, :], cst["epscol"], writes=[epscol])

        with mem.scope(S) as m1:
            ygt = m1.sb("p5_ygt", [128, (NDB // 2) * TOK], BF16)
            wout = m1.sb("p5_wout", [128, NDB * D], BF16)
            x2_t = [m1.sb(f"p5_x2{i}", [128, D]) for i in range(2)]
            xln_f = m1.sb("p5_xlnf", [128, D])
            xln_bf = m1.sb("p5_xlnbf", [128, D], BF16)
            sq = m1.sb("p5_sq", [128, D])
            S.dma("gpsimd", wout.ten.ap().rearrange("p (n c) -> p n c", n=NDB),
                  ios["W_out"].rearrange("(n p) c -> p n c", p=128), writes=[wout])
            for dh in range(2):
                S.dma("sync", ygt.ten.ap().rearrange("p (j t) -> p j t", j=NDB // 2),
                      ios["yg_g"].rearrange("j (g h p) t -> p (j g h) t", p=128, g=2)
                          [:, dh, :, :] if False else
                      ios["yg_g"][:, dh * 128:(dh + 1) * 128, :]
                          .rearrange("j p t -> p j t"),
                      writes=[ygt])
                for ti in range(NT):
                    xt = x2_t[ti % 2]
                    if dh == 0:
                        S.dma("sync", xt[:, :], ios["x2"][ti * 128:(ti + 1) * 128, :],
                              writes=[xt])
                    for oc in range(2):
                        psum = psA if oc == 0 else psB
                        items = [(ygt[:, it * TOK + ti * 128: it * TOK + (ti + 1) * 128],
                                  wout[:, (2 * it + dh) * D + oc * 512:
                                       (2 * it + dh) * D + (oc + 1) * 512])
                             for it in range(NDB // 2)]
                        S.mm_group(items, psum, reads=[ygt, wout])
                        if dh == 0:
                            S.step("vector", lambda e, psum=psum, oc=oc, ti=ti, xt=xt: e.tensor_add(
                                acc[:, ti * D + oc * 512: ti * D + (oc + 1) * 512],
                                psum.ten[:, :], xt.ten[:, oc * 512:(oc + 1) * 512]),
                                reads=[psum, xt], writes=[acc])
                        else:
                            S.step("vector", lambda e, psum=psum, oc=oc, ti=ti: e.tensor_add(
                                acc[:, ti * D + oc * 512: ti * D + (oc + 1) * 512],
                                psum.ten[:, :],
                                acc[:, ti * D + oc * 512: ti * D + (oc + 1) * 512]),
                                reads=[psum], writes=[acc])
            for ti in range(NT):
                ln_emit(S, m1, acc[:, ti * D:(ti + 1) * D], xln_f[:, :], stat, sq, epscol,
                        reads=[acc], writes=[xln_f])
                S.step("vector", lambda e: e.tensor_copy(xln_bf.ten[:, :], xln_f.ten[:, :]),
                       reads=[xln_f], writes=[xln_bf])
                for cb in range(NCB):
                    p = psT[cb % 2]
                    S.step("tensor", lambda e, cb=cb, p=p: e.transpose(
                        p.ten[:, :], xln_bf.ten[:, cb * 128:(cb + 1) * 128], ident_b.ten[:, :]),
                        reads=[xln_bf, ident_b], writes=[p])
                    S.step("scalar", lambda e, cb=cb, ti=ti, p=p: e.copy(
                        xfT[:, cb * TOK + ti * 128: cb * TOK + (ti + 1) * 128], p.ten[:, :]),
                        reads=[p], writes=[xfT])

            rwr = m1.sb("p5_rwr", [16, D], BF16)
            rwt = m1.sb("p5_rwt", [128, NCB * 16], BF16)
            scT = m1.sb("p5_scT", [16, TOK])
            scores_s = m1.sb("p5_scores", [128, NT * 16])
            S.dma("gpsimd", rwr[:, :], ios["router_w"], writes=[rwr])
            for cb in range(NCB):
                p = psT[cb % 2]
                S.step("tensor", lambda e, cb=cb, p=p: e.transpose(
                    p.ten[0:128, 0:16], rwr.ten[0:16, cb * 128:(cb + 1) * 128],
                    ident_b.ten[0:16, 0:16]),
                    reads=[rwr, ident_b], writes=[p])
                S.step("scalar", lambda e, cb=cb, p=p: e.copy(
                    rwt[:, cb * 16:(cb + 1) * 16], p.ten[0:128, 0:16]),
                    reads=[p], writes=[rwt])
            for tc in range(4):
                items = [(rwt[:, cb * 16:(cb + 1) * 16],
                          xfT[:, cb * TOK + tc * 512: cb * TOK + (tc + 1) * 512])
                         for cb in range(NCB)]
                S.mm_group(items, psA, reads=[rwt, xfT],
                           psum_slice=(slice(0, 16), slice(None)))
                S.step("scalar", lambda e, tc=tc: e.activation(
                    scT.ten[0:16, tc * 512:(tc + 1) * 512], psA.ten[0:16, :], AF.Sigmoid),
                    reads=[psA], writes=[scT])
            scT_bf = m1.sb("p5_scTbf", [16, TOK], BF16)
            S.step("vector", lambda e: e.tensor_copy(scT_bf.ten[0:16, :], scT.ten[0:16, :]),
                   reads=[scT], writes=[scT_bf])
            for ti in range(NT):
                p = psT[ti % 2]
                S.step("tensor", lambda e, ti=ti, p=p: e.transpose(
                    p.ten[0:128, 0:16], scT_bf.ten[0:16, ti * 128:(ti + 1) * 128],
                    ident_b.ten[0:16, 0:16]),
                    reads=[scT_bf, ident_b], writes=[p])
                S.step("scalar", lambda e, ti=ti, p=p: e.copy(
                    scores_s[:, ti * 16:(ti + 1) * 16], p.ten[0:128, 0:16]),
                    reads=[p], writes=[scores_s])

            cb_b = m1.sb("p5_cbb", [128, 16])
            cbr = m1.sb("p5_cbr", [1, 16])
            ones_f = m1.sb("p5_onesf", [1, 128])
            neginf = m1.sb("p5_neg", [128, 16])
            sb_t = m1.sb("p5_sbt", [128, 16])
            pstmp = m1.sb("p5_pstmp", [128, 24])
            gs_s = m1.sb("p5_gs", [128, 4])
            gmin = m1.sb("p5_gmin", [128, 6])
            red = m1.sb("p5_red", [128, 4])
            em_s = m1.sb("p5_em", [128, 16])
            msk = m1.sb("p5_msk", [128, 16])
            mm_s = m1.sb("p5_mm", [128, 16])
            mm2_s = m1.sb("p5_mm2", [128, 16])
            eq_s = m1.sb("p5_eq", [128, 16])
            wr_s2 = m1.sb("p5_wr", [128, 16])
            comb_bf = m1.sb("p5_combbf", [128, 16], BF16)
            em_u8 = m1.sb("p5_emu8", [128, 16], mybir.dt.uint8)
            eq_u8 = m1.sb("p5_equ8", [128, 16], mybir.dt.uint8)
            S.dma("sync", cbr[:, :], ios["corr_bias"].rearrange("(x e) -> x e", x=1),
                  writes=[cbr])
            S.dma("sync", ones_f[:, :], cst["ones1x128"], writes=[ones_f])
            S.step("vector", lambda e: e.memset(neginf.ten[:, :], -1e30), writes=[neginf])
            S.mm_group([(ones_f[0:1, :], cbr[0:1, :])], psA, reads=[ones_f, cbr],
                       psum_slice=(slice(None), slice(0, 16)))
            S.step("scalar", lambda e: e.copy(cb_b.ten[:, :], psA.ten[:, 0:16]),
                   reads=[psA], writes=[cb_b])
            for ti in range(NT):
                sc_ap = scores_s[:, ti * 16:(ti + 1) * 16]
                S.step("vector", lambda e, sc_ap=sc_ap: e.tensor_add(
                    sb_t.ten[:, :], sc_ap, cb_b.ten[:, :]),
                    reads=[scores_s, cb_b], writes=[sb_t])
                pk = 0
                for i in range(4):
                    for j in range(i + 1, 4):
                        S.step("vector", lambda e, i=i, j=j, pk=pk: e.tensor_add(
                            pstmp.ten[:, pk * 4:(pk + 1) * 4],
                            sb_t.ten.ap()[:, i:13 + i:4], sb_t.ten.ap()[:, j:13 + j:4]),
                            reads=[sb_t], writes=[pstmp])
                        pk += 1
                S.step("vector", lambda e: e.tensor_max(
                    gs_s.ten[:, :], pstmp.ten[:, 0:4], pstmp.ten[:, 4:8]),
                    reads=[pstmp], writes=[gs_s])
                for k in range(2, 6):
                    S.step("vector", lambda e, k=k: e.tensor_max(
                        gs_s.ten[:, :], gs_s.ten[:, :], pstmp.ten[:, k * 4:(k + 1) * 4]),
                        reads=[pstmp], writes=[gs_s])
                pk = 0
                for i in range(4):
                    for j in range(i + 1, 4):
                        S.step("vector", lambda e, i=i, j=j, pk=pk: e.tensor_tensor(
                            gmin.ten[:, pk:pk + 1], gs_s.ten[:, i:i + 1],
                            gs_s.ten[:, j:j + 1], ALU.min),
                            reads=[gs_s], writes=[gmin])
                        pk += 1
                S.step("vector", lambda e: e.reduce_max(red.ten[:, 0:1], gmin.ten[:, :], axis=AXX),
                       reads=[gmin], writes=[red])
                S.step("vector", lambda e: e.tensor_scalar(
                    gs_s.ten[:, :], gs_s.ten[:, :], red.ten[:, 0:1], None, ALU.is_ge),
                    reads=[red], writes=[gs_s])
                for j in range(4):
                    S.step("vector", lambda e, j=j: e.tensor_copy(
                        em_s.ten.ap()[:, j:13 + j:4], gs_s.ten[:, :]),
                        reads=[gs_s], writes=[em_s])
                S.step("vector", lambda e: e.tensor_copy(em_u8.ten[:, :], em_s.ten[:, :]),
                       reads=[em_s], writes=[em_u8])
                S.step("vector", lambda e: e.select(
                    msk.ten[:, :], em_u8.ten[:, :], sb_t.ten[:, :], neginf.ten[:, :], add_drain=True),
                    reads=[em_u8, sb_t, neginf], writes=[msk])
                S.step("vector", lambda e: e.tensor_copy(mm_s.ten[:, :], msk.ten[:, :]),
                       reads=[msk], writes=[mm_s])
                cur, nxt = mm_s, mm2_s
                for it in range(3):
                    S.step("vector", lambda e, cur=cur: e.reduce_max(
                        red.ten[:, 2:3], cur.ten[:, :], axis=AXX),
                        reads=[cur], writes=[red])
                    S.step("vector", lambda e, cur=cur: e.tensor_scalar(
                        eq_s.ten[:, :], cur.ten[:, :], red.ten[:, 2:3], None, ALU.is_equal),
                        reads=[cur, red], writes=[eq_s])
                    S.step("vector", lambda e: e.tensor_copy(eq_u8.ten[:, :], eq_s.ten[:, :]),
                           reads=[eq_s], writes=[eq_u8])
                    S.step("vector", lambda e, cur=cur, nxt=nxt: e.select(
                        nxt.ten[:, :], eq_u8.ten[:, :], neginf.ten[:, :], cur.ten[:, :], add_drain=True),
                        reads=[eq_u8, neginf, cur], writes=[nxt])
                    cur, nxt = nxt, cur
                S.step("vector", lambda e, cur=cur: e.reduce_max(
                    red.ten[:, 3:4], cur.ten[:, :], axis=AXX),
                    reads=[cur], writes=[red])
                S.step("vector", lambda e: e.tensor_scalar(
                    msk.ten[:, :], msk.ten[:, :], red.ten[:, 3:4], None, ALU.is_ge),
                    reads=[red], writes=[msk])
                S.step("vector", lambda e, sc_ap=sc_ap: e.tensor_mul(
                    wr_s2.ten[:, :], sc_ap, msk.ten[:, :]),
                    reads=[scores_s, msk], writes=[wr_s2])
                S.step("vector", lambda e: e.reduce_sum(
                    red.ten[:, 1:2], wr_s2.ten[:, :], axis=AXX),
                    reads=[wr_s2], writes=[red])
                S.step("vector", lambda e: e.tensor_scalar(
                    red.ten[:, 1:2], red.ten[:, 1:2], 1e-20, None, ALU.add),
                    reads=[red], writes=[red])
                S.step("vector", lambda e: e.reciprocal(
                    red.ten[:, 2:3], red.ten[:, 1:2]),
                    reads=[red], writes=[red])
                S.step("vector", lambda e, ti=ti: e.tensor_scalar(
                    comb_bf.ten[:, :], wr_s2.ten[:, :],
                    red.ten[:, 2:3], None, ALU.mult),
                    reads=[wr_s2, red], writes=[comb_bf])
                p = psT[ti % 2]
                S.step("tensor", lambda e, p=p: e.transpose(
                    p.ten[0:16, 0:128], comb_bf.ten[:, 0:16], ident_b.ten[:, :]),
                    reads=[comb_bf, ident_b], writes=[p])
                S.step("scalar", lambda e, ti=ti, p=p: e.copy(
                    combT.ten[0:16, ti * 128:(ti + 1) * 128], p.ten[0:16, 0:128]),
                    reads=[p], writes=[combT])

        with mem.scope(S) as m2:
            w_s = m2.sb("p5_ws", [128, NCB * D], BF16)
            swT = m2.sb("p5_swT", [128, NCB * (HSH // 2)], BF16)
            hwsh = m2.sb("p5_hwsh", [128, 8 * TOK], BF16)
            sdwT = m2.sb("p5_sdwT", [128, 8 * D], BF16)
            for hh in range(2):
                for wi_, wio in ((0, "shared_gate_w"), (1, "shared_up_w")):
                    S.dma("gpsimd", w_s.ten.ap().rearrange("p (n c) -> p n c", n=NCB),
                          ios[wio][hh * (HSH // 2):(hh + 1) * (HSH // 2), :]
                              .rearrange("(n p) c -> p n c", p=128),
                          writes=[w_s])
                    transpose_tiles(S, m2, w_s, 0, swT, 0, NCB, NCB, D, HSH // 2,
                                    ident_b, psT)
                    for ht in range(8):
                        for tc in range(4):
                            psum = psA if tc % 2 == 0 else psB
                            items = [(swT[:, cb * (HSH // 2) + ht * 128:
                                          cb * (HSH // 2) + (ht + 1) * 128],
                                      xfT[:, cb * TOK + tc * 512: cb * TOK + (tc + 1) * 512])
                                     for cb in range(NCB)]
                            S.mm_group(items, psum, reads=[swT, xfT])
                            if wi_ == 0:
                                S.step("scalar", lambda e, ht=ht, tc=tc, psum=psum: e.activation(
                                    hwsh.ten[:, ht * TOK + tc * 512: ht * TOK + (tc + 1) * 512],
                                    psum.ten[:, :], AF.Silu),
                                    reads=[psum], writes=[hwsh])
                            else:
                                S.step("vector", lambda e, ht=ht, tc=tc, psum=psum: e.tensor_mul(
                                    hwsh.ten[:, ht * TOK + tc * 512: ht * TOK + (tc + 1) * 512],
                                    psum.ten[:, :],
                                    hwsh.ten[:, ht * TOK + tc * 512: ht * TOK + (tc + 1) * 512]),
                                    reads=[psum], writes=[hwsh])
                S.dma("gpsimd", w_s.ten.ap().rearrange("p (n c) -> p n c", n=NCB),
                      ios["shared_down_w"][:, hh * (HSH // 2):(hh + 1) * (HSH // 2)]
                          .rearrange("(n p) c -> p n c", p=128),
                      writes=[w_s])
                transpose_tiles(S, m2, w_s, 0, sdwT, 0, NCB, 8, D, D, ident_b, psT)
                for ti in range(NT):
                    for oc in range(2):
                        psum = psA if oc == 0 else psB
                        items = [(hwsh[:, ht * TOK + ti * 128: ht * TOK + (ti + 1) * 128],
                                  sdwT[:, ht * D + oc * 512: ht * D + (oc + 1) * 512])
                                 for ht in range(8)]
                        S.mm_group(items, psum, reads=[hwsh, sdwT])
                        S.step("vector", lambda e, psum=psum, ti=ti, oc=oc: e.tensor_add(
                            acc[:, ti * D + oc * 512: ti * D + (oc + 1) * 512],
                            psum.ten[:, :],
                            acc[:, ti * D + oc * 512: ti * D + (oc + 1) * 512]),
                            reads=[psum], writes=[acc])

        with mem.scope(S) as m3:
            gw_t = [m3.sb(f"p5_gw{i}", [128, NCB * H_EXP], BF16) for i in range(2)]
            uw_t = [m3.sb(f"p5_uw{i}", [128, NCB * H_EXP], BF16) for i in range(2)]
            dw_t = [m3.sb(f"p5_dw{i}", [128, 2 * D], BF16) for i in range(2)]
            esel = m3.sb("p5_esel", [16, 16 * 128], BF16)
            ge_s = m3.sb("p5_ge", [128, 2 * TOK], BF16)
            hw_s = m3.sb("p5_hw", [128, 2 * TOK], BF16)
            S.dma("gpsimd", esel[:, :], ios["esel"], writes=[esel])
            for ex in range(NEXP):
                gw_s = gw_t[ex % 2]
                uw_s = uw_t[ex % 2]
                dw_s = dw_t[ex % 2]
                S.dma("gpsimd", gw_s.ten.ap().rearrange("p (n c) -> p n c", n=NCB),
                      ios["gate_w"][ex].rearrange("(n p) c -> p n c", p=128), writes=[gw_s])
                S.dma("gpsimd", uw_s.ten.ap().rearrange("p (n c) -> p n c", n=NCB),
                      ios["up_w"][ex].rearrange("(n p) c -> p n c", p=128), writes=[uw_s])
                S.dma("gpsimd", dw_s.ten.ap().rearrange("p (n c) -> p n c", n=2),
                      ios["down_w"][ex].rearrange("(n p) c -> p n c", p=128), writes=[dw_s])
                for tc in range(4):
                    S.mm_group([(esel[0:16, ex * 128:(ex + 1) * 128],
                                 combT[0:16, tc * 512:(tc + 1) * 512])], psC,
                               reads=[esel, combT])
                    for hb in range(2):
                        items = [(gw_s[:, cb * H_EXP + hb * 128: cb * H_EXP + (hb + 1) * 128],
                                  xfT[:, cb * TOK + tc * 512: cb * TOK + (tc + 1) * 512])
                                 for cb in range(NCB)]
                        S.mm_group(items, psA, reads=[gw_s, xfT])
                        S.step("scalar", lambda e, hb=hb, tc=tc: e.activation(
                            ge_s.ten[:, hb * TOK + tc * 512: hb * TOK + (tc + 1) * 512],
                            psA.ten[:, :], AF.Silu),
                            reads=[psA], writes=[ge_s])
                        items = [(uw_s[:, cb * H_EXP + hb * 128: cb * H_EXP + (hb + 1) * 128],
                                  xfT[:, cb * TOK + tc * 512: cb * TOK + (tc + 1) * 512])
                                 for cb in range(NCB)]
                        S.mm_group(items, psB, reads=[uw_s, xfT])
                        S.step("vector", lambda e, hb=hb, tc=tc: e.tensor_mul(
                            ge_s.ten[:, hb * TOK + tc * 512: hb * TOK + (tc + 1) * 512],
                            psB.ten[:, :],
                            ge_s.ten[:, hb * TOK + tc * 512: hb * TOK + (tc + 1) * 512]),
                            reads=[psB], writes=[ge_s])
                        S.step("vector", lambda e, hb=hb, tc=tc: e.tensor_mul(
                            hw_s.ten[:, hb * TOK + tc * 512: hb * TOK + (tc + 1) * 512],
                            psC.ten[:, :],
                            ge_s.ten[:, hb * TOK + tc * 512: hb * TOK + (tc + 1) * 512]),
                            reads=[psC, ge_s], writes=[hw_s])
                for ti in range(NT):
                    for oc in range(2):
                        psum = psA if oc == 0 else psB
                        items = [(hw_s[:, hb * TOK + ti * 128: hb * TOK + (ti + 1) * 128],
                                  dw_s[:, hb * D + oc * 512: hb * D + (oc + 1) * 512])
                                 for hb in range(2)]
                        S.mm_group(items, psum, reads=[hw_s, dw_s])
                        S.step("vector", lambda e, psum=psum, ti=ti, oc=oc: e.tensor_add(
                            acc[:, ti * D + oc * 512: ti * D + (oc + 1) * 512],
                            psum.ten[:, :],
                            acc[:, ti * D + oc * 512: ti * D + (oc + 1) * 512]),
                            reads=[psum], writes=[acc])

        S.dma("sync", ios["out"].rearrange("(n p) c -> p n c", p=128),
              acc.ten.ap().rearrange("p (n c) -> p n c", n=NT), reads=[acc])


def build_p5():
    nc = bass.Bass("TRN2", target_bir_lowering=False, debug=False)
    ios = {"const": {}}
    for name, shape, dt in CONST_SPECS:
        ios["const"][name] = nc.dram_tensor("c_" + name, shape, dt, kind="ExternalInput").ap()
    ios["x2"] = nc.dram_tensor("x2", [TOK, D], F32, kind="ExternalInput").ap()
    ios["yg_g"] = nc.dram_tensor("yg_g", [NCORES, 256, TOK], BF16, kind="ExternalInput").ap()
    ios["W_out"] = nc.dram_tensor("W_out", [D_IN, D], F32, kind="ExternalInput").ap()
    ios["router_w"] = nc.dram_tensor("router_w", [16, D], F32, kind="ExternalInput").ap()
    ios["corr_bias"] = nc.dram_tensor("corr_bias", [16], F32, kind="ExternalInput").ap()
    ios["gate_w"] = nc.dram_tensor("gate_w", [NEXP, D, H_EXP], F32, kind="ExternalInput").ap()
    ios["up_w"] = nc.dram_tensor("up_w", [NEXP, D, H_EXP], F32, kind="ExternalInput").ap()
    ios["down_w"] = nc.dram_tensor("down_w", [NEXP, H_EXP, D], F32, kind="ExternalInput").ap()
    ios["shared_gate_w"] = nc.dram_tensor("shared_gate_w", [HSH, D], F32, kind="ExternalInput").ap()
    ios["shared_up_w"] = nc.dram_tensor("shared_up_w", [HSH, D], F32, kind="ExternalInput").ap()
    ios["shared_down_w"] = nc.dram_tensor("shared_down_w", [D, HSH], F32, kind="ExternalInput").ap()
    ios["esel"] = nc.dram_tensor("esel", [16, 16 * 128], BF16, kind="ExternalInput").ap()
    ios["out"] = nc.dram_tensor("out", [TOK, D], F32, kind="ExternalOutput").ap()
    with contextlib.ExitStack() as ctx:
        mem = Mem(nc, ctx)
        S = Sched(nc)
        emit_p5(S, mem, ios)
        S.step("gpsimd", lambda e: e.nop(), waits=S.dma_barrier(), sem=None)
        block = ctx.enter_context(nc.Block())
        S.build(block)
    S.close()
    return nc


def p5_host_consts():
    import ml_dtypes
    esel = np.zeros((16, 16 * 128), np.float32)
    for ex in range(16):
        esel[ex, ex * 128:(ex + 1) * 128] = 1.0
    return esel.astype(ml_dtypes.bfloat16)



NTILE = D_IN // 128


def emit_p4a(S, mem, ios):
    NG = 16
    F = TOK
    dd_tok = ios.get("_dltdu_tok", {})
    bc_tok = ios.get("_bc_tok", S.dma_barrier())
    with mem.scope(S) as m1:
        e8 = m1.sb("a_e8", [128, NG * 128], BF16)
        ws = m1.sb("a_ws", [128, NG * 128], BF16)
        e16 = m1.sb("a_e16", [32, 256], BF16)
        alog = m1.sb("a_alog", [128, 256])
        alan = m1.sb("a_alan", [128, 256])
        bc_b = m1.sb("a_bc", [32, F], BF16)
        brep = m1.sb("a_brep", [128, F])
        crep = m1.sb("a_crep", [128, F])
        dlt_t = [m1.sb(f"a_dlt{i}", [128, F], BF16) for i in range(2)]
        du_t = [m1.sb(f"a_du{i}", [128, F], BF16) for i in range(2)]
        ones_b = m1.sb("a_ones", [128, F])
        dltf = m1.sb("a_dltf", [128, F])
        cdf = m1.sb("a_cdf", [128, F])
        cdbf_t = [m1.sb(f"a_cdbf{i}", [128, F], BF16) for i in range(2)]
        a_t = [m1.sb(f"a_a{i}", [128, F]) for i in range(2)]
        bu_t = [m1.sb(f"a_bu{i}", [128, F]) for i in range(2)]
        b_t = [m1.sb(f"a_b{i}", [128, F]) for i in range(2)]
        s_t = [m1.sb(f"a_s{i}", [128, F]) for i in range(3)]
        sc_t = [m1.sb(f"a_sc{i}", [128, F], BF16) for i in range(3)]
        y_t = [m1.sb(f"a_y{i}", [128, F], BF16) for i in range(2)]
        carry = m1.sb("a_carry", [128, NG])
        zcol = m1.sb("a_zcol", [128, 1])
        f_sb = m1.sb("a_f", [128, 256])
        q_sb = m1.sb("a_q", [128, 256])
        qtmp = m1.sb("a_qtmp", [128, NG])
        psE = [m1.ps(f"a_psE{i}", [128, 512]) for i in range(4)]
        psY = m1.ps("a_psY", [128, 2048])
        psU = psE

        S.dma("gpsimd", e8[:, :], ios["e8sel"], writes=[e8])
        S.dma("gpsimd", ws[:, :], ios["wsel"], writes=[ws])
        S.dma("gpsimd", e16[:, :], ios["e16sel"], writes=[e16])
        S.dma("gpsimd", alog[:, :], ios["alog_lanes"], writes=[alog])
        S.dma("gpsimd", bc_b[:, :], ios["bc"], writes=[bc_b], waits=bc_tok)
        S.step("scalar", lambda e: e.activation(alan.ten[:, :], alog.ten[:, :], AF.Exp),
               reads=[alog], writes=[alan])
        S.step("vector", lambda e: e.tensor_scalar(alan.ten[:, :], alan.ten[:, :],
                                                   -1.0, None, ALU.mult),
               reads=[alan], writes=[alan])
        S.step("vector", lambda e: e.memset(ones_b.ten[:, :], 1.0), writes=[ones_b])
        S.step("vector", lambda e: e.memset(zcol.ten[:, :], 0.0), writes=[zcol])

        for which, rep in ((0, brep), (1, crep)):
            for sub in range(4):
                ps = psE[sub]
                S.mm_group([(e16[0:32, which * 128:(which + 1) * 128],
                             bc_b[0:32, sub * 512:(sub + 1) * 512])],
                           ps, reads=[e16, bc_b])
                S.step("scalar", lambda e, rep=rep, sub=sub, ps=ps: e.copy(
                    rep.ten[:, sub * 512:(sub + 1) * 512], ps.ten[:, :]),
                    reads=[ps], writes=[rep])

        for T in range(NTILE):
            dlt = dlt_t[T % 2]
            du = du_t[T % 2]
            cdbf = cdbf_t[T % 2]
            yb = y_t[T % 2]
            tw = dd_tok.get(T, [])
            S.dma("sync", dlt[:, :], ios["dlt"][T * 128:(T + 1) * 128, :],
                  writes=[dlt], waits=tw)
            S.dma("sync", du[:, :], ios["du"][T * 128:(T + 1) * 128, :],
                  writes=[du], waits=tw)
            S.step("scalar", lambda e, dlt=dlt: e.copy(
                dltf.ten[:, :], dlt.ten[:, :]),
                reads=[dlt], writes=[dltf])
            S.step("vector", lambda e: e.tensor_tensor_scan(
                cdf.ten[:, :], ones_b.ten[:, :], dltf.ten[:, :],
                zcol.ten[:, 0:1], ALU.mult, ALU.add),
                reads=[ones_b, dltf, zcol], writes=[cdf])
            S.step("gpsimd", lambda e, cdbf=cdbf: e.tensor_copy(
                cdbf.ten[:, :], cdf.ten[:, :]),
                reads=[cdf], writes=[cdbf])
            S.dma("scalar", ios["cd"][T * 128:(T + 1) * 128, :], cdbf[:, :],
                  reads=[cdbf])
            for g in range(NG):
                S.mm_group([(e8[:, g * 128:(g + 1) * 128],
                             cdbf[:, F - 1:F])], psU[0],
                           reads=[e8, cdbf],
                           psum_slice=(slice(None), slice(g, g + 1)))
            S.step("vector", lambda e, T=T: e.tensor_mul(
                qtmp.ten[:, :], psU[0].ten[:, 0:NG],
                alan.ten[:, T * NG:(T + 1) * NG]),
                reads=[psU[0], alan], writes=[qtmp])
            S.step("scalar", lambda e, T=T: e.activation(
                q_sb.ten[:, T * NG:(T + 1) * NG], qtmp.ten[:, :], AF.Exp),
                reads=[qtmp], writes=[q_sb])
            ystart = [False] * 4

            def yemit(g, sc_sb):
                for sub in range(4):
                    wl = []
                    if not ystart[sub]:
                        if psY.wr is not None:
                            wl.append(psY.wr)
                        wl.extend(psY.rds)
                    if sc_sb.wr is not None:
                        wl.append(sc_sb.wr)
                    if ws.wr is not None:
                        wl.append(ws.wr)
                    toky = S._record(
                        "tensor",
                        lambda e, g=g, sub=sub, s=(not ystart[sub]), sc_sb=sc_sb,
                               last=(g == NG - 1): e.matmul(
                            psY.ten[:, sub * 512:(sub + 1) * 512],
                            ws.ten[:, g * 128:(g + 1) * 128],
                            sc_sb.ten[:, sub * 512:(sub + 1) * 512],
                            start=s, stop=last, skip_group_check=True),
                        wl, "pe", 1)
                    ystart[sub] = True
                    sc_sb.rds.append(toky)
                    ws.rds.append(toky)
                    if g == NG - 1 and sub == 3:
                        psY.wr = toky
                        psY.rds = []

            pend = []
            for g in range(NG):
                col = T * NG + g
                a_sb = a_t[g % 2]
                bu_sb = bu_t[g % 2]
                b_sb = b_t[g % 2]
                s_sb = s_t[g % 3]
                sc_sb = sc_t[g % 3]
                for sub in range(4):
                    ps = psE[sub]
                    S.mm_group([(e8[:, g * 128:(g + 1) * 128],
                                 dlt[:, sub * 512:(sub + 1) * 512])],
                               ps, reads=[e8, dlt])
                    S.step("scalar", lambda e, sub=sub, ps=ps, col=col, a_sb=a_sb: e.activation(
                        a_sb.ten[:, sub * 512:(sub + 1) * 512], ps.ten[:, :],
                        AF.Exp, scale=alan.ten[:, col:col + 1]),
                        reads=[ps, alan], writes=[a_sb])
                for sub in range(4):
                    ps = psE[sub]
                    S.mm_group([(e8[:, g * 128:(g + 1) * 128],
                                 du[:, sub * 512:(sub + 1) * 512])],
                               ps, reads=[e8, du])
                    S.step("vector", lambda e, sub=sub, ps=ps, b_sb=b_sb: e.tensor_mul(
                        b_sb.ten[:, sub * 512:(sub + 1) * 512], ps.ten[:, :],
                        brep.ten[:, sub * 512:(sub + 1) * 512]),
                        reads=[ps, brep], writes=[b_sb])
                if len(pend) >= 2:
                    yemit(*pend.pop(0))
                S.step("vector", lambda e, a_sb=a_sb, b_sb=b_sb, s_sb=s_sb: e.tensor_tensor_scan(
                    s_sb.ten[:, :], a_sb.ten[:, :], b_sb.ten[:, :],
                    zcol.ten[:, 0:1], ALU.mult, ALU.add),
                    reads=[a_sb, b_sb, zcol], writes=[s_sb])
                S.step("scalar", lambda e, col=col, s_sb=s_sb: e.copy(
                    f_sb.ten[:, col:col + 1], s_sb.ten[:, F - 1:F]),
                    reads=[s_sb], writes=[f_sb])
                S.step("gpsimd", lambda e, s_sb=s_sb, sc_sb=sc_sb: e.tensor_mul(
                    sc_sb.ten[:, :], s_sb.ten[:, :], crep.ten[:, :]),
                    reads=[s_sb, crep], writes=[sc_sb])
                pend.append((g, sc_sb))
            for pp in pend:
                yemit(*pp)
            S.step("scalar", lambda e, yb=yb: e.copy(yb.ten[:, :], psY.ten[:, :]),
                   reads=[psY], writes=[yb])
            S.dma("sync", ios["y_local"][T * 128:(T + 1) * 128, :], yb[:, :],
                  reads=[yb])
        S.dma("sync", ios["fq"][0], f_sb[:, :], reads=[f_sb])
        S.dma("sync", ios["fq"][1], q_sb[:, :], reads=[q_sb])



def emit_p4b(S, mem, ios):
    NG = 16
    F = TOK
    with mem.scope(S) as m1:
        e8 = m1.sb("b_e8", [128, NG * 128], BF16)
        ws = m1.sb("b_ws", [128, NG * 128], BF16)
        e16 = m1.sb("b_e16", [32, 256], BF16)
        alog = m1.sb("b_alog", [128, 256])
        alan = m1.sb("b_alan", [128, 256])
        dcol = m1.sb("b_dcol", [128, NTILE])
        oh4 = m1.sb("b_oh4", [128, 4])
        fqg = m1.sb("b_fqg", [128, 4 * 2 * 256])
        bc_b = m1.sb("b_bc", [32, F], BF16)
        crep = m1.sb("b_crep", [128, F])
        acc = m1.sb("b_acc", [128, 256])
        sel = m1.sb("b_sel", [128, 256])
        sfin = m1.sb("b_sfin", [128, 256])
        sfbf = m1.sb("b_sfbf", [128, 256], BF16)
        tmp = m1.sb("b_tmp", [128, 256])
        cd_t = [m1.sb(f"b_cd{i}", [128, F], BF16) for i in range(2)]
        y_t = [m1.sb(f"b_y{i}", [128, F], BF16) for i in range(2)]
        sres_t = [m1.sb(f"b_sres{i}", [128, F], BF16) for i in range(2)]
        p_t = [m1.sb(f"b_p{i}", [128, F]) for i in range(3)]
        sc_t = [m1.sb(f"b_sc{i}", [128, F], BF16) for i in range(3)]
        yg_t = [m1.sb(f"b_yg{i}", [128, F], BF16) for i in range(2)]
        yf = m1.sb("b_yf", [128, F])
        rcol = m1.sb("b_rcol", [128, 1])
        psE = [m1.ps(f"b_psE{i}", [128, 512]) for i in range(3)]
        psY = m1.ps("b_psY", [128, 2048])
        psR = m1.ps("b_psR", [128, 512])

        S.dma("gpsimd", e8[:, :], ios["e8sel"], writes=[e8])
        S.dma("gpsimd", ws[:, :], ios["wsel"], writes=[ws])
        S.dma("gpsimd", e16[:, :], ios["e16sel"], writes=[e16])
        S.dma("gpsimd", alog[:, :], ios["alog_lanes"], writes=[alog])
        S.dma("gpsimd", dcol[:, :], ios["d_col"], writes=[dcol])
        S.dma("gpsimd", oh4[:, :], ios["oh4"], writes=[oh4])
        S.dma("sync", fqg.ten.ap().rearrange("p (j two c) -> p j two c", j=4, two=2),
              ios["fq_g"].rearrange("j two p c -> p j two c"), writes=[fqg])
        S.dma("sync", bc_b[:, :], ios["bc"], writes=[bc_b])
        S.step("scalar", lambda e: e.activation(alan.ten[:, :], alog.ten[:, :], AF.Exp),
               reads=[alog], writes=[alan])
        S.step("vector", lambda e: e.tensor_scalar(alan.ten[:, :], alan.ten[:, :],
                                                   -1.0, None, ALU.mult),
               reads=[alan], writes=[alan])
        for sub in range(4):
            ps = psE[sub % 3]
            S.mm_group([(e16[0:32, 128:256],
                         bc_b[0:32, sub * 512:(sub + 1) * 512])],
                       ps, reads=[e16, bc_b])
            S.step("scalar", lambda e, sub=sub, ps=ps: e.copy(
                crep.ten[:, sub * 512:(sub + 1) * 512], ps.ten[:, :]),
                reads=[ps], writes=[crep])
        S.step("vector", lambda e: e.memset(acc.ten[:, :], 0.0), writes=[acc])
        S.step("vector", lambda e: e.memset(sel.ten[:, :], 0.0), writes=[sel])
        for k in range(4):
            if k > 0:
                fj = fqg[:, (k - 1) * 512:(k - 1) * 512 + 256]
                qj = fqg[:, (k - 1) * 512 + 256:(k - 1) * 512 + 512]
                S.step("vector", lambda e, qj=qj: e.tensor_mul(
                    acc.ten[:, :], acc.ten[:, :], qj),
                    reads=[fqg], writes=[acc])
                S.step("vector", lambda e, fj=fj: e.tensor_add(
                    acc.ten[:, :], acc.ten[:, :], fj),
                    reads=[fqg], writes=[acc])
            S.step("vector", lambda e, k=k: e.tensor_scalar(
                tmp.ten[:, :], acc.ten[:, :], oh4.ten[:, k:k + 1], None, ALU.mult),
                reads=[acc, oh4], writes=[tmp])
            S.step("vector", lambda e: e.tensor_add(
                sel.ten[:, :], sel.ten[:, :], tmp.ten[:, :]),
                reads=[tmp], writes=[sel])
        fj = fqg[:, 3 * 512:3 * 512 + 256]
        qj = fqg[:, 3 * 512 + 256:4 * 512]
        S.step("vector", lambda e, qj=qj: e.tensor_mul(
            sfin.ten[:, :], acc.ten[:, :], qj),
            reads=[acc, fqg], writes=[sfin])
        S.step("vector", lambda e, fj=fj: e.tensor_add(
            sfin.ten[:, :], sfin.ten[:, :], fj),
            reads=[fqg], writes=[sfin])
        S.step("vector", lambda e: e.tensor_copy(sfbf.ten[:, :], sfin.ten[:, :]),
               reads=[sfin], writes=[sfbf])

        for T in range(NTILE):
            cd = cd_t[T % 2]
            yb = y_t[T % 2]
            sres = sres_t[T % 2]
            yg = yg_t[T % 2]
            S.dma("sync", cd[:, :], ios["cd"][T * 128:(T + 1) * 128, :], writes=[cd])
            S.dma("sync", yb[:, :], ios["y_local"][T * 128:(T + 1) * 128, :], writes=[yb])
            S.dma("sync", sres[:, :], ios["sres"][T * 128:(T + 1) * 128, :], writes=[sres])
            ystart = [False] * 4

            def yemit(g, sc_sb):
                for sub in range(4):
                    wl = []
                    if not ystart[sub]:
                        if psY.wr is not None:
                            wl.append(psY.wr)
                        wl.extend(psY.rds)
                    if sc_sb.wr is not None:
                        wl.append(sc_sb.wr)
                    if ws.wr is not None:
                        wl.append(ws.wr)
                    toky = S._record(
                        "tensor",
                        lambda e, g=g, sub=sub, s=(not ystart[sub]), sc_sb=sc_sb,
                               last=(g == NG - 1): e.matmul(
                            psY.ten[:, sub * 512:(sub + 1) * 512],
                            ws.ten[:, g * 128:(g + 1) * 128],
                            sc_sb.ten[:, sub * 512:(sub + 1) * 512],
                            start=s, stop=last, skip_group_check=True),
                        wl, "pe", 1)
                    ystart[sub] = True
                    sc_sb.rds.append(toky)
                    ws.rds.append(toky)
                    if g == NG - 1 and sub == 3:
                        psY.wr = toky
                        psY.rds = []

            pend = []
            for g in range(NG):
                col = T * NG + g
                p_sb = p_t[g % 3]
                sc_sb = sc_t[g % 3]
                for sub in range(4):
                    ps = psE[sub % 3]
                    S.mm_group([(e8[:, g * 128:(g + 1) * 128],
                                 cd[:, sub * 512:(sub + 1) * 512])],
                               ps, reads=[e8, cd])
                    S.step("scalar", lambda e, sub=sub, ps=ps, col=col, p_sb=p_sb: e.activation(
                        p_sb.ten[:, sub * 512:(sub + 1) * 512], ps.ten[:, :],
                        AF.Exp, scale=alan.ten[:, col:col + 1]),
                        reads=[ps, alan], writes=[p_sb])
                if len(pend) >= 2:
                    yemit(*pend.pop(0))
                S.step("vector", lambda e, col=col, p_sb=p_sb: e.tensor_scalar(
                    p_sb.ten[:, :], p_sb.ten[:, :], sel.ten[:, col:col + 1],
                    None, ALU.mult),
                    reads=[sel], writes=[p_sb])
                S.step("vector", lambda e, p_sb=p_sb, sc_sb=sc_sb: e.tensor_mul(
                    sc_sb.ten[:, :], p_sb.ten[:, :], crep.ten[:, :]),
                    reads=[p_sb, crep], writes=[sc_sb])
                pend.append((g, sc_sb))
            for pp in pend:
                yemit(*pp)
            items = [(ws[:, g * 128:(g + 1) * 128],
                      sfbf[:, (T * NG + g):(T * NG + g) + 1]) for g in range(NG)]
            S.mm_group(items, psR, reads=[ws, sfbf],
                       psum_slice=(slice(None), slice(0, 1)))
            S.step("vector", lambda e, T=T: e.tensor_mul(
                rcol.ten[:, 0:1], psR.ten[:, 0:1], dcol.ten[:, T:T + 1]),
                reads=[psR, dcol], writes=[rcol])
            S.step("vector", lambda e, yb=yb: e.tensor_add(
                yf.ten[:, :], psY.ten[:, :], yb.ten[:, :]),
                reads=[psY, yb], writes=[yf])
            S.step("scalar", lambda e: e.add(
                yf.ten[:, :], yf.ten[:, :], rcol.ten[:, 0:1]),
                reads=[rcol], writes=[yf])
            S.step("vector", lambda e, yg=yg, sres=sres: e.tensor_mul(
                yg.ten[:, :], yf.ten[:, :], sres.ten[:, :]),
                reads=[yf, sres], writes=[yg])
            S.dma("sync", ios["ygate"][T * 128:(T + 1) * 128, :], yg[:, :],
                  reads=[yg])


def build_p34():
    nc = bass.Bass("TRN2", target_bir_lowering=False, debug=False)
    ios = {"const": {}}
    for name, shape, dt in CONST_SPECS:
        ios["const"][name] = nc.dram_tensor("c_" + name, shape, dt, kind="ExternalInput").ap()
    ios["x2"] = nc.dram_tensor("x2", [TOK, D], F32, kind="ExternalInput").ap()
    ios["halo_g"] = nc.dram_tensor("halo_g", [NCORES, 3, D], F32, kind="ExternalInput").ap()
    ios["onehot"] = nc.dram_tensor("onehot", [NCORES, 1], F32, kind="ExternalInput").ap()
    ios["W_in"] = nc.dram_tensor("W_in", [D, 2 * D_IN], F32, kind="ExternalInput").ap()
    ios["conv_w"] = nc.dram_tensor("conv_w", [4, D_IN, D_IN], F32, kind="ExternalInput").ap()
    ios["conv_b"] = nc.dram_tensor("conv_b", [D_IN], F32, kind="ExternalInput").ap()
    ios["W_x"] = nc.dram_tensor("W_x", [D_IN, 96], F32, kind="ExternalInput").ap()
    ios["W_dt"] = nc.dram_tensor("W_dt", [DT_RANK, D_IN], F32, kind="ExternalInput").ap()
    ios["b_dt"] = nc.dram_tensor("b_dt", [D_IN], F32, kind="ExternalInput").ap()
    ios["e8sel"] = nc.dram_tensor("e8sel", [128, 16 * 128], BF16, kind="ExternalInput").ap()
    ios["wsel"] = nc.dram_tensor("wsel", [128, 16 * 128], BF16, kind="ExternalInput").ap()
    ios["e16sel"] = nc.dram_tensor("e16sel", [32, 256], BF16, kind="ExternalInput").ap()
    ios["alog_lanes"] = nc.dram_tensor("alog_lanes", [128, 256], F32, kind="ExternalInput").ap()
    ios["sres"] = nc.dram_tensor("sres", [D_IN, TOK], BF16, kind="ExternalOutput").ap()
    ios["bc"] = nc.dram_tensor("bc", [32, TOK], BF16, kind="ExternalOutput").ap()
    ios["y_local"] = nc.dram_tensor("y_local", [D_IN, TOK], BF16, kind="ExternalOutput").ap()
    ios["cd"] = nc.dram_tensor("cd", [D_IN, TOK], BF16, kind="ExternalOutput").ap()
    ios["fq"] = nc.dram_tensor("fq", [2, 128, 256], F32, kind="ExternalOutput").ap()
    ios["dlt"] = nc.dram_tensor("dlt", [D_IN, TOK], BF16, kind="Internal").ap()
    ios["du"] = nc.dram_tensor("du", [D_IN, TOK], BF16, kind="Internal").ap()
    ios["halo_scr"] = nc.dram_tensor("halo_scr", [3, D], F32, kind="Internal").ap()
    ios["u_scr"] = nc.dram_tensor("u_scr", [D_IN, TOK], BF16, kind="Internal").ap()
    with contextlib.ExitStack() as ctx:
        mem = Mem(nc, ctx)
        S = Sched(nc)
        emit_p3(S, mem, ios)
        emit_p4a(S, mem, ios)
        S.step("gpsimd", lambda e: e.nop(), waits=S.dma_barrier(), sem=None)
        block = ctx.enter_context(nc.Block())
        S.build(block)
    S.close()
    return nc


def build_p45():
    nc = bass.Bass("TRN2", target_bir_lowering=False, debug=False)
    ios = {"const": {}}
    for name, shape, dt in CONST_SPECS:
        ios["const"][name] = nc.dram_tensor("c_" + name, shape, dt, kind="ExternalInput").ap()
    ios["x2"] = nc.dram_tensor("x2", [TOK, D], F32, kind="ExternalInput").ap()
    ios["fq_g"] = nc.dram_tensor("fq_g", [4, 2, 128, 256], F32, kind="ExternalInput").ap()
    ios["oh4"] = nc.dram_tensor("oh4", [128, 4], F32, kind="ExternalInput").ap()
    ios["cd"] = nc.dram_tensor("cd", [D_IN, TOK], BF16, kind="ExternalInput").ap()
    ios["y_local"] = nc.dram_tensor("y_local", [D_IN, TOK], BF16, kind="ExternalInput").ap()
    ios["sres"] = nc.dram_tensor("sres", [D_IN, TOK], BF16, kind="ExternalInput").ap()
    ios["bc"] = nc.dram_tensor("bc", [32, TOK], BF16, kind="ExternalInput").ap()
    ios["e8sel"] = nc.dram_tensor("e8sel", [128, 16 * 128], BF16, kind="ExternalInput").ap()
    ios["wsel"] = nc.dram_tensor("wsel", [128, 16 * 128], BF16, kind="ExternalInput").ap()
    ios["e16sel"] = nc.dram_tensor("e16sel", [32, 256], BF16, kind="ExternalInput").ap()
    ios["alog_lanes"] = nc.dram_tensor("alog_lanes", [128, 256], F32, kind="ExternalInput").ap()
    ios["d_col"] = nc.dram_tensor("d_col", [128, NTILE], F32, kind="ExternalInput").ap()
    ios["W_out"] = nc.dram_tensor("W_out", [D_IN, D], F32, kind="ExternalInput").ap()
    ios["router_w"] = nc.dram_tensor("router_w", [16, D], F32, kind="ExternalInput").ap()
    ios["corr_bias"] = nc.dram_tensor("corr_bias", [16], F32, kind="ExternalInput").ap()
    ios["gate_w"] = nc.dram_tensor("gate_w", [NEXP, D, H_EXP], F32, kind="ExternalInput").ap()
    ios["up_w"] = nc.dram_tensor("up_w", [NEXP, D, H_EXP], F32, kind="ExternalInput").ap()
    ios["down_w"] = nc.dram_tensor("down_w", [NEXP, H_EXP, D], F32, kind="ExternalInput").ap()
    ios["shared_gate_w"] = nc.dram_tensor("shared_gate_w", [HSH, D], F32, kind="ExternalInput").ap()
    ios["shared_up_w"] = nc.dram_tensor("shared_up_w", [HSH, D], F32, kind="ExternalInput").ap()
    ios["shared_down_w"] = nc.dram_tensor("shared_down_w", [D, HSH], F32, kind="ExternalInput").ap()
    ios["esel"] = nc.dram_tensor("esel", [16, 16 * 128], BF16, kind="ExternalInput").ap()
    ios["out"] = nc.dram_tensor("out", [TOK, D], F32, kind="ExternalOutput").ap()
    ygate_t = nc.dram_tensor("ygate", [D_IN, TOK], BF16, kind="Internal")
    ios["ygate"] = ygate_t.ap()
    with contextlib.ExitStack() as ctx:
        mem = Mem(nc, ctx)
        S = Sched(nc)
        emit_p4b(S, mem, ios)
        ios["yg_g"] = ygate_t.ap().rearrange("(j p) t -> j p t", j=NCORES)
        S.set_fence()
        emit_p5(S, mem, ios)
        S.step("gpsimd", lambda e: e.nop(), waits=S.dma_barrier(), sem=None)
        block = ctx.enter_context(nc.Block())
        S.build(block)
    S.close()
    return nc



def kernel(**inputs):
    import ml_dtypes
    x_full = np.ascontiguousarray(np.asarray(inputs["L0_attn_x"], np.float32)
                                  .reshape(BATCH * T_FULL, D))
    consts = [build_constants(c) for c in range(NCORES)]
    e8, ws, e16 = p4_host_consts()
    esel = p5_host_consts()

    def cmaps(extra_fn):
        maps = []
        for c in range(NCORES):
            m = {"c_" + k: v for k, v in consts[c].items()}
            m.update(extra_fn(c))
            maps.append(m)
        return maps

    nc = build_p0()
    r0 = run_bass_kernel_spmd(nc, cmaps(lambda c: {
        "x": np.ascontiguousarray(x_full[c * TOK:(c + 1) * TOK]),
        "wq": inputs["L0_attn_wq"], "wk": inputs["L0_attn_wk"],
        "wv": inputs["L0_attn_wv"],
        "w_freq_real": inputs["L0_ffn_w_freq_real"],
        "w_freq_imag": inputs["L0_ffn_w_freq_imag"]}),
        core_ids=list(range(NCORES))).results
    M = np.concatenate([np.asarray(r0[c]["m_slice"]) for c in range(NCORES)], 0)

    nc = build_p1()
    def p1_in(c):
        qg = np.stack([np.asarray(r0[j]["q_a2a"]).reshape(NCORES, 128, TOK)[c]
                       for j in range(NCORES)])
        kg = np.stack([np.asarray(r0[j]["k_a2a"]).reshape(NCORES, 128, TOK)[c]
                       for j in range(NCORES)])
        vg = np.stack([np.asarray(r0[j]["v_a2a"]).reshape(NCORES, TOK, 128)[c]
                       for j in range(NCORES)])
        return {"qg": qg, "kg": kg, "vg": vg}
    r1 = run_bass_kernel_spmd(nc, cmaps(p1_in), core_ids=list(range(NCORES))).results

    nc = build_p2()
    def p2_in(c):
        ao = np.stack([np.asarray(r1[j]["ao_a2a"]).reshape(NCORES, 128, TOK)[c]
                       for j in range(NCORES)])
        return {"x": np.ascontiguousarray(x_full[c * TOK:(c + 1) * TOK]),
                "ao": ao, "wo": inputs["L0_attn_wo"], "M": M,
                "ffn_bias": inputs["L0_ffn_bias"]}
    r2 = run_bass_kernel_spmd(nc, cmaps(p2_in), core_ids=list(range(NCORES))).results
    halo_g = np.stack([np.asarray(r2[j]["x2tail"]) for j in range(NCORES)])

    A_log = np.asarray(inputs["L1_attn_A_log"], np.float32)
    alog_lanes = np.ascontiguousarray(
        A_log.reshape(16, 16, 8, 16).transpose(2, 3, 0, 1).reshape(128, 256))
    d_col = np.ascontiguousarray(
        np.asarray(inputs["L1_attn_D"], np.float32).reshape(NTILE, 128).T)
    nc = build_p34()
    def p34_in(c):
        oh = np.zeros((NCORES, 1), np.float32)
        if c % 4 != 0:
            oh[c - 1, 0] = 1.0
        return {"x2": np.asarray(r2[c]["x2"]), "halo_g": halo_g, "onehot": oh,
                "W_in": inputs["L1_attn_W_in"], "conv_w": inputs["L1_attn_conv_w"],
                "conv_b": inputs["L1_attn_conv_b"], "W_x": inputs["L1_attn_W_x"],
                "W_dt": inputs["L1_attn_W_dt"], "b_dt": inputs["L1_attn_b_dt"],
                "e8sel": e8, "wsel": ws, "e16sel": e16,
                "alog_lanes": alog_lanes}
    r34 = run_bass_kernel_spmd(nc, cmaps(p34_in), core_ids=list(range(NCORES))).results

    fq_all = [np.asarray(r34[j]["fq"]) for j in range(NCORES)]
    nc = build_p45()
    def p45_in(c):
        g0 = 4 * (c // 4)
        oh4 = np.zeros((128, 4), np.float32)
        oh4[:, c % 4] = 1.0
        return {"x2": np.asarray(r2[c]["x2"]),
                "fq_g": np.stack(fq_all[g0:g0 + 4]),
                "oh4": oh4,
                "cd": np.asarray(r34[c]["cd"]),
                "y_local": np.asarray(r34[c]["y_local"]),
                "sres": np.asarray(r34[c]["sres"]),
                "bc": np.asarray(r34[c]["bc"]),
                "e8sel": e8, "wsel": ws, "e16sel": e16,
                "alog_lanes": alog_lanes, "d_col": d_col,
                "W_out": inputs["L1_attn_W_out"],
                "router_w": inputs["L1_ffn_router_w"],
                "corr_bias": inputs["L1_ffn_correction_bias"],
                "gate_w": inputs["L1_ffn_gate_w"], "up_w": inputs["L1_ffn_up_w"],
                "down_w": inputs["L1_ffn_down_w"],
                "shared_gate_w": inputs["L1_ffn_shared_gate_w"],
                "shared_up_w": inputs["L1_ffn_shared_up_w"],
                "shared_down_w": inputs["L1_ffn_shared_down_w"], "esel": esel}
    r5 = run_bass_kernel_spmd(nc, cmaps(p45_in), core_ids=list(range(NCORES))).results

    out = np.concatenate([np.asarray(r5[c]["out"]) for c in range(NCORES)], 0)
    return out.reshape(BATCH, T_FULL, D).astype(np.float32)



# version 09
# speedup vs baseline: 1.5346x; 1.0909x over previous
import contextlib
import numpy as np
import concourse.bass as bass
import concourse.mybir as mybir
from concourse.bass_utils import run_bass_kernel_spmd
from concourse.aot_env import allow_ici_all_to_all

F32 = mybir.dt.float32
BF16 = mybir.dt.bfloat16
AF = mybir.ActivationFunctionType
ALU = mybir.AluOpType
AXX = mybir.AxisListType.X

NCORES = 8
D = 1024
TOK = 2048
T_FULL = 8192
BATCH = 2
NHEAD = 16
DK = 64
D_IN = 2048
N_SSM = 16
DT_RANK = 64
NEXP = 16
H_EXP = 256
HSH = 2048
EPS = 1e-5
NT = TOK // 128
NCB = D // 128

ENGINES = ("tensor", "scalar", "vector", "gpsimd", "sync")
DMA_SEM = {"sync": "dmaS", "gpsimd": "dmaG", "scalar": "dmaA"}



def build_constants(core):
    c = {}
    c["ident"] = np.eye(128, dtype=np.float32)
    k = np.arange(D, dtype=np.float64)
    ang = 2.0 * np.pi * np.outer(k, k) / D
    c["dftC"] = np.cos(ang).astype(np.float32)
    c["dftS"] = np.sin(ang).astype(np.float32)
    c["dftSneg"] = (-np.sin(ang)).astype(np.float32)
    pos0 = (core % 4) * TOK
    t = np.arange(pos0, pos0 + TOK, dtype=np.float64)
    freqs = 1.0 / (10000.0 ** (np.arange(0, DK, 2, dtype=np.float64) / DK))
    angs = np.outer(freqs, t)
    cos2 = np.repeat(np.cos(angs), 2, axis=0)
    sin2 = np.repeat(np.sin(angs), 2, axis=0)
    sgn = np.tile(np.array([-1.0, 1.0]), DK // 2)[:, None]
    c["ropeCos"] = np.tile(cos2, (2, 1)).astype(np.float32)
    c["ropeSin"] = np.tile(sin2 * sgn, (2, 1)).astype(np.float32)
    for j in range(4):
        kk = np.arange(128)[:, None] + 128 * j
        qq = np.arange(512)[None, :]
        c[f"cmask{j}"] = (kk <= qq).astype(np.float32)
    c["ones1x128"] = np.ones((1, 128), dtype=np.float32)
    c["epscol"] = np.full((128, 1), EPS, dtype=np.float32)
    c["onecol"] = np.ones((128, 1), dtype=np.float32)
    r0 = core * 128
    c["dftC_r"] = np.ascontiguousarray(c["dftC"][:, r0:r0 + 128])
    c["dftS_r"] = np.ascontiguousarray(c["dftS"][:, r0:r0 + 128])
    c["dftSn_r"] = np.ascontiguousarray(c["dftSneg"][:, r0:r0 + 128])
    return c

CONST_SPECS = [
    ("ident", [128, 128], F32),
    ("dftC", [D, D], F32),
    ("dftS", [D, D], F32),
    ("dftSneg", [D, D], F32),
    ("ropeCos", [128, TOK], F32),
    ("ropeSin", [128, TOK], F32),
    ("cmask0", [128, 512], F32),
    ("cmask1", [128, 512], F32),
    ("cmask2", [128, 512], F32),
    ("cmask3", [128, 512], F32),
    ("ones1x128", [1, 128], F32),
    ("epscol", [128, 1], F32),
    ("onecol", [128, 1], F32),
    ("dftC_r", [D, 128], F32),
    ("dftS_r", [D, 128], F32),
    ("dftSn_r", [D, 128], F32),
]



class Buf:
    __slots__ = ("ten", "wr", "rds", "name")

    def __init__(self, ten, name):
        self.ten = ten
        self.name = name
        self.wr = None
        self.rds = []

    def __getitem__(self, idx):
        return self.ten[idx]

    @property
    def ap(self):
        return self.ten


class Sched:

    def __init__(self, nc):
        self.nc = nc
        self.steps = {e: [] for e in ENGINES}
        self.sem_handles = {}
        self.counts = {}
        self._ctx = contextlib.ExitStack()

    def sem(self, name):
        if name not in self.sem_handles:
            self.sem_handles[name] = self._ctx.enter_context(self.nc.semaphore(name))
            self.counts[name] = 0
        return self.sem_handles[name]

    fence = None

    def all_tokens(self):
        return [(s, v) for s, v in self.counts.items() if v > 0]

    def set_fence(self):
        self.fence = self.all_tokens()

    def _record(self, engine, fn, waits, sem, inc):
        wmap = {}
        for w in waits:
            if w is None:
                continue
            s, v = w
            if s not in wmap or wmap[s] < v:
                wmap[s] = v
        wl = sorted(wmap.items())
        tok = None
        if sem is not None:
            self.sem(sem)
            self.counts[sem] += inc
            tok = (sem, self.counts[sem])
        self.steps[engine].append((wl, fn, tok, inc if sem else 0))
        return tok

    def step(self, engine, fn, reads=(), writes=(), waits=(), sem="x", inc=1):
        if sem == "x":
            sem = {"tensor": "pe", "scalar": "se", "vector": "ve",
                   "gpsimd": "ge", "sync": "sy"}[engine]
        wl = list(waits)
        for b in reads:
            if b.wr is not None:
                wl.append(b.wr)
        for b in writes:
            if b.wr is None and b.rds == [] and self.fence:
                wl.extend(self.fence)
            if b.wr is not None:
                wl.append(b.wr)
            wl.extend(b.rds)
        tok = self._record(engine, fn, wl, sem, inc)
        for b in reads:
            if tok is not None:
                b.rds.append(tok)
        for b in writes:
            b.wr = tok
            b.rds = []
        return tok

    DMA_LANES = 4

    def dma(self, queue, out, in_, reads=(), writes=(), waits=()):
        if not hasattr(self, "dma_lane"):
            self.dma_lane = {}
        lane = self.dma_lane.get(queue, 0)
        self.dma_lane[queue] = (lane + 1) % self.DMA_LANES
        sname = DMA_SEM[queue] + str(lane)
        self.sem(sname)
        prev = self.counts[sname]
        wl = list(waits) + ([(sname, prev)] if prev else [])
        return self.step(
            queue, lambda e, o=out, i=in_: e.dma_start(out=o, in_=i),
            reads=reads, writes=writes, waits=wl, sem=sname, inc=16)

    def dma_barrier(self):
        toks = []
        for q in DMA_SEM.values():
            for lane in range(self.DMA_LANES):
                sname = q + str(lane)
                if self.counts.get(sname):
                    toks.append((sname, self.counts[sname]))
        return toks

    def mm_group(self, items, out_buf, reads=(), waits=(), psum_slice=None):
        n = len(items)
        wl = list(waits)
        for b in reads:
            if b.wr is not None:
                wl.append(b.wr)
        if out_buf.wr is None and out_buf.rds == [] and self.fence:
            wl.extend(self.fence)
        if out_buf.wr is not None:
            wl.append(out_buf.wr)
        wl.extend(out_buf.rds)
        tok = None
        for i, (lhs, rhs) in enumerate(items):
            dst = out_buf.ten[psum_slice] if psum_slice is not None else out_buf.ten[:, :]
            tok = self._record(
                "tensor",
                lambda e, l=lhs, r=rhs, d=dst, s=(i == 0), p=(i == n - 1): e.matmul(
                    d, l, r, start=s, stop=p),
                wl if i == 0 else (), "pe" if i == n - 1 else None, 1)
        for b in reads:
            b.rds.append(tok)
        out_buf.wr = tok
        out_buf.rds = []
        return tok

    def barrier_token(self, name):
        self.sem(name)
        return (name, self.counts[name])

    def build(self, block):
        for engine in ENGINES:
            steps = self.steps[engine]
            if not steps:
                continue
            dec = getattr(block, engine)

            def _make(_steps):
                def _body(eng):
                    for wl, fn, tok, inc in _steps:
                        for (sname, val) in wl:
                            eng.wait_ge(self.sem_handles[sname], val)
                        ins = fn(eng)
                        if tok is not None:
                            ins.then_inc(self.sem_handles[tok[0]], inc)
                return _body
            dec(_make(steps))

    def close(self):
        self._ctx.close()


class Mem:

    def __init__(self, nc, ctx):
        self.nc = nc
        self.ctx = ctx

    def sb(self, name, shape, dt=F32):
        return Buf(self.ctx.enter_context(self.nc.sbuf_tensor(name, shape, dt)), name)

    def ps(self, name, shape, dt=F32):
        return Buf(self.ctx.enter_context(self.nc.psum_tensor(name, shape, dt)), name)

    @contextlib.contextmanager
    def scope(self, S=None):
        with contextlib.ExitStack() as es:
            yield Mem(self.nc, es)
        if S is not None:
            S.set_fence()


def ln_emit(S, mem, x_ap, out_ap, stat, sq, epscol, reads, writes, extra_waits=(), rows=128):
    st = stat.ten[0:rows, :]
    S.step("scalar", lambda e: e.activation(sq.ten[0:rows, :], x_ap, AF.Square),
           reads=reads, writes=[sq], waits=extra_waits)
    S.step("vector", lambda e: e.reduce_sum(st[:, 0:1], x_ap, axis=AXX),
           reads=reads, writes=[stat], waits=extra_waits)
    S.step("vector", lambda e: e.reduce_sum(st[:, 1:2], sq.ten[0:rows, :], axis=AXX),
           reads=[sq], writes=[stat])
    S.step("vector", lambda e: e.tensor_scalar(st[:, 2:4], st[:, 0:2], 1.0 / D, None, ALU.mult),
           reads=[stat], writes=[stat])
    S.step("vector", lambda e: e.tensor_mul(st[:, 4:5], st[:, 2:3], st[:, 2:3]),
           reads=[stat], writes=[stat])
    S.step("vector", lambda e: e.tensor_sub(st[:, 5:6], st[:, 3:4], st[:, 4:5]),
           reads=[stat], writes=[stat])
    S.step("scalar", lambda e: e.activation(st[:, 6:7], st[:, 5:6], AF.Sqrt, bias=epscol.ten[0:rows, 0:1]),
           reads=[stat, epscol], writes=[stat])
    S.step("vector", lambda e: e.reciprocal(st[:, 7:8], st[:, 6:7]),
           reads=[stat], writes=[stat])
    return S.step("vector", lambda e: e.tensor_scalar(
        out_ap, x_ap, st[:, 2:3], st[:, 7:8], ALU.subtract, ALU.mult),
        reads=reads + [stat], writes=writes)


def transpose_tiles(S, mem, src_buf, src_col0, dst_buf, dst_col0, nrow_tiles,
                    ncol_tiles, src_stride, dst_stride, ident, pst, dt_bytes=2):
    for r in range(nrow_tiles):
        for c in range(ncol_tiles):
            p = pst[(r * ncol_tiles + c) % len(pst)]
            s_ap = src_buf[:, src_col0 + r * src_stride + c * 128:
                           src_col0 + r * src_stride + c * 128 + 128]
            d_ap = dst_buf[:, dst_col0 + c * dst_stride + r * 128:
                           dst_col0 + c * dst_stride + r * 128 + 128]
            S.step("tensor", lambda e, s=s_ap, p=p: e.transpose(p.ten[:, :], s, ident.ten[:, :]),
                   reads=[src_buf, ident], writes=[p])
            S.step("scalar", lambda e, d=d_ap, p=p: e.copy(d, p.ten[:, :]),
                   reads=[p], writes=[dst_buf])



def emit_p0(S, mem, ios):
    cst = ios["const"]
    with mem.scope(S) as m1:
        xlnT = m1.sb("p0_xlnT", [128, NCB * TOK], BF16)
        x_t = [m1.sb(f"p0_x{i}", [128, D]) for i in range(2)]
        sq = m1.sb("p0_sq", [128, D])
        xln_f = m1.sb("p0_xlnf", [128, D])
        xln_bf = m1.sb("p0_xlnbf", [128, D], BF16)
        stat = m1.sb("p0_stat", [128, 8])
        rope_c = m1.sb("p0_ropec", [128, TOK], BF16)
        rope_s = m1.sb("p0_ropes", [128, TOK], BF16)
        w_s = m1.sb("p0_w", [128, NCB * D], BF16)
        wT = m1.sb("p0_wT", [128, NCB * D], BF16)
        rot_t = [m1.sb(f"p0_rot{i}", [128, TOK], BF16) for i in range(2)]
        v_t = [m1.sb(f"p0_v{i}", [128, D], BF16) for i in range(2)]
        qtmp = m1.sb("p0_qtmp", [128, 512], BF16)
        qsw = m1.sb("p0_qsw", [128, 512], BF16)
        qm1 = m1.sb("p0_qm1", [128, 512], BF16)
        ident_b = m1.sb("p0_identb", [128, 128], BF16)
        psA = m1.ps("p0_psA", [128, 512])
        psB = m1.ps("p0_psB", [128, 512])
        psT = [m1.ps("p0_psT1", [128, 128], BF16), m1.ps("p0_psT2", [128, 128], BF16)]

        S.dma("gpsimd", ident_b[:, :], cst["ident"], writes=[ident_b])
        epscol = m1.sb("p0_eps", [128, 1])
        S.dma("gpsimd", epscol[:, :], cst["epscol"], writes=[epscol])
        S.dma("gpsimd", rope_c[:, :], cst["ropeCos"], writes=[rope_c])
        S.dma("gpsimd", rope_s[:, :], cst["ropeSin"], writes=[rope_s])

        for ti in range(NT):
            xt = x_t[ti % 2]
            S.dma("sync", xt[:, :], ios["x"][ti * 128:(ti + 1) * 128, :], writes=[xt])
            ln_emit(S, m1, xt[:, :], xln_f[:, :], stat, sq, epscol, reads=[xt], writes=[xln_f])
            S.step("vector", lambda e: e.tensor_copy(xln_bf.ten[:, :], xln_f.ten[:, :]),
                   reads=[xln_f], writes=[xln_bf])
            for cb in range(NCB):
                p = psT[cb % 2]
                S.step("tensor", lambda e, cb=cb, p=p: e.transpose(
                    p.ten[:, :], xln_bf.ten[:, cb * 128:(cb + 1) * 128], ident_b.ten[:, :]),
                    reads=[xln_bf, ident_b], writes=[p])
                dst = xlnT[:, cb * TOK + ti * 128: cb * TOK + (ti + 1) * 128]
                S.step("scalar", lambda e, dst=dst, p=p: e.copy(dst, p.ten[:, :]),
                       reads=[p], writes=[xlnT])

        for wi_, wname in enumerate(("wq", "wk", "wv")):
            S.dma("gpsimd", w_s.ten.ap().rearrange("p (n c) -> p n c", n=NCB),
                  ios[wname].rearrange("(n p) c -> p n c", p=128), writes=[w_s])
            transpose_tiles(S, m1, w_s, 0, wT, 0, NCB, NCB, D, D, ident_b, psT)
            if wname != "wv":
                rot_out = ios["q_a2a"] if wname == "wq" else ios["k_a2a"]
                for dqb in range(NCB):
                    rt = rot_t[dqb % 2]
                    for tc in range(4):
                        psum = psA if tc % 2 == 0 else psB
                        items = [(wT[:, cb * D + dqb * 128: cb * D + (dqb + 1) * 128],
                                  xlnT[:, cb * TOK + tc * 512: cb * TOK + (tc + 1) * 512])
                                 for cb in range(NCB)]
                        S.mm_group(items, psum, reads=[wT, xlnT])
                        sl = slice(tc * 512, (tc + 1) * 512)
                        S.step("scalar", lambda e, psum=psum: e.copy(qtmp.ten[:, :], psum.ten[:, :]),
                               reads=[psum], writes=[qtmp])
                        S.step("vector", lambda e: e.stream_shuffle(
                            qsw.ten[:, :], qtmp.ten[:, :], [i ^ 1 for i in range(32)]),
                            reads=[qtmp], writes=[qsw])
                        S.step("vector", lambda e, sl=sl: e.tensor_mul(
                            qm1.ten[:, :], qtmp.ten[:, :], rope_c.ten[:, sl]),
                            reads=[qtmp, rope_c], writes=[qm1])
                        S.step("vector", lambda e, sl=sl: e.tensor_mul(
                            qsw.ten[:, :], qsw.ten[:, :], rope_s.ten[:, sl]),
                            reads=[qsw, rope_s], writes=[qsw])
                        S.step("vector", lambda e, rt=rt, sl=sl: e.tensor_add(
                            rt.ten[:, sl], qm1.ten[:, :], qsw.ten[:, :]),
                            reads=[qm1, qsw], writes=[rt])
                    S.dma("sync", rot_out[dqb, :, :], rt[:, :], reads=[rt])
            else:
                for ti in range(NT):
                    vt = v_t[ti % 2]
                    for hc in range(2):
                        psum = psA if hc == 0 else psB
                        items = [(xlnT[:, cb * TOK + ti * 128: cb * TOK + (ti + 1) * 128],
                                  wT[:, cb * D + hc * 512: cb * D + (hc + 1) * 512])
                                 for cb in range(NCB)]
                        S.mm_group(items, psum, reads=[xlnT, wT])
                        dst = vt[:, hc * 512:(hc + 1) * 512]
                        S.step("scalar", lambda e, dst=dst, psum=psum: e.copy(dst, psum.ten[:, :]),
                               reads=[psum], writes=[vt])
                    S.dma("sync",
                          ios["v_a2a"][:, ti * 128:(ti + 1) * 128, :].rearrange("j p c -> p j c"),
                          vt.ten.ap().rearrange("p (j c) -> p j c", j=NCORES), reads=[vt])
        t_qkv_done = S.dma_barrier()

    with mem.scope(S) as m2:
        Cs = m2.sb("p0_C", [128, NCB * D])
        Sn = m2.sb("p0_Sn", [128, NCB * D])
        Crs = m2.sb("p0_Cr", [128, NCB * 128])
        Srs = m2.sb("p0_Sr", [128, NCB * 128])
        Snrs = m2.sb("p0_Snr", [128, NCB * 128])
        wr_s = m2.sb("p0_wr", [128, NCB * D])
        wi_s = m2.sb("p0_wi", [128, NCB * D])
        PTs = m2.sb("p0_PT", [128, NCB * 128])
        QTs = m2.sb("p0_QT", [128, NCB * 128])
        m_out = m2.sb("p0_m", [128, D])
        psA = m2.ps("p0_psA2", [128, 512])
        psB = m2.ps("p0_psB2", [128, 512])
        S.dma("sync", Cs.ten.ap().rearrange("p (n o) -> p n o", n=NCB),
              cst["dftC"].rearrange("(n p) o -> p n o", p=128),
              writes=[Cs], waits=t_qkv_done)
        S.dma("sync", Sn.ten.ap().rearrange("p (n o) -> p n o", n=NCB),
              cst["dftSneg"].rearrange("(n p) o -> p n o", p=128),
              writes=[Sn], waits=t_qkv_done)
        S.dma("sync", Crs.ten.ap().rearrange("p (n o) -> p n o", n=NCB),
              cst["dftC_r"].rearrange("(n p) o -> p n o", p=128),
              writes=[Crs], waits=t_qkv_done)
        S.dma("sync", Srs.ten.ap().rearrange("p (n o) -> p n o", n=NCB),
              cst["dftS_r"].rearrange("(n p) o -> p n o", p=128),
              writes=[Srs], waits=t_qkv_done)
        S.dma("sync", Snrs.ten.ap().rearrange("p (n o) -> p n o", n=NCB),
              cst["dftSn_r"].rearrange("(n p) o -> p n o", p=128),
              writes=[Snrs], waits=t_qkv_done)
        S.dma("gpsimd", wr_s.ten.ap().rearrange("p (n o) -> p n o", n=NCB),
              ios["w_freq_real"].rearrange("(n p) o -> p n o", p=128),
              writes=[wr_s], waits=t_qkv_done)
        S.dma("gpsimd", wi_s.ten.ap().rearrange("p (n o) -> p n o", n=NCB),
              ios["w_freq_imag"].rearrange("(n p) o -> p n o", p=128),
              writes=[wi_s], waits=t_qkv_done)
        for dstT, wa, wb, sb_b in ((PTs, wr_s, wi_s, Srs), (QTs, wi_s, wr_s, Snrs)):
            for kb in range(NCB):
                psum = psA if kb % 2 == 0 else psB
                items = []
                for db in range(NCB):
                    items.append((wa[:, db * D + kb * 128: db * D + (kb + 1) * 128],
                                  Crs[:, db * 128:(db + 1) * 128]))
                    items.append((wb[:, db * D + kb * 128: db * D + (kb + 1) * 128],
                                  sb_b[:, db * 128:(db + 1) * 128]))
                S.mm_group(items, psum, reads=[wa, wb, Crs, sb_b],
                           psum_slice=(slice(None), slice(0, 128)))
                S.step("scalar", lambda e, kb=kb, psum=psum, dstT=dstT: e.copy(
                    dstT[:, kb * 128:(kb + 1) * 128], psum.ten[:, 0:128]),
                    reads=[psum], writes=[dstT])
        for oc in range(2):
            psum = psA if oc == 0 else psB
            items = []
            for kb in range(NCB):
                items.append((PTs[:, kb * 128:(kb + 1) * 128],
                              Cs[:, kb * D + oc * 512: kb * D + (oc + 1) * 512]))
                items.append((QTs[:, kb * 128:(kb + 1) * 128],
                              Sn[:, kb * D + oc * 512: kb * D + (oc + 1) * 512]))
            S.mm_group(items, psum, reads=[PTs, QTs, Cs, Sn])
            S.step("scalar", lambda e, oc=oc, psum=psum: e.mul(
                m_out[:, oc * 512:(oc + 1) * 512], psum.ten[:, :], 1.0 / D),
                reads=[psum], writes=[m_out])
        S.dma("sync", ios["m_slice"], m_out[:, :], reads=[m_out])




def build_p0():
    nc = bass.Bass("TRN2", target_bir_lowering=False, debug=False)
    ios = {"const": {}}
    for name, shape, dt in CONST_SPECS:
        ios["const"][name] = nc.dram_tensor("c_" + name, shape, dt, kind="ExternalInput").ap()
    ios["x"] = nc.dram_tensor("x", [TOK, D], F32, kind="ExternalInput").ap()
    for w in ("wq", "wk", "wv"):
        ios[w] = nc.dram_tensor(w, [D, D], F32, kind="ExternalInput").ap()
    ios["w_freq_real"] = nc.dram_tensor("w_freq_real", [D, D], F32, kind="ExternalInput").ap()
    ios["w_freq_imag"] = nc.dram_tensor("w_freq_imag", [D, D], F32, kind="ExternalInput").ap()
    ios["q_a2a"] = nc.dram_tensor("q_a2a", [NCORES, 128, TOK], BF16, kind="ExternalOutput").ap()
    ios["k_a2a"] = nc.dram_tensor("k_a2a", [NCORES, 128, TOK], BF16, kind="ExternalOutput").ap()
    ios["v_a2a"] = nc.dram_tensor("v_a2a", [NCORES, TOK, 128], BF16, kind="ExternalOutput").ap()
    ios["m_slice"] = nc.dram_tensor("m_slice", [128, D], F32, kind="ExternalOutput").ap()
    with contextlib.ExitStack() as ctx:
        mem = Mem(nc, ctx)
        S = Sched(nc)
        emit_p0(S, mem, ios)
        end_tok = S.dma_barrier()
        S.step("gpsimd", lambda e: e.nop(), waits=end_tok, sem=None)
        block = ctx.enter_context(nc.Block())
        S.build(block)
    S.close()
    return nc





def emit_p1(S, mem, ios, units=4, qcs=16):
    cst = ios["const"]
    NKT = T_FULL // 128
    nbatch = max(1, units // 2)
    with mem.scope(S) as m1:
        qt_h = [m1.sb(f"p1_qt{h}", [64, T_FULL], BF16) for h in range(2)]
        kt_h = [m1.sb(f"p1_kt{h}", [64, T_FULL], BF16) for h in range(2)]
        vs_h = [m1.sb(f"p1_vs{h}", [128, NKT * 65], BF16) for h in range(2)]
        pt = [m1.sb(f"p1_pt{i}", [128, 512], BF16) for i in range(4)]
        masks = [m1.sb(f"p1_mask{j}", [128, 512], BF16) for j in range(4)]
        ones_s = m1.sb("p1_ones", [128, 64])
        den_s = m1.sb("p1_den", [128, 512])
        rec_sb = [m1.sb(f"p1_rec{h}", [64, 512]) for h in range(2)]
        outT = [m1.sb(f"p1_outT{h}", [64, 512], BF16) for h in range(2)]
        psS = [m1.ps(f"p1_psS{i}", [128, 512]) for i in range(4)]
        psO = [m1.ps(f"p1_psO{h}", [128, 512]) for h in range(2)]
        psR = [m1.ps(f"p1_psR{h}", [128, 512]) for h in range(2)]

        for j in range(4):
            S.dma("gpsimd", masks[j][:, :], cst[f"cmask{j}"], writes=[masks[j]])
        S.step("vector", lambda e: e.memset(ones_s.ten[:, :], 1.0), writes=[ones_s])

        for b in range(nbatch):
            for hl in range(2):
                qt = qt_h[hl]
                kt = kt_h[hl]
                S.dma("sync", qt.ten.ap().rearrange("p (s t) -> p s t", s=4),
                      ios["qg"][4 * b:4 * b + 4, 64 * hl:64 * hl + 64, :]
                          .rearrange("s p t -> p s t"),
                      writes=[qt])
                S.dma("sync", kt.ten.ap().rearrange("p (s t) -> p s t", s=4),
                      ios["kg"][4 * b:4 * b + 4, 64 * hl:64 * hl + 64, :]
                          .rearrange("s p t -> p s t"),
                      writes=[kt])
                vs = vs_h[hl]
                S.dma("sync",
                      vs.ten.ap().rearrange("p (n f) -> p n f", f=65)[:, :, 0:64]
                          .rearrange("p (s n) c -> p s n c", s=4),
                      ios["vg"][4 * b:4 * b + 4, :, 64 * hl:64 * hl + 64]
                          .rearrange("s (n p) c -> p s n c", p=128),
                      writes=[vs])
                S.step("vector", lambda e, vs=vs: e.memset(
                    vs.ten.ap().rearrange("p (n f) -> p n f", f=65)[:, :, 64:65], 1.0),
                    reads=[ones_s], writes=[vs])

            for qc in range(qcs):
                q0 = qc * 512
                nk = (q0 + 512) // 128
                first_pv = [True, True]

                def pvemit(kti):
                    for hl in range(2):
                        vs = vs_h[hl]
                        po = psO[hl]
                        p = pt[2 * hl + kti % 2]
                        wl = []
                        if first_pv[hl]:
                            if po.wr is not None:
                                wl.append(po.wr)
                            wl.extend(po.rds)
                            if vs.wr is not None:
                                wl.append(vs.wr)
                        if p.wr is not None:
                            wl.append(p.wr)
                        tokpv = S._record(
                            "tensor",
                            lambda e, kti=kti, p=p, po=po, vs=vs, s=first_pv[hl],
                                   last=(kti == nk - 1): e.matmul(
                                po.ten[0:65, :], vs.ten[:, kti * 65:kti * 65 + 65],
                                p.ten[:, :], start=s, stop=last, skip_group_check=True),
                            wl, "pe", 1)
                        if kti == nk - 1:
                            po.wr = tokpv
                            po.rds = []
                        p.rds.append(tokpv)
                        vs.rds.append(tokpv)
                        first_pv[hl] = False

                for kti in range(nk):
                    for hl in range(2):
                        qt = qt_h[hl]
                        kt = kt_h[hl]
                        ss = psS[2 * hl + kti % 2]
                        S.mm_group([(kt[:, kti * 128:(kti + 1) * 128],
                                     qt[:, q0:q0 + 512])],
                                   ss, reads=[kt, qt])
                    for hl in range(2):
                        ss = psS[2 * hl + kti % 2]
                        p = pt[2 * hl + kti % 2]
                        S.step("scalar", lambda e, ss=ss, p=p: e.activation(
                            p.ten[:, :], ss.ten[:, :], AF.Exp, scale=0.125),
                            reads=[ss], writes=[p])
                        j = kti - q0 // 128
                        if j >= 0:
                            S.step("vector", lambda e, p=p, j=j: e.tensor_mul(
                                p.ten[:, :], p.ten[:, :], masks[j].ten[:, :]),
                                reads=[p, masks[j]], writes=[p])
                    if kti > 0:
                        pvemit(kti - 1)
                pvemit(nk - 1)
                for hl in range(2):
                    po = psO[hl]
                    rec = rec_sb[hl]
                    ot = outT[hl]
                    dr = 64 * hl
                    S.step("scalar", lambda e, po=po, dr=dr: e.copy(
                        den_s.ten[dr + 0:dr + 1, :], po.ten[64:65, :]),
                        reads=[po], writes=[den_s])
                    S.step("vector", lambda e, dr=dr: e.reciprocal(
                        den_s.ten[dr + 0:dr + 1, :], den_s.ten[dr + 0:dr + 1, :]),
                        reads=[den_s], writes=[den_s])
                    S.mm_group([(ones_s[dr:dr + 1, 0:64], den_s[dr:dr + 1, :])],
                               psR[hl], reads=[ones_s, den_s],
                               psum_slice=(slice(0, 64), slice(None)))
                    S.step("scalar", lambda e, rec=rec, hl=hl: e.copy(
                        rec.ten[:, :], psR[hl].ten[0:64, :]),
                        reads=[psR[hl]], writes=[rec])
                    S.step("vector", lambda e, ot=ot, po=po, rec=rec: e.tensor_mul(
                        ot.ten[:, :], po.ten[0:64, :], rec.ten[:, :]),
                        reads=[po, rec], writes=[ot])
                    blk = 4 * b + q0 // TOK
                    S.dma("sync",
                          ios["ao_a2a"][blk, 64 * hl:64 * hl + 64,
                                        (q0 % TOK):(q0 % TOK) + 512],
                          ot[:, :], reads=[ot])


def build_p1(units=4, qcs=16):
    nc = bass.Bass("TRN2", target_bir_lowering=False, debug=False)
    ios = {"const": {}}
    for name, shape, dt in CONST_SPECS:
        ios["const"][name] = nc.dram_tensor("c_" + name, shape, dt, kind="ExternalInput").ap()
    ios["qg"] = nc.dram_tensor("qg", [NCORES, 128, TOK], BF16, kind="ExternalInput").ap()
    ios["kg"] = nc.dram_tensor("kg", [NCORES, 128, TOK], BF16, kind="ExternalInput").ap()
    ios["vg"] = nc.dram_tensor("vg", [NCORES, TOK, 128], BF16, kind="ExternalInput").ap()
    ios["ao_a2a"] = nc.dram_tensor("ao_a2a", [NCORES, 128, TOK], BF16, kind="ExternalOutput").ap()
    with contextlib.ExitStack() as ctx:
        mem = Mem(nc, ctx)
        S = Sched(nc)
        emit_p1(S, mem, ios, units, qcs)
        end_tok = S.dma_barrier()
        S.step("gpsimd", lambda e: e.nop(), waits=end_tok, sem=None)
        block = ctx.enter_context(nc.Block())
        S.build(block)
    S.close()
    return nc



def emit_p2(S, mem, ios):
    cst = ios["const"]
    with mem.scope(S) as m1:
        attnT = m1.sb("p2_attnT", [128, NCB * TOK], BF16)
        w_s = m1.sb("p2_w", [128, NCB * D], BF16)
        woT = m1.sb("p2_woT", [128, NCB * D], BF16)
        m_s = m1.sb("p2_m", [128, NCB * D], BF16)
        bias_s = m1.sb("p2_bias", [1, D], BF16)
        ones_s = m1.sb("p2_ones", [1, 128], BF16)
        ident_b = m1.sb("p2_identb", [128, 128], BF16)
        epscol = m1.sb("p2_eps", [128, 1])
        x_t = [m1.sb(f"p2_x{i}", [128, D]) for i in range(2)]
        x1_t = [m1.sb(f"p2_x1{i}", [128, D]) for i in range(2)]
        x2_t = [m1.sb(f"p2_x2{i}", [128, D]) for i in range(2)]
        xln_f = m1.sb("p2_xlnf", [128, D])
        xln_bf = m1.sb("p2_xlnbf", [128, D], BF16)
        xlnT_t = [m1.sb(f"p2_xlnT{i}", [128, NCB * 128], BF16) for i in range(2)]
        sq = m1.sb("p2_sq", [128, D])
        stat = m1.sb("p2_stat", [128, 8])
        psA = m1.ps("p2_psA", [128, 512])
        psB = m1.ps("p2_psB", [128, 512])
        psT = [m1.ps("p2_psT1", [128, 128], BF16), m1.ps("p2_psT2", [128, 128], BF16)]

        S.dma("gpsimd", ident_b[:, :], cst["ident"], writes=[ident_b])
        S.dma("gpsimd", epscol[:, :], cst["epscol"], writes=[epscol])
        S.dma("gpsimd", ones_s[:, :], cst["ones1x128"], writes=[ones_s])
        S.dma("sync", attnT.ten.ap().rearrange("p (j t) -> p j t", j=NCORES),
              ios["ao"].rearrange("j p t -> p j t"), writes=[attnT])
        S.dma("gpsimd", w_s.ten.ap().rearrange("p (n c) -> p n c", n=NCB),
              ios["wo"].rearrange("(n p) c -> p n c", p=128), writes=[w_s])
        transpose_tiles(S, m1, w_s, 0, woT, 0, NCB, NCB, D, D, ident_b, psT)
        S.dma("gpsimd", m_s.ten.ap().rearrange("p (n c) -> p n c", n=NCB),
              ios["M"].rearrange("(n p) c -> p n c", p=128), writes=[m_s])
        S.dma("gpsimd", bias_s[:, :], ios["ffn_bias"].rearrange("(x c) -> x c", x=1), writes=[bias_s])

        for ti in range(NT):
            xt = x_t[ti % 2]
            x1 = x1_t[ti % 2]
            x2 = x2_t[ti % 2]
            xlnT = xlnT_t[ti % 2]
            S.dma("sync", xt[:, :], ios["x"][ti * 128:(ti + 1) * 128, :], writes=[xt])
            for oc in range(2):
                psum = psA if oc == 0 else psB
                items = [(attnT[:, cb * TOK + ti * 128: cb * TOK + (ti + 1) * 128],
                          woT[:, cb * D + oc * 512: cb * D + (oc + 1) * 512])
                         for cb in range(NCB)]
                S.mm_group(items, psum, reads=[attnT, woT])
                S.step("vector", lambda e, psum=psum, oc=oc, x1=x1, xt=xt: e.tensor_add(
                    x1.ten[:, oc * 512:(oc + 1) * 512], psum.ten[:, :],
                    xt.ten[:, oc * 512:(oc + 1) * 512]),
                    reads=[psum, xt], writes=[x1])
            ln_emit(S, m1, x1[:, :], xln_f[:, :], stat, sq, epscol,
                    reads=[x1], writes=[xln_f])
            S.step("vector", lambda e: e.tensor_copy(xln_bf.ten[:, :], xln_f.ten[:, :]),
                   reads=[xln_f], writes=[xln_bf])
            for cb in range(NCB):
                p = psT[cb % 2]
                S.step("tensor", lambda e, cb=cb, p=p: e.transpose(
                    p.ten[:, :], xln_bf.ten[:, cb * 128:(cb + 1) * 128], ident_b.ten[:, :]),
                    reads=[xln_bf, ident_b], writes=[p])
                S.step("scalar", lambda e, cb=cb, p=p, xlnT=xlnT: e.copy(
                    xlnT[:, cb * 128:(cb + 1) * 128], p.ten[:, :]),
                    reads=[p], writes=[xlnT])
            for oc in range(2):
                psum = psA if oc == 0 else psB
                items = [(xlnT[:, cb * 128:(cb + 1) * 128],
                          m_s[:, cb * D + oc * 512: cb * D + (oc + 1) * 512])
                         for cb in range(NCB)]
                items.append((ones_s[0:1, :], bias_s[0:1, oc * 512:(oc + 1) * 512]))
                S.mm_group(items, psum, reads=[xlnT, m_s, ones_s, bias_s])
                S.step("vector", lambda e, psum=psum, oc=oc, x1=x1, x2=x2: e.tensor_add(
                    x2.ten[:, oc * 512:(oc + 1) * 512], psum.ten[:, :],
                    x1.ten[:, oc * 512:(oc + 1) * 512]),
                    reads=[psum, x1], writes=[x2])
            tx2 = S.dma("sync", ios["x2"][ti * 128:(ti + 1) * 128, :], x2[:, :], reads=[x2])
            ios.setdefault("_x2_tok", {})[ti] = [tx2]
            if ti == NT - 1 and "x2tail" in ios:
                S.dma("sync", ios["x2tail"], x2[125:128, :], reads=[x2])

        if "ao_h" in ios:
            aoh = m1.sb("p2_aoh", [128, NCB * 3], BF16)
            xh3 = m1.sb("p2_xh3", [3, D])
            x1h = m1.sb("p2_x1h", [3, D])
            x2h = m1.sb("p2_x2h", [3, D])
            xlnh = m1.sb("p2_xlnh", [3, D])
            xlnh_bf = m1.sb("p2_xlnhbf", [3, D], BF16)
            xlnT_h = m1.sb("p2_xlnTh", [128, NCB * 3], BF16)
            sqh = m1.sb("p2_sqh", [3, D])
            stath = m1.sb("p2_stath", [128, 8])
            hm = m1.sb("p2_hm", [3, 1])
            S.dma("sync", aoh.ten.ap().rearrange("p (j t) -> p j t", j=NCORES),
                  ios["ao_h"].rearrange("j p t -> p j t"), writes=[aoh])
            S.dma("sync", xh3[:, :], ios["x_h"], writes=[xh3])
            S.dma("sync", hm[:, :], ios["hmask"], writes=[hm])
            for oc in range(2):
                psum = psA if oc == 0 else psB
                items = [(aoh[:, cb * 3:(cb + 1) * 3],
                          woT[:, cb * D + oc * 512: cb * D + (oc + 1) * 512])
                         for cb in range(NCB)]
                S.mm_group(items, psum, reads=[aoh, woT],
                           psum_slice=(slice(0, 3), slice(None)))
                S.step("vector", lambda e, psum=psum, oc=oc: e.tensor_add(
                    x1h.ten[0:3, oc * 512:(oc + 1) * 512], psum.ten[0:3, :],
                    xh3.ten[0:3, oc * 512:(oc + 1) * 512]),
                    reads=[psum, xh3], writes=[x1h])
            ln_emit(S, m1, x1h[0:3, :], xlnh[0:3, :], stath, sqh, epscol,
                    reads=[x1h], writes=[xlnh], rows=3)
            S.step("vector", lambda e: e.tensor_copy(xlnh_bf.ten[0:3, :], xlnh.ten[0:3, :]),
                   reads=[xlnh], writes=[xlnh_bf])
            for cb in range(NCB):
                p = psT[cb % 2]
                S.step("tensor", lambda e, cb=cb, p=p: e.transpose(
                    p.ten[0:128, 0:3], xlnh_bf.ten[0:3, cb * 128:(cb + 1) * 128],
                    ident_b.ten[0:3, 0:3]),
                    reads=[xlnh_bf, ident_b], writes=[p])
                S.step("scalar", lambda e, cb=cb, p=p: e.copy(
                    xlnT_h[:, cb * 3:(cb + 1) * 3], p.ten[0:128, 0:3]),
                    reads=[p], writes=[xlnT_h])
            for oc in range(2):
                psum = psA if oc == 0 else psB
                items = [(xlnT_h[:, cb * 3:(cb + 1) * 3],
                          m_s[:, cb * D + oc * 512: cb * D + (oc + 1) * 512])
                         for cb in range(NCB)]
                items.append((ones_s[0:1, 0:3], bias_s[0:1, oc * 512:(oc + 1) * 512]))
                S.mm_group(items, psum, reads=[xlnT_h, m_s, ones_s, bias_s],
                           psum_slice=(slice(0, 3), slice(None)))
                S.step("vector", lambda e, psum=psum, oc=oc: e.tensor_add(
                    x2h.ten[0:3, oc * 512:(oc + 1) * 512], psum.ten[0:3, :],
                    x1h.ten[0:3, oc * 512:(oc + 1) * 512]),
                    reads=[psum, x1h], writes=[x2h])
            S.step("vector", lambda e: e.tensor_scalar(
                x2h.ten[0:3, :], x2h.ten[0:3, :], hm.ten[0:3, 0:1], None, ALU.mult),
                reads=[hm], writes=[x2h])
            th = S.dma("sync", ios["halo_scr"], x2h[0:3, :], reads=[x2h])
            ios["_halo_tok"] = [th]



def emit_p3(S, mem, ios):
    cst = ios["const"]
    NDB = D_IN // 128
    EXT = TOK + 3
    with mem.scope(S) as m0:
        ident_b = m0.sb("p3_identb", [128, 128], BF16)
        epscol = m0.sb("p3_eps", [128, 1])
        onecol = m0.sb("p3_one", [128, 1])
        u_preT = m0.sb("p3_upre", [128, NDB * EXT], BF16)
        S.dma("gpsimd", ident_b[:, :], cst["ident"], writes=[ident_b])
        S.dma("gpsimd", epscol[:, :], cst["epscol"], writes=[epscol])
        S.dma("gpsimd", onecol[:, :], cst["onecol"], writes=[onecol])

        with mem.scope(S) as m1:
            xh = m1.sb("p3_xh", [3, D])
            xah = m1.sb("p3_xah", [3, D])
            xah_bf = m1.sb("p3_xahbf", [3, D], BF16)
            xaT_halo = m1.sb("p3_xaTh", [128, NCB * 3], BF16)
            stat = m1.sb("p3_stat", [128, 8])
            sqh = m1.sb("p3_sqh", [3, D])
            psH = m1.ps("p3_psH", [128, 512])
            psT = [m1.ps("p3_psT1", [128, 128], BF16), m1.ps("p3_psT2", [128, 128], BF16)]
            if "_halo_tok" not in ios:
                halo_s = m1.sb("p3_halo", [8, 3 * D])
                oneh = m1.sb("p3_oneh", [8, 1])
                xhf = m1.sb("p3_xhf", [1, 3 * D])
                S.dma("sync", halo_s.ten.ap().rearrange("p (r c) -> p r c", r=3),
                      ios["halo_g"], writes=[halo_s])
                S.dma("sync", oneh[:, :], ios["onehot"], writes=[oneh])
                for k in range(6):
                    S.mm_group([(oneh[0:8, 0:1], halo_s[:, k * 512:(k + 1) * 512])], psH,
                               reads=[oneh, halo_s], psum_slice=(slice(0, 1), slice(None)))
                    S.step("scalar", lambda e, k=k: e.copy(xhf.ten[0:1, k * 512:(k + 1) * 512],
                                                           psH.ten[0:1, :]),
                           reads=[psH], writes=[xhf])
                tw = [S.dma("sync", ios["halo_scr"].rearrange("(x r) c -> x r c", x=1),
                            xhf[0:1, :].rearrange("p (r c) -> p r c", r=3), reads=[xhf])]
            else:
                tw = ios["_halo_tok"]
            S.dma("sync", xh[:, :], ios["halo_scr"], writes=[xh], waits=tw)
            ln_emit(S, m1, xh[0:3, :], xah[0:3, :], stat, sqh, epscol,
                    reads=[xh], writes=[xah], rows=3)
            S.step("vector", lambda e: e.tensor_copy(xah_bf.ten[0:3, :], xah.ten[0:3, :]),
                   reads=[xah], writes=[xah_bf])
            for cb in range(NCB):
                p = psT[cb % 2]
                S.step("tensor", lambda e, cb=cb, p=p: e.transpose(
                    p.ten[0:128, 0:3], xah_bf.ten[0:3, cb * 128:(cb + 1) * 128],
                    ident_b.ten[0:3, 0:3]),
                    reads=[xah_bf, ident_b], writes=[p])
                S.step("scalar", lambda e, cb=cb, p=p: e.copy(
                    xaT_halo[:, cb * 3:(cb + 1) * 3], p.ten[0:128, 0:3]),
                    reads=[p], writes=[xaT_halo])

            xaT = m1.sb("p3_xaT", [128, NCB * TOK], BF16)
            x2t = [m1.sb(f"p3_x2{i}", [128, D]) for i in range(2)]
            xln_f = m1.sb("p3_xlnf", [128, D])
            xln_bf = m1.sb("p3_xlnbf", [128, D], BF16)
            sq = m1.sb("p3_sq", [128, D])
            for ti in range(NT):
                xt = x2t[ti % 2]
                S.dma("sync", xt[:, :], ios["x2"][ti * 128:(ti + 1) * 128, :], writes=[xt],
                      waits=ios.get("_x2_tok", {}).get(ti, []))
                ln_emit(S, m1, xt[:, :], xln_f[:, :], stat, sq, epscol,
                        reads=[xt], writes=[xln_f])
                S.step("vector", lambda e: e.tensor_copy(xln_bf.ten[:, :], xln_f.ten[:, :]),
                       reads=[xln_f], writes=[xln_bf])
                for cb in range(NCB):
                    p = psT[cb % 2]
                    S.step("tensor", lambda e, cb=cb, p=p: e.transpose(
                        p.ten[:, :], xln_bf.ten[:, cb * 128:(cb + 1) * 128], ident_b.ten[:, :]),
                        reads=[xln_bf, ident_b], writes=[p])
                    S.step("scalar", lambda e, cb=cb, ti=ti, p=p: e.copy(
                        xaT[:, cb * TOK + ti * 128: cb * TOK + (ti + 1) * 128], p.ten[:, :]),
                        reads=[p], writes=[xaT])

            win_s = m1.sb("p3_win", [128, NCB * D_IN], BF16)
            res_t = [m1.sb(f"p3_res{i}", [128, TOK], BF16) for i in range(2)]
            S.dma("gpsimd", win_s.ten.ap().rearrange("p (n c) -> p n c", n=NCB),
                  ios["W_in"][:, 0:D_IN].rearrange("(n p) c -> p n c", p=128),
                  writes=[win_s])
            for db in range(NDB):
                items = [(win_s[:, cb * D_IN + db * 128:cb * D_IN + (db + 1) * 128],
                          xaT_halo[:, cb * 3:(cb + 1) * 3]) for cb in range(NCB)]
                S.mm_group(items, psH, reads=[win_s, xaT_halo],
                           psum_slice=(slice(None), slice(0, 3)))
                S.step("scalar", lambda e, db=db: e.copy(
                    u_preT[:, db * EXT: db * EXT + 3], psH.ten[:, 0:3]),
                    reads=[psH], writes=[u_preT])
                for tc in range(4):
                    items = [(win_s[:, cb * D_IN + db * 128:cb * D_IN + (db + 1) * 128],
                              xaT[:, cb * TOK + tc * 512: cb * TOK + (tc + 1) * 512])
                             for cb in range(NCB)]
                    S.mm_group(items, psH, reads=[win_s, xaT])
                    S.step("scalar", lambda e, db=db, tc=tc: e.copy(
                        u_preT[:, db * EXT + 3 + tc * 512: db * EXT + 3 + (tc + 1) * 512],
                        psH.ten[:, :]),
                        reads=[psH], writes=[u_preT])
            S.dma("gpsimd", win_s.ten.ap().rearrange("p (n c) -> p n c", n=NCB),
                  ios["W_in"][:, D_IN:2 * D_IN].rearrange("(n p) c -> p n c", p=128),
                  writes=[win_s])
            for db in range(NDB):
                rt = res_t[db % 2]
                for tc in range(4):
                    items = [(win_s[:, cb * D_IN + db * 128:cb * D_IN + (db + 1) * 128],
                              xaT[:, cb * TOK + tc * 512: cb * TOK + (tc + 1) * 512])
                             for cb in range(NCB)]
                    S.mm_group(items, psH, reads=[win_s, xaT])
                    S.step("scalar", lambda e, tc=tc, rt=rt: e.activation(
                        rt.ten[:, tc * 512:(tc + 1) * 512], psH.ten[:, :], AF.Silu),
                        reads=[psH], writes=[rt])
                S.dma("sync", ios["sres"][db * 128:(db + 1) * 128, :], rt[:, :], reads=[rt])

        with mem.scope(S) as m2:
            cw_t = [m2.sb(f"p3_cw{i}", [128, 4 * NDB * 128], BF16) for i in range(2)]
            u_t = [m2.sb(f"p3_u{i}", [128, TOK], BF16) for i in range(2)]
            wx_s = m2.sb("p3_wx", [128, NDB * 96], BF16)
            cb_col = m2.sb("p3_cb", [128, NDB])
            psC = [m2.ps(f"p3_psC{i}", [128, 512]) for i in range(2)]
            psX = [m2.ps(f"p3_psX{i}", [128, 512]) for i in range(4)]
            S.dma("gpsimd", wx_s.ten.ap().rearrange("p (n c) -> p n c", n=NDB),
                  ios["W_x"].rearrange("(n p) c -> p n c", p=128), writes=[wx_s])
            ident_f = m2.sb("p3_identf", [128, 128])
            colr = m2.sb("p3_colr", [16, 128])
            psF = m2.ps("p3_psF", [128, 128])
            S.dma("gpsimd", ident_f[:, :], cst["ident"], writes=[ident_f])
            S.dma("gpsimd", colr[:, :], ios["conv_b"].rearrange("(n p) -> n p", n=NDB),
                  writes=[colr])
            S.step("tensor", lambda e: e.transpose(
                psF.ten[0:128, 0:16], colr.ten[0:16, :], ident_f.ten[0:16, 0:16]),
                reads=[colr, ident_f], writes=[psF])
            S.step("scalar", lambda e: e.copy(cb_col.ten[:, :], psF.ten[0:128, 0:16]),
                   reads=[psF], writes=[cb_col])
            xdbl_started = [False] * 4
            for ob in range(NDB):
                cw = cw_t[ob % 2]
                S.dma("gpsimd", cw.ten.ap().rearrange("p (j it o) -> p j it o", j=4, it=NDB),
                      ios["conv_w"][:, :, ob * 128:(ob + 1) * 128]
                          .rearrange("j (it p) o -> p j it o", p=128),
                      writes=[cw])
                ut = u_t[ob % 2]
                for tc in range(4):
                    psum = psC[tc % 2]
                    items = []
                    for j in range(4):
                        for it in range(NDB):
                            items.append((cw[:, (j * NDB + it) * 128:(j * NDB + it + 1) * 128],
                                          u_preT[:, it * EXT + tc * 512 + j:
                                                 it * EXT + tc * 512 + j + 512]))
                    S.mm_group(items, psum, reads=[cw, u_preT])
                    S.step("scalar", lambda e, tc=tc, ob=ob, psum=psum, ut=ut: e.activation(
                        ut.ten[:, tc * 512:(tc + 1) * 512], psum.ten[:, :], AF.Silu,
                        bias=cb_col.ten[:, ob:ob + 1]),
                        reads=[psum, cb_col], writes=[ut])
                    px = psX[tc]
                    wl = []
                    if not xdbl_started[tc]:
                        if px.wr is not None:
                            wl.append(px.wr)
                        wl.extend(px.rds)
                    wl.append(ut.wr)
                    wl.append(wx_s.wr)
                    tokx = S._record(
                        "tensor",
                        lambda e, ob=ob, tc=tc, px=px, ut=ut, s=(ob == 0): e.matmul(
                            px.ten[0:96, :], wx_s.ten[:, ob * 96:(ob + 1) * 96],
                            ut.ten[:, tc * 512:(tc + 1) * 512],
                            start=s, stop=(ob == NDB - 1), skip_group_check=True),
                        wl, "pe", 1)
                    xdbl_started[tc] = True
                    ut.rds.append(tokx)
                    wx_s.rds.append(tokx)
                    if ob == NDB - 1:
                        px.wr = tokx
                        px.rds = []
                S.dma("sync", ios["u_scr"][ob * 128:(ob + 1) * 128, :], ut[:, :], reads=[ut])

            dpre = m2.sb("p3_dpre", [64, TOK], BF16)
            bc_s = m2.sb("p3_bc", [32, TOK], BF16)
            for tc in range(4):
                S.step("scalar", lambda e, tc=tc: e.copy(
                    dpre.ten[0:64, tc * 512:(tc + 1) * 512], psX[tc].ten[0:64, :]),
                    reads=[psX[tc]], writes=[dpre])
                S.step("scalar", lambda e, tc=tc: e.copy(
                    bc_s.ten[0:32, tc * 512:(tc + 1) * 512], psX[tc].ten[64:96, :]),
                    reads=[psX[tc]], writes=[bc_s])
            ios["_bc_tok"] = [S.dma("sync", ios["bc"], bc_s[0:32, :], reads=[bc_s])]

            wdt_s = m2.sb("p3_wdt", [64, D_IN], BF16)
            bdt_col = m2.sb("p3_bdt", [128, NDB])
            dtmp = m2.sb("p3_dtmp", [128, 512])
            dbf_t = [m2.sb(f"p3_dbf{i}", [128, TOK], BF16) for i in range(2)]
            du_t = [m2.sb(f"p3_du{i}", [128, TOK], BF16) for i in range(2)]
            ur_t = [m2.sb(f"p3_ur{i}", [128, TOK], BF16) for i in range(2)]
            u_scr_done = S.dma_barrier()
            S.dma("gpsimd", wdt_s[:, :], ios["W_dt"], writes=[wdt_s])
            S.dma("gpsimd", colr[:, :], ios["b_dt"].rearrange("(n p) -> n p", n=NDB),
                  writes=[colr])
            S.step("tensor", lambda e: e.transpose(
                psF.ten[0:128, 0:16], colr.ten[0:16, :], ident_f.ten[0:16, 0:16]),
                reads=[colr, ident_f], writes=[psF])
            S.step("scalar", lambda e: e.copy(bdt_col.ten[:, :], psF.ten[0:128, 0:16]),
                   reads=[psF], writes=[bdt_col])
            for db in range(NDB):
                dbf = dbf_t[db % 2]
                du = du_t[db % 2]
                ur = ur_t[db % 2]
                S.dma("sync", ur[:, :], ios["u_scr"][db * 128:(db + 1) * 128, :],
                      writes=[ur], waits=u_scr_done)
                for tc in range(4):
                    psum = psC[tc % 2]
                    S.mm_group([(wdt_s[0:64, db * 128:(db + 1) * 128],
                                 dpre[0:64, tc * 512:(tc + 1) * 512])], psum,
                               reads=[wdt_s, dpre])
                    S.step("scalar", lambda e, db=db, psum=psum: e.activation(
                        dtmp.ten[:, :], psum.ten[:, :], AF.Exp,
                        bias=bdt_col.ten[:, db:db + 1]),
                        reads=[psum, bdt_col], writes=[dtmp])
                    S.step("scalar", lambda e, tc=tc, dbf=dbf: e.activation(
                        dbf.ten[:, tc * 512:(tc + 1) * 512], dtmp.ten[:, :], AF.Ln,
                        bias=onecol.ten[:, 0:1]),
                        reads=[dtmp, onecol], writes=[dbf])
                S.step("vector", lambda e, dbf=dbf, du=du, ur=ur: e.tensor_mul(
                    du.ten[:, :], dbf.ten[:, :], ur.ten[:, :]),
                    reads=[dbf, ur], writes=[du])
                t1 = S.dma("sync", ios["dlt"][db * 128:(db + 1) * 128, :], dbf[:, :], reads=[dbf])
                t2 = S.dma("sync", ios["du"][db * 128:(db + 1) * 128, :], du[:, :], reads=[du])
                ios.setdefault("_dltdu_tok", {})[db] = [t1, t2]


def build_p2():
    nc = bass.Bass("TRN2", target_bir_lowering=False, debug=False)
    ios = {"const": {}}
    for name, shape, dt in CONST_SPECS:
        ios["const"][name] = nc.dram_tensor("c_" + name, shape, dt, kind="ExternalInput").ap()
    ios["x"] = nc.dram_tensor("x", [TOK, D], F32, kind="ExternalInput").ap()
    ios["ao"] = nc.dram_tensor("ao", [NCORES, 128, TOK], BF16, kind="ExternalInput").ap()
    ios["wo"] = nc.dram_tensor("wo", [D, D], F32, kind="ExternalInput").ap()
    ios["M"] = nc.dram_tensor("M", [D, D], F32, kind="ExternalInput").ap()
    ios["ffn_bias"] = nc.dram_tensor("ffn_bias", [D], F32, kind="ExternalInput").ap()
    ios["x2"] = nc.dram_tensor("x2", [TOK, D], F32, kind="ExternalOutput").ap()
    ios["x2tail"] = nc.dram_tensor("x2tail", [3, D], F32, kind="ExternalOutput").ap()
    with contextlib.ExitStack() as ctx:
        mem = Mem(nc, ctx)
        S = Sched(nc)
        emit_p2(S, mem, ios)
        S.step("gpsimd", lambda e: e.nop(), waits=S.dma_barrier(), sem=None)
        block = ctx.enter_context(nc.Block())
        S.build(block)
    S.close()
    return nc


def build_p3():
    nc = bass.Bass("TRN2", target_bir_lowering=False, debug=False)
    ios = {"const": {}}
    for name, shape, dt in CONST_SPECS:
        ios["const"][name] = nc.dram_tensor("c_" + name, shape, dt, kind="ExternalInput").ap()
    ios["x2"] = nc.dram_tensor("x2", [TOK, D], F32, kind="ExternalInput").ap()
    ios["halo_g"] = nc.dram_tensor("halo_g", [NCORES, 3, D], F32, kind="ExternalInput").ap()
    ios["onehot"] = nc.dram_tensor("onehot", [NCORES, 1], F32, kind="ExternalInput").ap()
    ios["W_in"] = nc.dram_tensor("W_in", [D, 2 * D_IN], F32, kind="ExternalInput").ap()
    ios["conv_w"] = nc.dram_tensor("conv_w", [4, D_IN, D_IN], F32, kind="ExternalInput").ap()
    ios["conv_b"] = nc.dram_tensor("conv_b", [D_IN], F32, kind="ExternalInput").ap()
    ios["W_x"] = nc.dram_tensor("W_x", [D_IN, 96], F32, kind="ExternalInput").ap()
    ios["W_dt"] = nc.dram_tensor("W_dt", [DT_RANK, D_IN], F32, kind="ExternalInput").ap()
    ios["b_dt"] = nc.dram_tensor("b_dt", [D_IN], F32, kind="ExternalInput").ap()
    ios["dlt"] = nc.dram_tensor("dlt", [D_IN, TOK], BF16, kind="ExternalOutput").ap()
    ios["du"] = nc.dram_tensor("du", [D_IN, TOK], BF16, kind="ExternalOutput").ap()
    ios["sres"] = nc.dram_tensor("sres", [D_IN, TOK], BF16, kind="ExternalOutput").ap()
    ios["bc"] = nc.dram_tensor("bc", [32, TOK], BF16, kind="ExternalOutput").ap()
    ios["halo_scr"] = nc.dram_tensor("halo_scr", [3, D], F32, kind="Internal").ap()
    ios["u_scr"] = nc.dram_tensor("u_scr", [D_IN, TOK], BF16, kind="Internal").ap()
    with contextlib.ExitStack() as ctx:
        mem = Mem(nc, ctx)
        S = Sched(nc)
        emit_p3(S, mem, ios)
        S.step("gpsimd", lambda e: e.nop(), waits=S.dma_barrier(), sem=None)
        block = ctx.enter_context(nc.Block())
        S.build(block)
    S.close()
    return nc



def emit_p4(S, mem, ios):
    NG = 16
    F = 2048
    with mem.scope(S) as m1:
        e8 = m1.sb("p4_e8", [128, NG * 128], BF16)
        ws = m1.sb("p4_ws", [128, NG * 128], BF16)
        e16 = m1.sb("p4_e16", [32, 256], BF16)
        alog = m1.sb("p4_alog", [128, 32])
        alan = m1.sb("p4_alan", [128, 32])
        dcol = m1.sb("p4_dcol", [128, 2])
        dlt_b = m1.sb("p4_dlt", [128, 2 * T_FULL], BF16)
        du_b = m1.sb("p4_du", [128, 2 * T_FULL], BF16)
        sres_b = m1.sb("p4_sres", [128, 2 * T_FULL], BF16)
        y_sb = m1.sb("p4_y", [128, 2 * T_FULL], BF16)
        bc_b = m1.sb("p4_bc", [32, T_FULL], BF16)
        brep = m1.sb("p4_brep", [128, F])
        crep = m1.sb("p4_crep", [128, F])
        a_sb = m1.sb("p4_a", [128, F])
        b_sb = m1.sb("p4_b", [128, F])
        s_sb = m1.sb("p4_s", [128, F])
        sc_sb = m1.sb("p4_sc", [128, F], BF16)
        carry = m1.sb("p4_carry", [128, 64])
        resid = m1.sb("p4_resid", [128, 2])
        carrybf = m1.sb("p4_carrybf", [128, 64], BF16)
        psE = [m1.ps(f"p4_psE{i}", [128, 512]) for i in range(2)]
        psU = [m1.ps(f"p4_psU{i}", [128, 512]) for i in range(2)]
        psY = m1.ps("p4_psY", [128, 2048])
        cst = ios["const"]

        S.dma("gpsimd", e8[:, :], ios["e8sel"], writes=[e8])
        S.dma("gpsimd", ws[:, :], ios["wsel"], writes=[ws])
        S.dma("gpsimd", e16[:, :], ios["e16sel"], writes=[e16])
        S.dma("gpsimd", alog[:, :], ios["alog_lanes"], writes=[alog])
        S.dma("gpsimd", dcol[:, :], ios["d_col"], writes=[dcol])
        S.step("scalar", lambda e: e.activation(alan.ten[:, :], alog.ten[:, :], AF.Exp),
               reads=[alog], writes=[alan])
        S.step("vector", lambda e: e.tensor_scalar(alan.ten[:, :], alan.ten[:, :],
                                                   -1.0, None, ALU.mult),
               reads=[alan], writes=[alan])
        S.step("vector", lambda e: e.memset(carry.ten[:, :], 0.0), writes=[carry])

        for b in range(2):
            for t3, io3 in ((dlt_b, "dlt_g"), (du_b, "du_g"), (sres_b, "sres_g")):
                for dh in range(2):
                    S.dma("sync",
                          t3.ten[:, dh * T_FULL:(dh + 1) * T_FULL]
                              .rearrange("p (s t) -> p s t", s=4),
                          ios[io3][4 * b:4 * b + 4, dh * 128:(dh + 1) * 128, :]
                              .rearrange("s p t -> p s t"),
                          writes=[t3])
            S.dma("sync", bc_b.ten.ap().rearrange("p (s t) -> p s t", s=4),
                  ios["bc_g"][4 * b:4 * b + 4, :, :].rearrange("s p t -> p s t"),
                  writes=[bc_b])
            if b == 0:
                S.step("vector", lambda e: e.memset(carry.ten[:, :], 0.0),
                       reads=[], writes=[carry])
            else:
                S.step("vector", lambda e: e.memset(carry.ten[:, :], 0.0),
                       writes=[carry])
            for tq in range(4):
                t0 = tq * F
                for which, rep, r0 in ((0, brep, 0), (1, crep, 16)):
                    for sub in range(4):
                        ps = psE[sub % 2]
                        S.mm_group([(e16[0:32, r0 * 8:r0 * 8 + 128] if False else
                                     e16[0:32, which * 128:(which + 1) * 128],
                                     bc_b[0:32, t0 + sub * 512: t0 + (sub + 1) * 512])],
                                   ps, reads=[e16, bc_b])
                        S.step("scalar", lambda e, rep=rep, sub=sub, ps=ps: e.copy(
                            rep.ten[:, sub * 512:(sub + 1) * 512], ps.ten[:, :]),
                            reads=[ps], writes=[rep])
                ystart = [False] * 4
                for dh in range(2):
                    for g16 in range(NG):
                        gidx = dh * NG + g16
                        dsl = slice(dh * T_FULL + t0, dh * T_FULL + t0 + F)
                        for sub in range(4):
                            ps = psE[sub % 2]
                            S.mm_group([(e8[:, g16 * 128:(g16 + 1) * 128],
                                         dlt_b[:, dh * T_FULL + t0 + sub * 512:
                                               dh * T_FULL + t0 + (sub + 1) * 512])],
                                       ps, reads=[e8, dlt_b])
                            S.step("scalar", lambda e, sub=sub, ps=ps, gidx=gidx: e.activation(
                                a_sb.ten[:, sub * 512:(sub + 1) * 512], ps.ten[:, :],
                                AF.Exp, scale=alan.ten[:, gidx:gidx + 1]),
                                reads=[ps, alan], writes=[a_sb])
                        for sub in range(4):
                            ps = psU[sub % 2]
                            S.mm_group([(e8[:, g16 * 128:(g16 + 1) * 128],
                                         du_b[:, dh * T_FULL + t0 + sub * 512:
                                              dh * T_FULL + t0 + (sub + 1) * 512])],
                                       ps, reads=[e8, du_b])
                            S.step("vector", lambda e, sub=sub, ps=ps: e.tensor_mul(
                                b_sb.ten[:, sub * 512:(sub + 1) * 512], ps.ten[:, :],
                                brep.ten[:, sub * 512:(sub + 1) * 512]),
                                reads=[ps, brep], writes=[b_sb])
                        ccol = carry.ten[:, b * 0 + gidx + (0 if b == 0 else 0):gidx + 1]
                        S.step("vector", lambda e, gidx=gidx: e.tensor_tensor_scan(
                            s_sb.ten[:, :], a_sb.ten[:, :], b_sb.ten[:, :],
                            carry.ten[:, gidx:gidx + 1], ALU.mult, ALU.add),
                            reads=[a_sb, b_sb, carry], writes=[s_sb])
                        S.step("vector", lambda e, gidx=gidx: e.tensor_copy(
                            carry.ten[:, gidx:gidx + 1], s_sb.ten[:, F - 1:F]),
                            reads=[s_sb], writes=[carry])
                        S.step("gpsimd", lambda e: e.tensor_mul(
                            sc_sb.ten[:, :], s_sb.ten[:, :], crep.ten[:, :]),
                            reads=[s_sb, crep], writes=[sc_sb])
                        for sub in range(4):
                            wl = []
                            if not ystart[sub]:
                                if psY.wr is not None:
                                    wl.append(psY.wr)
                                wl.extend(psY.rds)
                            if sc_sb.wr is not None:
                                wl.append(sc_sb.wr)
                            if ws.wr is not None:
                                wl.append(ws.wr)
                            toky = S._record(
                                "tensor",
                                lambda e, g16=g16, sub=sub, s=(not ystart[sub]),
                                       last=(g16 == NG - 1): e.matmul(
                                    psY.ten[:, sub * 512:(sub + 1) * 512],
                                    ws.ten[:, g16 * 128:(g16 + 1) * 128],
                                    sc_sb.ten[:, sub * 512:(sub + 1) * 512],
                                    start=s, stop=last, skip_group_check=True),
                                wl, "pe", 1)
                            ystart[sub] = True
                            sc_sb.rds.append(toky)
                            ws.rds.append(toky)
                            if g16 == NG - 1:
                                if sub == 3:
                                    psY.wr = toky
                                    psY.rds = []
                    S.step("scalar", lambda e, dh=dh, t0=t0: e.copy(
                        y_sb.ten[:, dh * T_FULL + t0: dh * T_FULL + t0 + F],
                        psY.ten[:, :]),
                        reads=[psY], writes=[y_sb])
                    ystart = [False] * 4
            for dh in range(2):
                S.step("vector", lambda e, dh=dh: e.tensor_copy(
                    carrybf.ten[:, dh * NG:(dh + 1) * NG],
                    carry.ten[:, dh * NG:(dh + 1) * NG]),
                    reads=[carry], writes=[carrybf])
                items = []
                for g16 in range(NG):
                    gidx = dh * NG + g16
                    items.append((ws[:, g16 * 128:(g16 + 1) * 128],
                                  carrybf[:, gidx:gidx + 1]))
                S.mm_group(items, psE[0], reads=[ws, carrybf],
                           psum_slice=(slice(None), slice(0, 1)))
                S.step("vector", lambda e, dh=dh: e.tensor_mul(
                    resid.ten[:, dh:dh + 1], psE[0].ten[:, 0:1], dcol.ten[:, dh:dh + 1]),
                    reads=[psE[0], dcol], writes=[resid])
                S.step("scalar", lambda e, dh=dh: e.add(
                    y_sb.ten[:, dh * T_FULL:(dh + 1) * T_FULL],
                    y_sb.ten[:, dh * T_FULL:(dh + 1) * T_FULL],
                    resid.ten[:, dh:dh + 1]),
                    reads=[resid], writes=[y_sb])
                S.step("vector", lambda e, dh=dh: e.tensor_mul(
                    y_sb.ten[:, dh * T_FULL:(dh + 1) * T_FULL],
                    y_sb.ten[:, dh * T_FULL:(dh + 1) * T_FULL],
                    sres_b.ten[:, dh * T_FULL:(dh + 1) * T_FULL]),
                    reads=[sres_b], writes=[y_sb])
                for tq in range(4):
                    S.dma("sync", ios["ya2a"][4 * b + tq, dh * 128:(dh + 1) * 128, :],
                          y_sb[:, dh * T_FULL + tq * F: dh * T_FULL + (tq + 1) * F],
                          reads=[y_sb])


def build_p4():
    nc = bass.Bass("TRN2", target_bir_lowering=False, debug=False)
    ios = {"const": {}}
    for name, shape, dt in CONST_SPECS:
        ios["const"][name] = nc.dram_tensor("c_" + name, shape, dt, kind="ExternalInput").ap()
    for nm in ("dlt_g", "du_g", "sres_g"):
        ios[nm] = nc.dram_tensor(nm, [NCORES, 256, TOK], BF16, kind="ExternalInput").ap()
    ios["bc_g"] = nc.dram_tensor("bc_g", [NCORES, 32, TOK], BF16, kind="ExternalInput").ap()
    ios["alog_lanes"] = nc.dram_tensor("alog_lanes", [128, 32], F32, kind="ExternalInput").ap()
    ios["d_col"] = nc.dram_tensor("d_col", [128, 2], F32, kind="ExternalInput").ap()
    ios["e8sel"] = nc.dram_tensor("e8sel", [128, 16 * 128], BF16, kind="ExternalInput").ap()
    ios["wsel"] = nc.dram_tensor("wsel", [128, 16 * 128], BF16, kind="ExternalInput").ap()
    ios["e16sel"] = nc.dram_tensor("e16sel", [32, 256], BF16, kind="ExternalInput").ap()
    ios["ya2a"] = nc.dram_tensor("ya2a", [NCORES, 256, TOK], BF16, kind="ExternalOutput").ap()
    with contextlib.ExitStack() as ctx:
        mem = Mem(nc, ctx)
        S = Sched(nc)
        emit_p4(S, mem, ios)
        S.step("gpsimd", lambda e: e.nop(), waits=S.dma_barrier(), sem=None)
        block = ctx.enter_context(nc.Block())
        S.build(block)
    S.close()
    return nc


def p4_host_consts():
    e8 = np.zeros((128, 16 * 128), np.float32)
    for g in range(16):
        for l in range(128):
            e8[g * 8 + l // 16, g * 128 + l] = 1.0
    ws = np.zeros((128, 16 * 128), np.float32)
    for g in range(16):
        for l in range(128):
            ws[l, g * 128 + g * 8 + l // 16] = 1.0
    e16 = np.zeros((32, 256), np.float32)
    for l in range(128):
        e16[l % 16, l] = 1.0
        e16[16 + l % 16, 128 + l] = 1.0
    import ml_dtypes
    bf = ml_dtypes.bfloat16
    return e8.astype(bf), ws.astype(bf), e16.astype(bf)



def emit_p5(S, mem, ios):
    cst = ios["const"]
    NDB = D_IN // 128
    with mem.scope(S) as m0:
        ident_b = m0.sb("p5_identb", [128, 128], BF16)
        epscol = m0.sb("p5_eps", [128, 1])
        acc = m0.sb("p5_acc", [128, NT * D])
        xfT = m0.sb("p5_xfT", [128, NCB * TOK], BF16)
        combT = m0.sb("p5_combT", [16, TOK], BF16)
        stat = m0.sb("p5_stat", [128, 8])
        psA = m0.ps("p5_psA", [128, 512])
        psB = m0.ps("p5_psB", [128, 512])
        psC = m0.ps("p5_psC", [128, 512])
        psT = [m0.ps("p5_psT1", [128, 128], BF16), m0.ps("p5_psT2", [128, 128], BF16)]
        S.dma("gpsimd", ident_b[:, :], cst["ident"], writes=[ident_b])
        S.dma("gpsimd", epscol[:, :], cst["epscol"], writes=[epscol])

        with mem.scope(S) as m1:
            ygt = m1.sb("p5_ygt", [128, (NDB // 2) * TOK], BF16)
            wout = m1.sb("p5_wout", [128, NDB * D], BF16)
            x2_t = [m1.sb(f"p5_x2{i}", [128, D]) for i in range(2)]
            xln_f = m1.sb("p5_xlnf", [128, D])
            xln_bf = m1.sb("p5_xlnbf", [128, D], BF16)
            sq = m1.sb("p5_sq", [128, D])
            S.dma("gpsimd", wout.ten.ap().rearrange("p (n c) -> p n c", n=NDB),
                  ios["W_out"].rearrange("(n p) c -> p n c", p=128), writes=[wout])
            for dh in range(2):
                S.dma("sync", ygt.ten.ap().rearrange("p (j t) -> p j t", j=NDB // 2),
                      ios["yg_g"].rearrange("j (g h p) t -> p (j g h) t", p=128, g=2)
                          [:, dh, :, :] if False else
                      ios["yg_g"][:, dh * 128:(dh + 1) * 128, :]
                          .rearrange("j p t -> p j t"),
                      writes=[ygt])
                for ti in range(NT):
                    xt = x2_t[ti % 2]
                    if dh == 0:
                        S.dma("sync", xt[:, :], ios["x2"][ti * 128:(ti + 1) * 128, :],
                              writes=[xt])
                    for oc in range(2):
                        psum = psA if oc == 0 else psB
                        items = [(ygt[:, it * TOK + ti * 128: it * TOK + (ti + 1) * 128],
                                  wout[:, (2 * it + dh) * D + oc * 512:
                                       (2 * it + dh) * D + (oc + 1) * 512])
                             for it in range(NDB // 2)]
                        S.mm_group(items, psum, reads=[ygt, wout])
                        if dh == 0:
                            S.step("vector", lambda e, psum=psum, oc=oc, ti=ti, xt=xt: e.tensor_add(
                                acc[:, ti * D + oc * 512: ti * D + (oc + 1) * 512],
                                psum.ten[:, :], xt.ten[:, oc * 512:(oc + 1) * 512]),
                                reads=[psum, xt], writes=[acc])
                        else:
                            S.step("vector", lambda e, psum=psum, oc=oc, ti=ti: e.tensor_add(
                                acc[:, ti * D + oc * 512: ti * D + (oc + 1) * 512],
                                psum.ten[:, :],
                                acc[:, ti * D + oc * 512: ti * D + (oc + 1) * 512]),
                                reads=[psum], writes=[acc])
            for ti in range(NT):
                ln_emit(S, m1, acc[:, ti * D:(ti + 1) * D], xln_f[:, :], stat, sq, epscol,
                        reads=[acc], writes=[xln_f])
                S.step("vector", lambda e: e.tensor_copy(xln_bf.ten[:, :], xln_f.ten[:, :]),
                       reads=[xln_f], writes=[xln_bf])
                for cb in range(NCB):
                    p = psT[cb % 2]
                    S.step("tensor", lambda e, cb=cb, p=p: e.transpose(
                        p.ten[:, :], xln_bf.ten[:, cb * 128:(cb + 1) * 128], ident_b.ten[:, :]),
                        reads=[xln_bf, ident_b], writes=[p])
                    S.step("scalar", lambda e, cb=cb, ti=ti, p=p: e.copy(
                        xfT[:, cb * TOK + ti * 128: cb * TOK + (ti + 1) * 128], p.ten[:, :]),
                        reads=[p], writes=[xfT])

            rwr = m1.sb("p5_rwr", [16, D], BF16)
            rwt = m1.sb("p5_rwt", [128, NCB * 16], BF16)
            scT = m1.sb("p5_scT", [16, TOK])
            scores_s = m1.sb("p5_scores", [128, NT * 16])
            S.dma("gpsimd", rwr[:, :], ios["router_w"], writes=[rwr])
            for cb in range(NCB):
                p = psT[cb % 2]
                S.step("tensor", lambda e, cb=cb, p=p: e.transpose(
                    p.ten[0:128, 0:16], rwr.ten[0:16, cb * 128:(cb + 1) * 128],
                    ident_b.ten[0:16, 0:16]),
                    reads=[rwr, ident_b], writes=[p])
                S.step("scalar", lambda e, cb=cb, p=p: e.copy(
                    rwt[:, cb * 16:(cb + 1) * 16], p.ten[0:128, 0:16]),
                    reads=[p], writes=[rwt])
            for tc in range(4):
                items = [(rwt[:, cb * 16:(cb + 1) * 16],
                          xfT[:, cb * TOK + tc * 512: cb * TOK + (tc + 1) * 512])
                         for cb in range(NCB)]
                S.mm_group(items, psA, reads=[rwt, xfT],
                           psum_slice=(slice(0, 16), slice(None)))
                S.step("scalar", lambda e, tc=tc: e.activation(
                    scT.ten[0:16, tc * 512:(tc + 1) * 512], psA.ten[0:16, :], AF.Sigmoid),
                    reads=[psA], writes=[scT])
            scT_bf = m1.sb("p5_scTbf", [16, TOK], BF16)
            S.step("vector", lambda e: e.tensor_copy(scT_bf.ten[0:16, :], scT.ten[0:16, :]),
                   reads=[scT], writes=[scT_bf])
            for ti in range(NT):
                p = psT[ti % 2]
                S.step("tensor", lambda e, ti=ti, p=p: e.transpose(
                    p.ten[0:128, 0:16], scT_bf.ten[0:16, ti * 128:(ti + 1) * 128],
                    ident_b.ten[0:16, 0:16]),
                    reads=[scT_bf, ident_b], writes=[p])
                S.step("scalar", lambda e, ti=ti, p=p: e.copy(
                    scores_s[:, ti * 16:(ti + 1) * 16], p.ten[0:128, 0:16]),
                    reads=[p], writes=[scores_s])

            cb_b = m1.sb("p5_cbb", [128, 16])
            cbr = m1.sb("p5_cbr", [1, 16])
            ones_f = m1.sb("p5_onesf", [1, 128])
            neginf = m1.sb("p5_neg", [128, 16])
            sb_t = m1.sb("p5_sbt", [128, 16])
            pstmp = m1.sb("p5_pstmp", [128, 24])
            gs_s = m1.sb("p5_gs", [128, 4])
            gmin = m1.sb("p5_gmin", [128, 6])
            red = m1.sb("p5_red", [128, 4])
            em_s = m1.sb("p5_em", [128, 16])
            msk = m1.sb("p5_msk", [128, 16])
            mm_s = m1.sb("p5_mm", [128, 16])
            mm2_s = m1.sb("p5_mm2", [128, 16])
            eq_s = m1.sb("p5_eq", [128, 16])
            wr_s2 = m1.sb("p5_wr", [128, 16])
            comb_bf = m1.sb("p5_combbf", [128, 16], BF16)
            em_u8 = m1.sb("p5_emu8", [128, 16], mybir.dt.uint8)
            eq_u8 = m1.sb("p5_equ8", [128, 16], mybir.dt.uint8)
            S.dma("sync", cbr[:, :], ios["corr_bias"].rearrange("(x e) -> x e", x=1),
                  writes=[cbr])
            S.dma("sync", ones_f[:, :], cst["ones1x128"], writes=[ones_f])
            S.step("vector", lambda e: e.memset(neginf.ten[:, :], -1e30), writes=[neginf])
            S.mm_group([(ones_f[0:1, :], cbr[0:1, :])], psA, reads=[ones_f, cbr],
                       psum_slice=(slice(None), slice(0, 16)))
            S.step("scalar", lambda e: e.copy(cb_b.ten[:, :], psA.ten[:, 0:16]),
                   reads=[psA], writes=[cb_b])
            for ti in range(NT):
                sc_ap = scores_s[:, ti * 16:(ti + 1) * 16]
                S.step("vector", lambda e, sc_ap=sc_ap: e.tensor_add(
                    sb_t.ten[:, :], sc_ap, cb_b.ten[:, :]),
                    reads=[scores_s, cb_b], writes=[sb_t])
                pk = 0
                for i in range(4):
                    for j in range(i + 1, 4):
                        S.step("vector", lambda e, i=i, j=j, pk=pk: e.tensor_add(
                            pstmp.ten[:, pk * 4:(pk + 1) * 4],
                            sb_t.ten.ap()[:, i:13 + i:4], sb_t.ten.ap()[:, j:13 + j:4]),
                            reads=[sb_t], writes=[pstmp])
                        pk += 1
                S.step("vector", lambda e: e.tensor_max(
                    gs_s.ten[:, :], pstmp.ten[:, 0:4], pstmp.ten[:, 4:8]),
                    reads=[pstmp], writes=[gs_s])
                for k in range(2, 6):
                    S.step("vector", lambda e, k=k: e.tensor_max(
                        gs_s.ten[:, :], gs_s.ten[:, :], pstmp.ten[:, k * 4:(k + 1) * 4]),
                        reads=[pstmp], writes=[gs_s])
                pk = 0
                for i in range(4):
                    for j in range(i + 1, 4):
                        S.step("vector", lambda e, i=i, j=j, pk=pk: e.tensor_tensor(
                            gmin.ten[:, pk:pk + 1], gs_s.ten[:, i:i + 1],
                            gs_s.ten[:, j:j + 1], ALU.min),
                            reads=[gs_s], writes=[gmin])
                        pk += 1
                S.step("vector", lambda e: e.reduce_max(red.ten[:, 0:1], gmin.ten[:, :], axis=AXX),
                       reads=[gmin], writes=[red])
                S.step("vector", lambda e: e.tensor_scalar(
                    gs_s.ten[:, :], gs_s.ten[:, :], red.ten[:, 0:1], None, ALU.is_ge),
                    reads=[red], writes=[gs_s])
                for j in range(4):
                    S.step("vector", lambda e, j=j: e.tensor_copy(
                        em_s.ten.ap()[:, j:13 + j:4], gs_s.ten[:, :]),
                        reads=[gs_s], writes=[em_s])
                S.step("vector", lambda e: e.tensor_copy(em_u8.ten[:, :], em_s.ten[:, :]),
                       reads=[em_s], writes=[em_u8])
                S.step("vector", lambda e: e.select(
                    msk.ten[:, :], em_u8.ten[:, :], sb_t.ten[:, :], neginf.ten[:, :], add_drain=True),
                    reads=[em_u8, sb_t, neginf], writes=[msk])
                S.step("vector", lambda e: e.tensor_copy(mm_s.ten[:, :], msk.ten[:, :]),
                       reads=[msk], writes=[mm_s])
                cur, nxt = mm_s, mm2_s
                for it in range(3):
                    S.step("vector", lambda e, cur=cur: e.reduce_max(
                        red.ten[:, 2:3], cur.ten[:, :], axis=AXX),
                        reads=[cur], writes=[red])
                    S.step("vector", lambda e, cur=cur: e.tensor_scalar(
                        eq_s.ten[:, :], cur.ten[:, :], red.ten[:, 2:3], None, ALU.is_equal),
                        reads=[cur, red], writes=[eq_s])
                    S.step("vector", lambda e: e.tensor_copy(eq_u8.ten[:, :], eq_s.ten[:, :]),
                           reads=[eq_s], writes=[eq_u8])
                    S.step("vector", lambda e, cur=cur, nxt=nxt: e.select(
                        nxt.ten[:, :], eq_u8.ten[:, :], neginf.ten[:, :], cur.ten[:, :], add_drain=True),
                        reads=[eq_u8, neginf, cur], writes=[nxt])
                    cur, nxt = nxt, cur
                S.step("vector", lambda e, cur=cur: e.reduce_max(
                    red.ten[:, 3:4], cur.ten[:, :], axis=AXX),
                    reads=[cur], writes=[red])
                S.step("vector", lambda e: e.tensor_scalar(
                    msk.ten[:, :], msk.ten[:, :], red.ten[:, 3:4], None, ALU.is_ge),
                    reads=[red], writes=[msk])
                S.step("vector", lambda e, sc_ap=sc_ap: e.tensor_mul(
                    wr_s2.ten[:, :], sc_ap, msk.ten[:, :]),
                    reads=[scores_s, msk], writes=[wr_s2])
                S.step("vector", lambda e: e.reduce_sum(
                    red.ten[:, 1:2], wr_s2.ten[:, :], axis=AXX),
                    reads=[wr_s2], writes=[red])
                S.step("vector", lambda e: e.tensor_scalar(
                    red.ten[:, 1:2], red.ten[:, 1:2], 1e-20, None, ALU.add),
                    reads=[red], writes=[red])
                S.step("vector", lambda e: e.reciprocal(
                    red.ten[:, 2:3], red.ten[:, 1:2]),
                    reads=[red], writes=[red])
                S.step("vector", lambda e, ti=ti: e.tensor_scalar(
                    comb_bf.ten[:, :], wr_s2.ten[:, :],
                    red.ten[:, 2:3], None, ALU.mult),
                    reads=[wr_s2, red], writes=[comb_bf])
                p = psT[ti % 2]
                S.step("tensor", lambda e, p=p: e.transpose(
                    p.ten[0:16, 0:128], comb_bf.ten[:, 0:16], ident_b.ten[:, :]),
                    reads=[comb_bf, ident_b], writes=[p])
                S.step("scalar", lambda e, ti=ti, p=p: e.copy(
                    combT.ten[0:16, ti * 128:(ti + 1) * 128], p.ten[0:16, 0:128]),
                    reads=[p], writes=[combT])

        with mem.scope(S) as m2:
            w_s = m2.sb("p5_ws", [128, NCB * D], BF16)
            swT = m2.sb("p5_swT", [128, NCB * (HSH // 2)], BF16)
            hwsh = m2.sb("p5_hwsh", [128, 8 * TOK], BF16)
            sdwT = m2.sb("p5_sdwT", [128, 8 * D], BF16)
            for hh in range(2):
                for wi_, wio in ((0, "shared_gate_w"), (1, "shared_up_w")):
                    S.dma("gpsimd", w_s.ten.ap().rearrange("p (n c) -> p n c", n=NCB),
                          ios[wio][hh * (HSH // 2):(hh + 1) * (HSH // 2), :]
                              .rearrange("(n p) c -> p n c", p=128),
                          writes=[w_s])
                    transpose_tiles(S, m2, w_s, 0, swT, 0, NCB, NCB, D, HSH // 2,
                                    ident_b, psT)
                    for ht in range(8):
                        for tc in range(4):
                            psum = psA if tc % 2 == 0 else psB
                            items = [(swT[:, cb * (HSH // 2) + ht * 128:
                                          cb * (HSH // 2) + (ht + 1) * 128],
                                      xfT[:, cb * TOK + tc * 512: cb * TOK + (tc + 1) * 512])
                                     for cb in range(NCB)]
                            S.mm_group(items, psum, reads=[swT, xfT])
                            if wi_ == 0:
                                S.step("scalar", lambda e, ht=ht, tc=tc, psum=psum: e.activation(
                                    hwsh.ten[:, ht * TOK + tc * 512: ht * TOK + (tc + 1) * 512],
                                    psum.ten[:, :], AF.Silu),
                                    reads=[psum], writes=[hwsh])
                            else:
                                S.step("vector", lambda e, ht=ht, tc=tc, psum=psum: e.tensor_mul(
                                    hwsh.ten[:, ht * TOK + tc * 512: ht * TOK + (tc + 1) * 512],
                                    psum.ten[:, :],
                                    hwsh.ten[:, ht * TOK + tc * 512: ht * TOK + (tc + 1) * 512]),
                                    reads=[psum], writes=[hwsh])
                S.dma("gpsimd", w_s.ten.ap().rearrange("p (n c) -> p n c", n=NCB),
                      ios["shared_down_w"][:, hh * (HSH // 2):(hh + 1) * (HSH // 2)]
                          .rearrange("(n p) c -> p n c", p=128),
                      writes=[w_s])
                transpose_tiles(S, m2, w_s, 0, sdwT, 0, NCB, 8, D, D, ident_b, psT)
                for ti in range(NT):
                    for oc in range(2):
                        psum = psA if oc == 0 else psB
                        items = [(hwsh[:, ht * TOK + ti * 128: ht * TOK + (ti + 1) * 128],
                                  sdwT[:, ht * D + oc * 512: ht * D + (oc + 1) * 512])
                                 for ht in range(8)]
                        S.mm_group(items, psum, reads=[hwsh, sdwT])
                        S.step("vector", lambda e, psum=psum, ti=ti, oc=oc: e.tensor_add(
                            acc[:, ti * D + oc * 512: ti * D + (oc + 1) * 512],
                            psum.ten[:, :],
                            acc[:, ti * D + oc * 512: ti * D + (oc + 1) * 512]),
                            reads=[psum], writes=[acc])

        with mem.scope(S) as m3:
            gw_t = [m3.sb(f"p5_gw{i}", [128, NCB * H_EXP], BF16) for i in range(2)]
            uw_t = [m3.sb(f"p5_uw{i}", [128, NCB * H_EXP], BF16) for i in range(2)]
            dw_t = [m3.sb(f"p5_dw{i}", [128, 2 * D], BF16) for i in range(2)]
            esel = m3.sb("p5_esel", [16, 16 * 128], BF16)
            ge_s = m3.sb("p5_ge", [128, 2 * TOK], BF16)
            hw_s = m3.sb("p5_hw", [128, 2 * TOK], BF16)
            S.dma("gpsimd", esel[:, :], ios["esel"], writes=[esel])
            for ex in range(NEXP):
                gw_s = gw_t[ex % 2]
                uw_s = uw_t[ex % 2]
                dw_s = dw_t[ex % 2]
                S.dma("gpsimd", gw_s.ten.ap().rearrange("p (n c) -> p n c", n=NCB),
                      ios["gate_w"][ex].rearrange("(n p) c -> p n c", p=128), writes=[gw_s])
                S.dma("gpsimd", uw_s.ten.ap().rearrange("p (n c) -> p n c", n=NCB),
                      ios["up_w"][ex].rearrange("(n p) c -> p n c", p=128), writes=[uw_s])
                S.dma("gpsimd", dw_s.ten.ap().rearrange("p (n c) -> p n c", n=2),
                      ios["down_w"][ex].rearrange("(n p) c -> p n c", p=128), writes=[dw_s])
                for tc in range(4):
                    S.mm_group([(esel[0:16, ex * 128:(ex + 1) * 128],
                                 combT[0:16, tc * 512:(tc + 1) * 512])], psC,
                               reads=[esel, combT])
                    for hb in range(2):
                        items = [(gw_s[:, cb * H_EXP + hb * 128: cb * H_EXP + (hb + 1) * 128],
                                  xfT[:, cb * TOK + tc * 512: cb * TOK + (tc + 1) * 512])
                                 for cb in range(NCB)]
                        S.mm_group(items, psA, reads=[gw_s, xfT])
                        S.step("scalar", lambda e, hb=hb, tc=tc: e.activation(
                            ge_s.ten[:, hb * TOK + tc * 512: hb * TOK + (tc + 1) * 512],
                            psA.ten[:, :], AF.Silu),
                            reads=[psA], writes=[ge_s])
                        items = [(uw_s[:, cb * H_EXP + hb * 128: cb * H_EXP + (hb + 1) * 128],
                                  xfT[:, cb * TOK + tc * 512: cb * TOK + (tc + 1) * 512])
                                 for cb in range(NCB)]
                        S.mm_group(items, psB, reads=[uw_s, xfT])
                        S.step("vector", lambda e, hb=hb, tc=tc: e.tensor_mul(
                            ge_s.ten[:, hb * TOK + tc * 512: hb * TOK + (tc + 1) * 512],
                            psB.ten[:, :],
                            ge_s.ten[:, hb * TOK + tc * 512: hb * TOK + (tc + 1) * 512]),
                            reads=[psB], writes=[ge_s])
                        S.step("vector", lambda e, hb=hb, tc=tc: e.tensor_mul(
                            hw_s.ten[:, hb * TOK + tc * 512: hb * TOK + (tc + 1) * 512],
                            psC.ten[:, :],
                            ge_s.ten[:, hb * TOK + tc * 512: hb * TOK + (tc + 1) * 512]),
                            reads=[psC, ge_s], writes=[hw_s])
                for ti in range(NT):
                    for oc in range(2):
                        psum = psA if oc == 0 else psB
                        items = [(hw_s[:, hb * TOK + ti * 128: hb * TOK + (ti + 1) * 128],
                                  dw_s[:, hb * D + oc * 512: hb * D + (oc + 1) * 512])
                                 for hb in range(2)]
                        S.mm_group(items, psum, reads=[hw_s, dw_s])
                        S.step("vector", lambda e, psum=psum, ti=ti, oc=oc: e.tensor_add(
                            acc[:, ti * D + oc * 512: ti * D + (oc + 1) * 512],
                            psum.ten[:, :],
                            acc[:, ti * D + oc * 512: ti * D + (oc + 1) * 512]),
                            reads=[psum], writes=[acc])

        S.dma("sync", ios["out"].rearrange("(n p) c -> p n c", p=128),
              acc.ten.ap().rearrange("p (n c) -> p n c", n=NT), reads=[acc])


def build_p5():
    nc = bass.Bass("TRN2", target_bir_lowering=False, debug=False)
    ios = {"const": {}}
    for name, shape, dt in CONST_SPECS:
        ios["const"][name] = nc.dram_tensor("c_" + name, shape, dt, kind="ExternalInput").ap()
    ios["x2"] = nc.dram_tensor("x2", [TOK, D], F32, kind="ExternalInput").ap()
    ios["yg_g"] = nc.dram_tensor("yg_g", [NCORES, 256, TOK], BF16, kind="ExternalInput").ap()
    ios["W_out"] = nc.dram_tensor("W_out", [D_IN, D], F32, kind="ExternalInput").ap()
    ios["router_w"] = nc.dram_tensor("router_w", [16, D], F32, kind="ExternalInput").ap()
    ios["corr_bias"] = nc.dram_tensor("corr_bias", [16], F32, kind="ExternalInput").ap()
    ios["gate_w"] = nc.dram_tensor("gate_w", [NEXP, D, H_EXP], F32, kind="ExternalInput").ap()
    ios["up_w"] = nc.dram_tensor("up_w", [NEXP, D, H_EXP], F32, kind="ExternalInput").ap()
    ios["down_w"] = nc.dram_tensor("down_w", [NEXP, H_EXP, D], F32, kind="ExternalInput").ap()
    ios["shared_gate_w"] = nc.dram_tensor("shared_gate_w", [HSH, D], F32, kind="ExternalInput").ap()
    ios["shared_up_w"] = nc.dram_tensor("shared_up_w", [HSH, D], F32, kind="ExternalInput").ap()
    ios["shared_down_w"] = nc.dram_tensor("shared_down_w", [D, HSH], F32, kind="ExternalInput").ap()
    ios["esel"] = nc.dram_tensor("esel", [16, 16 * 128], BF16, kind="ExternalInput").ap()
    ios["out"] = nc.dram_tensor("out", [TOK, D], F32, kind="ExternalOutput").ap()
    with contextlib.ExitStack() as ctx:
        mem = Mem(nc, ctx)
        S = Sched(nc)
        emit_p5(S, mem, ios)
        S.step("gpsimd", lambda e: e.nop(), waits=S.dma_barrier(), sem=None)
        block = ctx.enter_context(nc.Block())
        S.build(block)
    S.close()
    return nc


def p5_host_consts():
    import ml_dtypes
    esel = np.zeros((16, 16 * 128), np.float32)
    for ex in range(16):
        esel[ex, ex * 128:(ex + 1) * 128] = 1.0
    return esel.astype(ml_dtypes.bfloat16)



NTILE = D_IN // 128


def emit_p4a(S, mem, ios):
    NG = 16
    F = TOK
    dd_tok = ios.get("_dltdu_tok", {})
    bc_tok = ios.get("_bc_tok", S.dma_barrier())
    with mem.scope(S) as m1:
        e8 = m1.sb("a_e8", [128, NG * 128], BF16)
        ws = m1.sb("a_ws", [128, NG * 128], BF16)
        e16 = m1.sb("a_e16", [32, 256], BF16)
        alog = m1.sb("a_alog", [128, 256])
        alan = m1.sb("a_alan", [128, 256])
        bc_b = m1.sb("a_bc", [32, F], BF16)
        brep = m1.sb("a_brep", [128, F])
        crep = m1.sb("a_crep", [128, F])
        dlt_t = [m1.sb(f"a_dlt{i}", [128, F], BF16) for i in range(2)]
        du_t = [m1.sb(f"a_du{i}", [128, F], BF16) for i in range(2)]
        ones_b = m1.sb("a_ones", [128, F])
        dltf = m1.sb("a_dltf", [128, F])
        cdf = m1.sb("a_cdf", [128, F])
        cdbf_t = [m1.sb(f"a_cdbf{i}", [128, F], BF16) for i in range(2)]
        a_t = [m1.sb(f"a_a{i}", [128, F]) for i in range(2)]
        bu_t = [m1.sb(f"a_bu{i}", [128, F]) for i in range(2)]
        b_t = [m1.sb(f"a_b{i}", [128, F]) for i in range(2)]
        s_t = [m1.sb(f"a_s{i}", [128, F]) for i in range(3)]
        sc_t = [m1.sb(f"a_sc{i}", [128, F], BF16) for i in range(3)]
        y_t = [m1.sb(f"a_y{i}", [128, F], BF16) for i in range(2)]
        carry = m1.sb("a_carry", [128, NG])
        zcol = m1.sb("a_zcol", [128, 1])
        f_sb = m1.sb("a_f", [128, 256])
        q_sb = m1.sb("a_q", [128, 256])
        qtmp = m1.sb("a_qtmp", [128, NG])
        psE = [m1.ps(f"a_psE{i}", [128, 512]) for i in range(4)]
        psY = m1.ps("a_psY", [128, 2048])
        psU = psE

        S.dma("gpsimd", e8[:, :], ios["e8sel"], writes=[e8])
        S.dma("gpsimd", ws[:, :], ios["wsel"], writes=[ws])
        S.dma("gpsimd", e16[:, :], ios["e16sel"], writes=[e16])
        S.dma("gpsimd", alog[:, :], ios["alog_lanes"], writes=[alog])
        S.dma("gpsimd", bc_b[:, :], ios["bc"], writes=[bc_b], waits=bc_tok)
        S.step("scalar", lambda e: e.activation(alan.ten[:, :], alog.ten[:, :], AF.Exp),
               reads=[alog], writes=[alan])
        S.step("vector", lambda e: e.tensor_scalar(alan.ten[:, :], alan.ten[:, :],
                                                   -1.0, None, ALU.mult),
               reads=[alan], writes=[alan])
        S.step("vector", lambda e: e.memset(ones_b.ten[:, :], 1.0), writes=[ones_b])
        S.step("vector", lambda e: e.memset(zcol.ten[:, :], 0.0), writes=[zcol])

        for which, rep in ((0, brep), (1, crep)):
            for sub in range(4):
                ps = psE[sub]
                S.mm_group([(e16[0:32, which * 128:(which + 1) * 128],
                             bc_b[0:32, sub * 512:(sub + 1) * 512])],
                           ps, reads=[e16, bc_b])
                S.step("scalar", lambda e, rep=rep, sub=sub, ps=ps: e.copy(
                    rep.ten[:, sub * 512:(sub + 1) * 512], ps.ten[:, :]),
                    reads=[ps], writes=[rep])

        for T in range(NTILE):
            dlt = dlt_t[T % 2]
            du = du_t[T % 2]
            cdbf = cdbf_t[T % 2]
            yb = y_t[T % 2]
            tw = dd_tok.get(T, [])
            S.dma("sync", dlt[:, :], ios["dlt"][T * 128:(T + 1) * 128, :],
                  writes=[dlt], waits=tw)
            S.dma("sync", du[:, :], ios["du"][T * 128:(T + 1) * 128, :],
                  writes=[du], waits=tw)
            S.step("scalar", lambda e, dlt=dlt: e.copy(
                dltf.ten[:, :], dlt.ten[:, :]),
                reads=[dlt], writes=[dltf])
            S.step("vector", lambda e: e.tensor_tensor_scan(
                cdf.ten[:, :], ones_b.ten[:, :], dltf.ten[:, :],
                zcol.ten[:, 0:1], ALU.mult, ALU.add),
                reads=[ones_b, dltf, zcol], writes=[cdf])
            S.step("gpsimd", lambda e, cdbf=cdbf: e.tensor_copy(
                cdbf.ten[:, :], cdf.ten[:, :]),
                reads=[cdf], writes=[cdbf])
            S.dma("scalar", ios["cd"][T * 128:(T + 1) * 128, :], cdbf[:, :],
                  reads=[cdbf])
            for g in range(NG):
                S.mm_group([(e8[:, g * 128:(g + 1) * 128],
                             cdbf[:, F - 1:F])], psU[0],
                           reads=[e8, cdbf],
                           psum_slice=(slice(None), slice(g, g + 1)))
            S.step("vector", lambda e, T=T: e.tensor_mul(
                qtmp.ten[:, :], psU[0].ten[:, 0:NG],
                alan.ten[:, T * NG:(T + 1) * NG]),
                reads=[psU[0], alan], writes=[qtmp])
            S.step("scalar", lambda e, T=T: e.activation(
                q_sb.ten[:, T * NG:(T + 1) * NG], qtmp.ten[:, :], AF.Exp),
                reads=[qtmp], writes=[q_sb])
            ystart = [False] * 4

            def yemit(g, sc_sb):
                for sub in range(4):
                    wl = []
                    if not ystart[sub]:
                        if psY.wr is not None:
                            wl.append(psY.wr)
                        wl.extend(psY.rds)
                    if sc_sb.wr is not None:
                        wl.append(sc_sb.wr)
                    if ws.wr is not None:
                        wl.append(ws.wr)
                    toky = S._record(
                        "tensor",
                        lambda e, g=g, sub=sub, s=(not ystart[sub]), sc_sb=sc_sb,
                               last=(g == NG - 1): e.matmul(
                            psY.ten[:, sub * 512:(sub + 1) * 512],
                            ws.ten[:, g * 128:(g + 1) * 128],
                            sc_sb.ten[:, sub * 512:(sub + 1) * 512],
                            start=s, stop=last, skip_group_check=True),
                        wl, "pe", 1)
                    ystart[sub] = True
                    sc_sb.rds.append(toky)
                    ws.rds.append(toky)
                    if g == NG - 1 and sub == 3:
                        psY.wr = toky
                        psY.rds = []

            pend = []
            for g in range(NG):
                col = T * NG + g
                a_sb = a_t[g % 2]
                bu_sb = bu_t[g % 2]
                b_sb = b_t[g % 2]
                s_sb = s_t[g % 3]
                sc_sb = sc_t[g % 3]
                for sub in range(4):
                    ps = psE[sub]
                    S.mm_group([(e8[:, g * 128:(g + 1) * 128],
                                 dlt[:, sub * 512:(sub + 1) * 512])],
                               ps, reads=[e8, dlt])
                    S.step("scalar", lambda e, sub=sub, ps=ps, col=col, a_sb=a_sb: e.activation(
                        a_sb.ten[:, sub * 512:(sub + 1) * 512], ps.ten[:, :],
                        AF.Exp, scale=alan.ten[:, col:col + 1]),
                        reads=[ps, alan], writes=[a_sb])
                for sub in range(4):
                    ps = psE[sub]
                    S.mm_group([(e8[:, g * 128:(g + 1) * 128],
                                 du[:, sub * 512:(sub + 1) * 512])],
                               ps, reads=[e8, du])
                    S.step("vector", lambda e, sub=sub, ps=ps, b_sb=b_sb: e.tensor_mul(
                        b_sb.ten[:, sub * 512:(sub + 1) * 512], ps.ten[:, :],
                        brep.ten[:, sub * 512:(sub + 1) * 512]),
                        reads=[ps, brep], writes=[b_sb])
                if len(pend) >= 2:
                    yemit(*pend.pop(0))
                S.step("vector", lambda e, a_sb=a_sb, b_sb=b_sb, s_sb=s_sb: e.tensor_tensor_scan(
                    s_sb.ten[:, :], a_sb.ten[:, :], b_sb.ten[:, :],
                    zcol.ten[:, 0:1], ALU.mult, ALU.add),
                    reads=[a_sb, b_sb, zcol], writes=[s_sb])
                S.step("scalar", lambda e, col=col, s_sb=s_sb: e.copy(
                    f_sb.ten[:, col:col + 1], s_sb.ten[:, F - 1:F]),
                    reads=[s_sb], writes=[f_sb])
                S.step("gpsimd", lambda e, s_sb=s_sb, sc_sb=sc_sb: e.tensor_mul(
                    sc_sb.ten[:, :], s_sb.ten[:, :], crep.ten[:, :]),
                    reads=[s_sb, crep], writes=[sc_sb])
                pend.append((g, sc_sb))
            for pp in pend:
                yemit(*pp)
            S.step("scalar", lambda e, yb=yb: e.copy(yb.ten[:, :], psY.ten[:, :]),
                   reads=[psY], writes=[yb])
            S.dma("sync", ios["y_local"][T * 128:(T + 1) * 128, :], yb[:, :],
                  reads=[yb])
        S.dma("sync", ios["fq"][0], f_sb[:, :], reads=[f_sb])
        S.dma("sync", ios["fq"][1], q_sb[:, :], reads=[q_sb])



def emit_p4b(S, mem, ios):
    NG = 16
    F = TOK
    with mem.scope(S) as m1:
        e8 = m1.sb("b_e8", [128, NG * 128], BF16)
        ws = m1.sb("b_ws", [128, NG * 128], BF16)
        e16 = m1.sb("b_e16", [32, 256], BF16)
        alog = m1.sb("b_alog", [128, 256])
        alan = m1.sb("b_alan", [128, 256])
        dcol = m1.sb("b_dcol", [128, NTILE])
        oh4 = m1.sb("b_oh4", [128, 4])
        fqg = m1.sb("b_fqg", [128, 4 * 2 * 256])
        bc_b = m1.sb("b_bc", [32, F], BF16)
        crep = m1.sb("b_crep", [128, F])
        acc = m1.sb("b_acc", [128, 256])
        sel = m1.sb("b_sel", [128, 256])
        sfin = m1.sb("b_sfin", [128, 256])
        sfbf = m1.sb("b_sfbf", [128, 256], BF16)
        tmp = m1.sb("b_tmp", [128, 256])
        cd_t = [m1.sb(f"b_cd{i}", [128, F], BF16) for i in range(2)]
        y_t = [m1.sb(f"b_y{i}", [128, F], BF16) for i in range(2)]
        sres_t = [m1.sb(f"b_sres{i}", [128, F], BF16) for i in range(2)]
        p_t = [m1.sb(f"b_p{i}", [128, F]) for i in range(3)]
        sc_t = [m1.sb(f"b_sc{i}", [128, F], BF16) for i in range(3)]
        yg_t = [m1.sb(f"b_yg{i}", [128, F], BF16) for i in range(2)]
        yf = m1.sb("b_yf", [128, F])
        rcol = m1.sb("b_rcol", [128, 1])
        psE = [m1.ps(f"b_psE{i}", [128, 512]) for i in range(3)]
        psY = m1.ps("b_psY", [128, 2048])
        psR = m1.ps("b_psR", [128, 512])

        S.dma("gpsimd", e8[:, :], ios["e8sel"], writes=[e8])
        S.dma("gpsimd", ws[:, :], ios["wsel"], writes=[ws])
        S.dma("gpsimd", e16[:, :], ios["e16sel"], writes=[e16])
        S.dma("gpsimd", alog[:, :], ios["alog_lanes"], writes=[alog])
        S.dma("gpsimd", dcol[:, :], ios["d_col"], writes=[dcol])
        S.dma("gpsimd", oh4[:, :], ios["oh4"], writes=[oh4])
        S.dma("sync", fqg.ten.ap().rearrange("p (j two c) -> p j two c", j=4, two=2),
              ios["fq_g"].rearrange("j two p c -> p j two c"), writes=[fqg])
        S.dma("sync", bc_b[:, :], ios["bc"], writes=[bc_b])
        S.step("scalar", lambda e: e.activation(alan.ten[:, :], alog.ten[:, :], AF.Exp),
               reads=[alog], writes=[alan])
        S.step("vector", lambda e: e.tensor_scalar(alan.ten[:, :], alan.ten[:, :],
                                                   -1.0, None, ALU.mult),
               reads=[alan], writes=[alan])
        for sub in range(4):
            ps = psE[sub % 3]
            S.mm_group([(e16[0:32, 128:256],
                         bc_b[0:32, sub * 512:(sub + 1) * 512])],
                       ps, reads=[e16, bc_b])
            S.step("scalar", lambda e, sub=sub, ps=ps: e.copy(
                crep.ten[:, sub * 512:(sub + 1) * 512], ps.ten[:, :]),
                reads=[ps], writes=[crep])
        S.step("vector", lambda e: e.memset(acc.ten[:, :], 0.0), writes=[acc])
        S.step("vector", lambda e: e.memset(sel.ten[:, :], 0.0), writes=[sel])
        for k in range(4):
            if k > 0:
                fj = fqg[:, (k - 1) * 512:(k - 1) * 512 + 256]
                qj = fqg[:, (k - 1) * 512 + 256:(k - 1) * 512 + 512]
                S.step("vector", lambda e, qj=qj: e.tensor_mul(
                    acc.ten[:, :], acc.ten[:, :], qj),
                    reads=[fqg], writes=[acc])
                S.step("vector", lambda e, fj=fj: e.tensor_add(
                    acc.ten[:, :], acc.ten[:, :], fj),
                    reads=[fqg], writes=[acc])
            S.step("vector", lambda e, k=k: e.tensor_scalar(
                tmp.ten[:, :], acc.ten[:, :], oh4.ten[:, k:k + 1], None, ALU.mult),
                reads=[acc, oh4], writes=[tmp])
            S.step("vector", lambda e: e.tensor_add(
                sel.ten[:, :], sel.ten[:, :], tmp.ten[:, :]),
                reads=[tmp], writes=[sel])
        fj = fqg[:, 3 * 512:3 * 512 + 256]
        qj = fqg[:, 3 * 512 + 256:4 * 512]
        S.step("vector", lambda e, qj=qj: e.tensor_mul(
            sfin.ten[:, :], acc.ten[:, :], qj),
            reads=[acc, fqg], writes=[sfin])
        S.step("vector", lambda e, fj=fj: e.tensor_add(
            sfin.ten[:, :], sfin.ten[:, :], fj),
            reads=[fqg], writes=[sfin])
        S.step("vector", lambda e: e.tensor_copy(sfbf.ten[:, :], sfin.ten[:, :]),
               reads=[sfin], writes=[sfbf])

        for T in range(NTILE):
            cd = cd_t[T % 2]
            yb = y_t[T % 2]
            sres = sres_t[T % 2]
            yg = yg_t[T % 2]
            S.dma("sync", cd[:, :], ios["cd"][T * 128:(T + 1) * 128, :], writes=[cd])
            S.dma("sync", yb[:, :], ios["y_local"][T * 128:(T + 1) * 128, :], writes=[yb])
            S.dma("sync", sres[:, :], ios["sres"][T * 128:(T + 1) * 128, :], writes=[sres])
            ystart = [False] * 4

            def yemit(g, sc_sb):
                for sub in range(4):
                    wl = []
                    if not ystart[sub]:
                        if psY.wr is not None:
                            wl.append(psY.wr)
                        wl.extend(psY.rds)
                    if sc_sb.wr is not None:
                        wl.append(sc_sb.wr)
                    if ws.wr is not None:
                        wl.append(ws.wr)
                    toky = S._record(
                        "tensor",
                        lambda e, g=g, sub=sub, s=(not ystart[sub]), sc_sb=sc_sb,
                               last=(g == NG - 1): e.matmul(
                            psY.ten[:, sub * 512:(sub + 1) * 512],
                            ws.ten[:, g * 128:(g + 1) * 128],
                            sc_sb.ten[:, sub * 512:(sub + 1) * 512],
                            start=s, stop=last, skip_group_check=True),
                        wl, "pe", 1)
                    ystart[sub] = True
                    sc_sb.rds.append(toky)
                    ws.rds.append(toky)
                    if g == NG - 1 and sub == 3:
                        psY.wr = toky
                        psY.rds = []

            pend = []
            for g in range(NG):
                col = T * NG + g
                p_sb = p_t[g % 3]
                sc_sb = sc_t[g % 3]
                for sub in range(4):
                    ps = psE[sub % 3]
                    S.mm_group([(e8[:, g * 128:(g + 1) * 128],
                                 cd[:, sub * 512:(sub + 1) * 512])],
                               ps, reads=[e8, cd])
                    S.step("scalar", lambda e, sub=sub, ps=ps, col=col, p_sb=p_sb: e.activation(
                        p_sb.ten[:, sub * 512:(sub + 1) * 512], ps.ten[:, :],
                        AF.Exp, scale=alan.ten[:, col:col + 1]),
                        reads=[ps, alan], writes=[p_sb])
                if len(pend) >= 2:
                    yemit(*pend.pop(0))
                S.step("vector", lambda e, col=col, p_sb=p_sb: e.tensor_scalar(
                    p_sb.ten[:, :], p_sb.ten[:, :], sel.ten[:, col:col + 1],
                    None, ALU.mult),
                    reads=[sel], writes=[p_sb])
                S.step("vector", lambda e, p_sb=p_sb, sc_sb=sc_sb: e.tensor_mul(
                    sc_sb.ten[:, :], p_sb.ten[:, :], crep.ten[:, :]),
                    reads=[p_sb, crep], writes=[sc_sb])
                pend.append((g, sc_sb))
            for pp in pend:
                yemit(*pp)
            items = [(ws[:, g * 128:(g + 1) * 128],
                      sfbf[:, (T * NG + g):(T * NG + g) + 1]) for g in range(NG)]
            S.mm_group(items, psR, reads=[ws, sfbf],
                       psum_slice=(slice(None), slice(0, 1)))
            S.step("vector", lambda e, T=T: e.tensor_mul(
                rcol.ten[:, 0:1], psR.ten[:, 0:1], dcol.ten[:, T:T + 1]),
                reads=[psR, dcol], writes=[rcol])
            S.step("vector", lambda e, yb=yb: e.tensor_add(
                yf.ten[:, :], psY.ten[:, :], yb.ten[:, :]),
                reads=[psY, yb], writes=[yf])
            S.step("scalar", lambda e: e.add(
                yf.ten[:, :], yf.ten[:, :], rcol.ten[:, 0:1]),
                reads=[rcol], writes=[yf])
            S.step("vector", lambda e, yg=yg, sres=sres: e.tensor_mul(
                yg.ten[:, :], yf.ten[:, :], sres.ten[:, :]),
                reads=[yf, sres], writes=[yg])
            S.dma("sync", ios["ygate"][T * 128:(T + 1) * 128, :], yg[:, :],
                  reads=[yg])


def build_p234():
    nc = bass.Bass("TRN2", target_bir_lowering=False, debug=False)
    ios = {"const": {}}
    for name, shape, dt in CONST_SPECS:
        ios["const"][name] = nc.dram_tensor("c_" + name, shape, dt, kind="ExternalInput").ap()
    ios["x"] = nc.dram_tensor("x", [TOK, D], F32, kind="ExternalInput").ap()
    ios["ao"] = nc.dram_tensor("ao", [NCORES, 128, TOK], BF16, kind="ExternalInput").ap()
    ios["ao_h"] = nc.dram_tensor("ao_h", [NCORES, 128, 3], BF16, kind="ExternalInput").ap()
    ios["x_h"] = nc.dram_tensor("x_h", [3, D], F32, kind="ExternalInput").ap()
    ios["hmask"] = nc.dram_tensor("hmask", [3, 1], F32, kind="ExternalInput").ap()
    ios["wo"] = nc.dram_tensor("wo", [D, D], F32, kind="ExternalInput").ap()
    ios["M"] = nc.dram_tensor("M", [D, D], F32, kind="ExternalInput").ap()
    ios["ffn_bias"] = nc.dram_tensor("ffn_bias", [D], F32, kind="ExternalInput").ap()
    ios["x2"] = nc.dram_tensor("x2", [TOK, D], F32, kind="ExternalOutput").ap()
    ios["W_in"] = nc.dram_tensor("W_in", [D, 2 * D_IN], F32, kind="ExternalInput").ap()
    ios["conv_w"] = nc.dram_tensor("conv_w", [4, D_IN, D_IN], F32, kind="ExternalInput").ap()
    ios["conv_b"] = nc.dram_tensor("conv_b", [D_IN], F32, kind="ExternalInput").ap()
    ios["W_x"] = nc.dram_tensor("W_x", [D_IN, 96], F32, kind="ExternalInput").ap()
    ios["W_dt"] = nc.dram_tensor("W_dt", [DT_RANK, D_IN], F32, kind="ExternalInput").ap()
    ios["b_dt"] = nc.dram_tensor("b_dt", [D_IN], F32, kind="ExternalInput").ap()
    ios["e8sel"] = nc.dram_tensor("e8sel", [128, 16 * 128], BF16, kind="ExternalInput").ap()
    ios["wsel"] = nc.dram_tensor("wsel", [128, 16 * 128], BF16, kind="ExternalInput").ap()
    ios["e16sel"] = nc.dram_tensor("e16sel", [32, 256], BF16, kind="ExternalInput").ap()
    ios["alog_lanes"] = nc.dram_tensor("alog_lanes", [128, 256], F32, kind="ExternalInput").ap()
    ios["sres"] = nc.dram_tensor("sres", [D_IN, TOK], BF16, kind="ExternalOutput").ap()
    ios["bc"] = nc.dram_tensor("bc", [32, TOK], BF16, kind="ExternalOutput").ap()
    ios["y_local"] = nc.dram_tensor("y_local", [D_IN, TOK], BF16, kind="ExternalOutput").ap()
    ios["cd"] = nc.dram_tensor("cd", [D_IN, TOK], BF16, kind="ExternalOutput").ap()
    ios["fq"] = nc.dram_tensor("fq", [2, 128, 256], F32, kind="ExternalOutput").ap()
    ios["dlt"] = nc.dram_tensor("dlt", [D_IN, TOK], BF16, kind="Internal").ap()
    ios["du"] = nc.dram_tensor("du", [D_IN, TOK], BF16, kind="Internal").ap()
    ios["halo_scr"] = nc.dram_tensor("halo_scr", [3, D], F32, kind="Internal").ap()
    ios["u_scr"] = nc.dram_tensor("u_scr", [D_IN, TOK], BF16, kind="Internal").ap()
    with contextlib.ExitStack() as ctx:
        mem = Mem(nc, ctx)
        S = Sched(nc)
        emit_p2(S, mem, ios)
        S.set_fence()
        emit_p3(S, mem, ios)
        emit_p4a(S, mem, ios)
        S.step("gpsimd", lambda e: e.nop(), waits=S.dma_barrier(), sem=None)
        block = ctx.enter_context(nc.Block())
        S.build(block)
    S.close()
    return nc


def build_p45():
    nc = bass.Bass("TRN2", target_bir_lowering=False, debug=False)
    ios = {"const": {}}
    for name, shape, dt in CONST_SPECS:
        ios["const"][name] = nc.dram_tensor("c_" + name, shape, dt, kind="ExternalInput").ap()
    ios["x2"] = nc.dram_tensor("x2", [TOK, D], F32, kind="ExternalInput").ap()
    ios["fq_g"] = nc.dram_tensor("fq_g", [4, 2, 128, 256], F32, kind="ExternalInput").ap()
    ios["oh4"] = nc.dram_tensor("oh4", [128, 4], F32, kind="ExternalInput").ap()
    ios["cd"] = nc.dram_tensor("cd", [D_IN, TOK], BF16, kind="ExternalInput").ap()
    ios["y_local"] = nc.dram_tensor("y_local", [D_IN, TOK], BF16, kind="ExternalInput").ap()
    ios["sres"] = nc.dram_tensor("sres", [D_IN, TOK], BF16, kind="ExternalInput").ap()
    ios["bc"] = nc.dram_tensor("bc", [32, TOK], BF16, kind="ExternalInput").ap()
    ios["e8sel"] = nc.dram_tensor("e8sel", [128, 16 * 128], BF16, kind="ExternalInput").ap()
    ios["wsel"] = nc.dram_tensor("wsel", [128, 16 * 128], BF16, kind="ExternalInput").ap()
    ios["e16sel"] = nc.dram_tensor("e16sel", [32, 256], BF16, kind="ExternalInput").ap()
    ios["alog_lanes"] = nc.dram_tensor("alog_lanes", [128, 256], F32, kind="ExternalInput").ap()
    ios["d_col"] = nc.dram_tensor("d_col", [128, NTILE], F32, kind="ExternalInput").ap()
    ios["W_out"] = nc.dram_tensor("W_out", [D_IN, D], F32, kind="ExternalInput").ap()
    ios["router_w"] = nc.dram_tensor("router_w", [16, D], F32, kind="ExternalInput").ap()
    ios["corr_bias"] = nc.dram_tensor("corr_bias", [16], F32, kind="ExternalInput").ap()
    ios["gate_w"] = nc.dram_tensor("gate_w", [NEXP, D, H_EXP], F32, kind="ExternalInput").ap()
    ios["up_w"] = nc.dram_tensor("up_w", [NEXP, D, H_EXP], F32, kind="ExternalInput").ap()
    ios["down_w"] = nc.dram_tensor("down_w", [NEXP, H_EXP, D], F32, kind="ExternalInput").ap()
    ios["shared_gate_w"] = nc.dram_tensor("shared_gate_w", [HSH, D], F32, kind="ExternalInput").ap()
    ios["shared_up_w"] = nc.dram_tensor("shared_up_w", [HSH, D], F32, kind="ExternalInput").ap()
    ios["shared_down_w"] = nc.dram_tensor("shared_down_w", [D, HSH], F32, kind="ExternalInput").ap()
    ios["esel"] = nc.dram_tensor("esel", [16, 16 * 128], BF16, kind="ExternalInput").ap()
    ios["out"] = nc.dram_tensor("out", [TOK, D], F32, kind="ExternalOutput").ap()
    ygate_t = nc.dram_tensor("ygate", [D_IN, TOK], BF16, kind="Internal")
    ios["ygate"] = ygate_t.ap()
    with contextlib.ExitStack() as ctx:
        mem = Mem(nc, ctx)
        S = Sched(nc)
        emit_p4b(S, mem, ios)
        ios["yg_g"] = ygate_t.ap().rearrange("(j p) t -> j p t", j=NCORES)
        S.set_fence()
        emit_p5(S, mem, ios)
        S.step("gpsimd", lambda e: e.nop(), waits=S.dma_barrier(), sem=None)
        block = ctx.enter_context(nc.Block())
        S.build(block)
    S.close()
    return nc



def kernel(**inputs):
    import ml_dtypes
    x_full = np.ascontiguousarray(np.asarray(inputs["L0_attn_x"], np.float32)
                                  .reshape(BATCH * T_FULL, D))
    consts = [build_constants(c) for c in range(NCORES)]
    e8, ws, e16 = p4_host_consts()
    esel = p5_host_consts()

    def cmaps(extra_fn):
        maps = []
        for c in range(NCORES):
            m = {"c_" + k: v for k, v in consts[c].items()}
            m.update(extra_fn(c))
            maps.append(m)
        return maps

    nc = build_p0()
    r0 = run_bass_kernel_spmd(nc, cmaps(lambda c: {
        "x": np.ascontiguousarray(x_full[c * TOK:(c + 1) * TOK]),
        "wq": inputs["L0_attn_wq"], "wk": inputs["L0_attn_wk"],
        "wv": inputs["L0_attn_wv"],
        "w_freq_real": inputs["L0_ffn_w_freq_real"],
        "w_freq_imag": inputs["L0_ffn_w_freq_imag"]}),
        core_ids=list(range(NCORES))).results
    M = np.concatenate([np.asarray(r0[c]["m_slice"]) for c in range(NCORES)], 0)

    nc = build_p1()
    def p1_in(c):
        qg = np.stack([np.asarray(r0[j]["q_a2a"]).reshape(NCORES, 128, TOK)[c]
                       for j in range(NCORES)])
        kg = np.stack([np.asarray(r0[j]["k_a2a"]).reshape(NCORES, 128, TOK)[c]
                       for j in range(NCORES)])
        vg = np.stack([np.asarray(r0[j]["v_a2a"]).reshape(NCORES, TOK, 128)[c]
                       for j in range(NCORES)])
        return {"qg": qg, "kg": kg, "vg": vg}
    r1 = run_bass_kernel_spmd(nc, cmaps(p1_in), core_ids=list(range(NCORES))).results

    A_log = np.asarray(inputs["L1_attn_A_log"], np.float32)
    alog_lanes = np.ascontiguousarray(
        A_log.reshape(16, 16, 8, 16).transpose(2, 3, 0, 1).reshape(128, 256))
    d_col = np.ascontiguousarray(
        np.asarray(inputs["L1_attn_D"], np.float32).reshape(NTILE, 128).T)
    nc = build_p234()
    def p234_in(c):
        ao = np.stack([np.asarray(r1[j]["ao_a2a"]).reshape(NCORES, 128, TOK)[c]
                       for j in range(NCORES)])
        if c % 4 != 0:
            ao_h = np.stack([np.asarray(r1[j]["ao_a2a"])
                             .reshape(NCORES, 128, TOK)[c - 1][:, -3:]
                             for j in range(NCORES)])
            x_h = np.ascontiguousarray(x_full[c * TOK - 3:c * TOK])
            hmask = np.ones((3, 1), np.float32)
        else:
            ao_h = np.zeros((NCORES, 128, 3), r1[0]["ao_a2a"].dtype)
            x_h = np.zeros((3, D), np.float32)
            hmask = np.zeros((3, 1), np.float32)
        return {"x": np.ascontiguousarray(x_full[c * TOK:(c + 1) * TOK]),
                "ao": ao, "ao_h": np.ascontiguousarray(ao_h), "x_h": x_h,
                "hmask": hmask, "wo": inputs["L0_attn_wo"], "M": M,
                "ffn_bias": inputs["L0_ffn_bias"],
                "W_in": inputs["L1_attn_W_in"], "conv_w": inputs["L1_attn_conv_w"],
                "conv_b": inputs["L1_attn_conv_b"], "W_x": inputs["L1_attn_W_x"],
                "W_dt": inputs["L1_attn_W_dt"], "b_dt": inputs["L1_attn_b_dt"],
                "e8sel": e8, "wsel": ws, "e16sel": e16,
                "alog_lanes": alog_lanes}
    r34 = run_bass_kernel_spmd(nc, cmaps(p234_in), core_ids=list(range(NCORES))).results
    r2 = r34

    fq_all = [np.asarray(r34[j]["fq"]) for j in range(NCORES)]
    nc = build_p45()
    def p45_in(c):
        g0 = 4 * (c // 4)
        oh4 = np.zeros((128, 4), np.float32)
        oh4[:, c % 4] = 1.0
        return {"x2": np.asarray(r2[c]["x2"]),
                "fq_g": np.stack(fq_all[g0:g0 + 4]),
                "oh4": oh4,
                "cd": np.asarray(r34[c]["cd"]),
                "y_local": np.asarray(r34[c]["y_local"]),
                "sres": np.asarray(r34[c]["sres"]),
                "bc": np.asarray(r34[c]["bc"]),
                "e8sel": e8, "wsel": ws, "e16sel": e16,
                "alog_lanes": alog_lanes, "d_col": d_col,
                "W_out": inputs["L1_attn_W_out"],
                "router_w": inputs["L1_ffn_router_w"],
                "corr_bias": inputs["L1_ffn_correction_bias"],
                "gate_w": inputs["L1_ffn_gate_w"], "up_w": inputs["L1_ffn_up_w"],
                "down_w": inputs["L1_ffn_down_w"],
                "shared_gate_w": inputs["L1_ffn_shared_gate_w"],
                "shared_up_w": inputs["L1_ffn_shared_up_w"],
                "shared_down_w": inputs["L1_ffn_shared_down_w"], "esel": esel}
    r5 = run_bass_kernel_spmd(nc, cmaps(p45_in), core_ids=list(range(NCORES))).results

    out = np.concatenate([np.asarray(r5[c]["out"]) for c in range(NCORES)], 0)
    return out.reshape(BATCH, T_FULL, D).astype(np.float32)



# version 10 (final)
# speedup vs baseline: 1.5353x; 1.0004x over previous
import contextlib
import numpy as np
import concourse.bass as bass
import concourse.mybir as mybir
from concourse.bass_utils import run_bass_kernel_spmd
from concourse.aot_env import allow_ici_all_to_all

F32 = mybir.dt.float32
BF16 = mybir.dt.bfloat16
AF = mybir.ActivationFunctionType
ALU = mybir.AluOpType
AXX = mybir.AxisListType.X

NCORES = 8
D = 1024
TOK = 2048
T_FULL = 8192
BATCH = 2
NHEAD = 16
DK = 64
D_IN = 2048
N_SSM = 16
DT_RANK = 64
NEXP = 16
H_EXP = 256
HSH = 2048
EPS = 1e-5
NT = TOK // 128
NCB = D // 128

ENGINES = ("tensor", "scalar", "vector", "gpsimd", "sync")
DMA_SEM = {"sync": "dmaS", "gpsimd": "dmaG", "scalar": "dmaA"}



def build_constants(core):
    c = {}
    c["ident"] = np.eye(128, dtype=np.float32)
    k = np.arange(D, dtype=np.float64)
    ang = 2.0 * np.pi * np.outer(k, k) / D
    c["dftC"] = np.cos(ang).astype(np.float32)
    c["dftS"] = np.sin(ang).astype(np.float32)
    c["dftSneg"] = (-np.sin(ang)).astype(np.float32)
    pos0 = (core % 4) * TOK
    t = np.arange(pos0, pos0 + TOK, dtype=np.float64)
    freqs = 1.0 / (10000.0 ** (np.arange(0, DK, 2, dtype=np.float64) / DK))
    angs = np.outer(freqs, t)
    cos2 = np.repeat(np.cos(angs), 2, axis=0)
    sin2 = np.repeat(np.sin(angs), 2, axis=0)
    sgn = np.tile(np.array([-1.0, 1.0]), DK // 2)[:, None]
    c["ropeCos"] = np.tile(cos2, (2, 1)).astype(np.float32)
    c["ropeSin"] = np.tile(sin2 * sgn, (2, 1)).astype(np.float32)
    for j in range(4):
        kk = np.arange(128)[:, None] + 128 * j
        qq = np.arange(512)[None, :]
        c[f"cmask{j}"] = (kk <= qq).astype(np.float32)
    c["ones1x128"] = np.ones((1, 128), dtype=np.float32)
    c["epscol"] = np.full((128, 1), EPS, dtype=np.float32)
    c["onecol"] = np.ones((128, 1), dtype=np.float32)
    r0 = core * 128
    c["dftC_r"] = np.ascontiguousarray(c["dftC"][:, r0:r0 + 128])
    c["dftS_r"] = np.ascontiguousarray(c["dftS"][:, r0:r0 + 128])
    c["dftSn_r"] = np.ascontiguousarray(c["dftSneg"][:, r0:r0 + 128])
    return c

CONST_SPECS = [
    ("ident", [128, 128], F32),
    ("dftC", [D, D], F32),
    ("dftS", [D, D], F32),
    ("dftSneg", [D, D], F32),
    ("ropeCos", [128, TOK], F32),
    ("ropeSin", [128, TOK], F32),
    ("cmask0", [128, 512], F32),
    ("cmask1", [128, 512], F32),
    ("cmask2", [128, 512], F32),
    ("cmask3", [128, 512], F32),
    ("ones1x128", [1, 128], F32),
    ("epscol", [128, 1], F32),
    ("onecol", [128, 1], F32),
    ("dftC_r", [D, 128], F32),
    ("dftS_r", [D, 128], F32),
    ("dftSn_r", [D, 128], F32),
]



class Buf:
    __slots__ = ("ten", "wr", "rds", "name")

    def __init__(self, ten, name):
        self.ten = ten
        self.name = name
        self.wr = None
        self.rds = []

    def __getitem__(self, idx):
        return self.ten[idx]

    @property
    def ap(self):
        return self.ten


class Sched:

    def __init__(self, nc):
        self.nc = nc
        self.steps = {e: [] for e in ENGINES}
        self.sem_handles = {}
        self.counts = {}
        self._ctx = contextlib.ExitStack()

    def sem(self, name):
        if name not in self.sem_handles:
            self.sem_handles[name] = self._ctx.enter_context(self.nc.semaphore(name))
            self.counts[name] = 0
        return self.sem_handles[name]

    fence = None

    def all_tokens(self):
        return [(s, v) for s, v in self.counts.items() if v > 0]

    def set_fence(self):
        self.fence = self.all_tokens()

    def _record(self, engine, fn, waits, sem, inc):
        wmap = {}
        for w in waits:
            if w is None:
                continue
            s, v = w
            if s not in wmap or wmap[s] < v:
                wmap[s] = v
        wl = sorted(wmap.items())
        tok = None
        if sem is not None:
            self.sem(sem)
            self.counts[sem] += inc
            tok = (sem, self.counts[sem])
        self.steps[engine].append((wl, fn, tok, inc if sem else 0))
        return tok

    def step(self, engine, fn, reads=(), writes=(), waits=(), sem="x", inc=1):
        if sem == "x":
            sem = {"tensor": "pe", "scalar": "se", "vector": "ve",
                   "gpsimd": "ge", "sync": "sy"}[engine]
        wl = list(waits)
        for b in reads:
            if b.wr is not None:
                wl.append(b.wr)
        for b in writes:
            if b.wr is None and b.rds == [] and self.fence:
                wl.extend(self.fence)
            if b.wr is not None:
                wl.append(b.wr)
            wl.extend(b.rds)
        tok = self._record(engine, fn, wl, sem, inc)
        for b in reads:
            if tok is not None:
                b.rds.append(tok)
        for b in writes:
            b.wr = tok
            b.rds = []
        return tok

    DMA_LANES = 4

    def dma(self, queue, out, in_, reads=(), writes=(), waits=()):
        if not hasattr(self, "dma_lane"):
            self.dma_lane = {}
        lane = self.dma_lane.get(queue, 0)
        self.dma_lane[queue] = (lane + 1) % self.DMA_LANES
        sname = DMA_SEM[queue] + str(lane)
        self.sem(sname)
        prev = self.counts[sname]
        wl = list(waits) + ([(sname, prev)] if prev else [])
        return self.step(
            queue, lambda e, o=out, i=in_: e.dma_start(out=o, in_=i),
            reads=reads, writes=writes, waits=wl, sem=sname, inc=16)

    def dma_barrier(self):
        toks = []
        for q in DMA_SEM.values():
            for lane in range(self.DMA_LANES):
                sname = q + str(lane)
                if self.counts.get(sname):
                    toks.append((sname, self.counts[sname]))
        return toks

    def mm_group(self, items, out_buf, reads=(), waits=(), psum_slice=None):
        n = len(items)
        wl = list(waits)
        for b in reads:
            if b.wr is not None:
                wl.append(b.wr)
        if out_buf.wr is None and out_buf.rds == [] and self.fence:
            wl.extend(self.fence)
        if out_buf.wr is not None:
            wl.append(out_buf.wr)
        wl.extend(out_buf.rds)
        tok = None
        for i, (lhs, rhs) in enumerate(items):
            dst = out_buf.ten[psum_slice] if psum_slice is not None else out_buf.ten[:, :]
            tok = self._record(
                "tensor",
                lambda e, l=lhs, r=rhs, d=dst, s=(i == 0), p=(i == n - 1): e.matmul(
                    d, l, r, start=s, stop=p),
                wl if i == 0 else (), "pe" if i == n - 1 else None, 1)
        for b in reads:
            b.rds.append(tok)
        out_buf.wr = tok
        out_buf.rds = []
        return tok

    def barrier_token(self, name):
        self.sem(name)
        return (name, self.counts[name])

    def build(self, block):
        for engine in ENGINES:
            steps = self.steps[engine]
            if not steps:
                continue
            dec = getattr(block, engine)

            def _make(_steps):
                def _body(eng):
                    for wl, fn, tok, inc in _steps:
                        for (sname, val) in wl:
                            eng.wait_ge(self.sem_handles[sname], val)
                        ins = fn(eng)
                        if tok is not None:
                            ins.then_inc(self.sem_handles[tok[0]], inc)
                return _body
            dec(_make(steps))

    def close(self):
        self._ctx.close()


class Mem:

    def __init__(self, nc, ctx):
        self.nc = nc
        self.ctx = ctx

    def sb(self, name, shape, dt=F32):
        return Buf(self.ctx.enter_context(self.nc.sbuf_tensor(name, shape, dt)), name)

    def ps(self, name, shape, dt=F32):
        return Buf(self.ctx.enter_context(self.nc.psum_tensor(name, shape, dt)), name)

    @contextlib.contextmanager
    def scope(self, S=None):
        with contextlib.ExitStack() as es:
            yield Mem(self.nc, es)
        if S is not None:
            S.set_fence()


def ln_emit(S, mem, x_ap, out_ap, stat, sq, epscol, reads, writes, extra_waits=(), rows=128):
    st = stat.ten[0:rows, :]
    S.step("scalar", lambda e: e.activation(sq.ten[0:rows, :], x_ap, AF.Square),
           reads=reads, writes=[sq], waits=extra_waits)
    S.step("vector", lambda e: e.reduce_sum(st[:, 0:1], x_ap, axis=AXX),
           reads=reads, writes=[stat], waits=extra_waits)
    S.step("vector", lambda e: e.reduce_sum(st[:, 1:2], sq.ten[0:rows, :], axis=AXX),
           reads=[sq], writes=[stat])
    S.step("vector", lambda e: e.tensor_scalar(st[:, 2:4], st[:, 0:2], 1.0 / D, None, ALU.mult),
           reads=[stat], writes=[stat])
    S.step("vector", lambda e: e.tensor_mul(st[:, 4:5], st[:, 2:3], st[:, 2:3]),
           reads=[stat], writes=[stat])
    S.step("vector", lambda e: e.tensor_sub(st[:, 5:6], st[:, 3:4], st[:, 4:5]),
           reads=[stat], writes=[stat])
    S.step("scalar", lambda e: e.activation(st[:, 6:7], st[:, 5:6], AF.Sqrt, bias=epscol.ten[0:rows, 0:1]),
           reads=[stat, epscol], writes=[stat])
    S.step("vector", lambda e: e.reciprocal(st[:, 7:8], st[:, 6:7]),
           reads=[stat], writes=[stat])
    return S.step("vector", lambda e: e.tensor_scalar(
        out_ap, x_ap, st[:, 2:3], st[:, 7:8], ALU.subtract, ALU.mult),
        reads=reads + [stat], writes=writes)


def transpose_tiles(S, mem, src_buf, src_col0, dst_buf, dst_col0, nrow_tiles,
                    ncol_tiles, src_stride, dst_stride, ident, pst, dt_bytes=2):
    for r in range(nrow_tiles):
        for c in range(ncol_tiles):
            p = pst[(r * ncol_tiles + c) % len(pst)]
            s_ap = src_buf[:, src_col0 + r * src_stride + c * 128:
                           src_col0 + r * src_stride + c * 128 + 128]
            d_ap = dst_buf[:, dst_col0 + c * dst_stride + r * 128:
                           dst_col0 + c * dst_stride + r * 128 + 128]
            S.step("tensor", lambda e, s=s_ap, p=p: e.transpose(p.ten[:, :], s, ident.ten[:, :]),
                   reads=[src_buf, ident], writes=[p])
            S.step("scalar", lambda e, d=d_ap, p=p: e.copy(d, p.ten[:, :]),
                   reads=[p], writes=[dst_buf])



def emit_p0(S, mem, ios):
    cst = ios["const"]
    with mem.scope(S) as m1:
        xlnT = m1.sb("p0_xlnT", [128, NCB * TOK], BF16)
        x_t = [m1.sb(f"p0_x{i}", [128, D]) for i in range(2)]
        sq = m1.sb("p0_sq", [128, D])
        xln_f = m1.sb("p0_xlnf", [128, D])
        xln_bf = m1.sb("p0_xlnbf", [128, D], BF16)
        stat = m1.sb("p0_stat", [128, 8])
        rope_c = m1.sb("p0_ropec", [128, TOK], BF16)
        rope_s = m1.sb("p0_ropes", [128, TOK], BF16)
        w_s = m1.sb("p0_w", [128, NCB * D], BF16)
        wT = m1.sb("p0_wT", [128, NCB * D], BF16)
        rot_t = [m1.sb(f"p0_rot{i}", [128, TOK], BF16) for i in range(2)]
        v_t = [m1.sb(f"p0_v{i}", [128, D], BF16) for i in range(2)]
        qtmp = m1.sb("p0_qtmp", [128, 512], BF16)
        qsw = m1.sb("p0_qsw", [128, 512], BF16)
        qm1 = m1.sb("p0_qm1", [128, 512], BF16)
        ident_b = m1.sb("p0_identb", [128, 128], BF16)
        psA = m1.ps("p0_psA", [128, 512])
        psB = m1.ps("p0_psB", [128, 512])
        psT = [m1.ps("p0_psT1", [128, 128], BF16), m1.ps("p0_psT2", [128, 128], BF16)]

        S.dma("gpsimd", ident_b[:, :], cst["ident"], writes=[ident_b])
        epscol = m1.sb("p0_eps", [128, 1])
        S.dma("gpsimd", epscol[:, :], cst["epscol"], writes=[epscol])
        S.dma("gpsimd", rope_c[:, :], cst["ropeCos"], writes=[rope_c])
        S.dma("gpsimd", rope_s[:, :], cst["ropeSin"], writes=[rope_s])

        for ti in range(NT):
            xt = x_t[ti % 2]
            S.dma("sync", xt[:, :], ios["x"][ti * 128:(ti + 1) * 128, :], writes=[xt])
            ln_emit(S, m1, xt[:, :], xln_f[:, :], stat, sq, epscol, reads=[xt], writes=[xln_f])
            S.step("vector", lambda e: e.tensor_copy(xln_bf.ten[:, :], xln_f.ten[:, :]),
                   reads=[xln_f], writes=[xln_bf])
            for cb in range(NCB):
                p = psT[cb % 2]
                S.step("tensor", lambda e, cb=cb, p=p: e.transpose(
                    p.ten[:, :], xln_bf.ten[:, cb * 128:(cb + 1) * 128], ident_b.ten[:, :]),
                    reads=[xln_bf, ident_b], writes=[p])
                dst = xlnT[:, cb * TOK + ti * 128: cb * TOK + (ti + 1) * 128]
                S.step("scalar", lambda e, dst=dst, p=p: e.copy(dst, p.ten[:, :]),
                       reads=[p], writes=[xlnT])

        for wi_, wname in enumerate(("wq", "wk", "wv")):
            S.dma("gpsimd", w_s.ten.ap().rearrange("p (n c) -> p n c", n=NCB),
                  ios[wname].rearrange("(n p) c -> p n c", p=128), writes=[w_s])
            transpose_tiles(S, m1, w_s, 0, wT, 0, NCB, NCB, D, D, ident_b, psT)
            if wname != "wv":
                rot_out = ios["q_a2a"] if wname == "wq" else ios["k_a2a"]
                for dqb in range(NCB):
                    rt = rot_t[dqb % 2]
                    for tc in range(4):
                        psum = psA if tc % 2 == 0 else psB
                        items = [(wT[:, cb * D + dqb * 128: cb * D + (dqb + 1) * 128],
                                  xlnT[:, cb * TOK + tc * 512: cb * TOK + (tc + 1) * 512])
                                 for cb in range(NCB)]
                        S.mm_group(items, psum, reads=[wT, xlnT])
                        sl = slice(tc * 512, (tc + 1) * 512)
                        S.step("scalar", lambda e, psum=psum: e.copy(qtmp.ten[:, :], psum.ten[:, :]),
                               reads=[psum], writes=[qtmp])
                        S.step("vector", lambda e: e.stream_shuffle(
                            qsw.ten[:, :], qtmp.ten[:, :], [i ^ 1 for i in range(32)]),
                            reads=[qtmp], writes=[qsw])
                        S.step("vector", lambda e, sl=sl: e.tensor_mul(
                            qm1.ten[:, :], qtmp.ten[:, :], rope_c.ten[:, sl]),
                            reads=[qtmp, rope_c], writes=[qm1])
                        S.step("vector", lambda e, sl=sl: e.tensor_mul(
                            qsw.ten[:, :], qsw.ten[:, :], rope_s.ten[:, sl]),
                            reads=[qsw, rope_s], writes=[qsw])
                        S.step("vector", lambda e, rt=rt, sl=sl: e.tensor_add(
                            rt.ten[:, sl], qm1.ten[:, :], qsw.ten[:, :]),
                            reads=[qm1, qsw], writes=[rt])
                    S.dma("sync", rot_out[dqb, :, :], rt[:, :], reads=[rt])
            else:
                for ti in range(NT):
                    vt = v_t[ti % 2]
                    for hc in range(2):
                        psum = psA if hc == 0 else psB
                        items = [(xlnT[:, cb * TOK + ti * 128: cb * TOK + (ti + 1) * 128],
                                  wT[:, cb * D + hc * 512: cb * D + (hc + 1) * 512])
                                 for cb in range(NCB)]
                        S.mm_group(items, psum, reads=[xlnT, wT])
                        dst = vt[:, hc * 512:(hc + 1) * 512]
                        S.step("scalar", lambda e, dst=dst, psum=psum: e.copy(dst, psum.ten[:, :]),
                               reads=[psum], writes=[vt])
                    S.dma("sync",
                          ios["v_a2a"][:, ti * 128:(ti + 1) * 128, :].rearrange("j p c -> p j c"),
                          vt.ten.ap().rearrange("p (j c) -> p j c", j=NCORES), reads=[vt])
        t_qkv_done = S.dma_barrier()

    with mem.scope(S) as m2:
        Cs = m2.sb("p0_C", [128, NCB * D])
        Sn = m2.sb("p0_Sn", [128, NCB * D])
        Crs = m2.sb("p0_Cr", [128, NCB * 128])
        Srs = m2.sb("p0_Sr", [128, NCB * 128])
        Snrs = m2.sb("p0_Snr", [128, NCB * 128])
        wr_s = m2.sb("p0_wr", [128, NCB * D])
        wi_s = m2.sb("p0_wi", [128, NCB * D])
        PTs = m2.sb("p0_PT", [128, NCB * 128])
        QTs = m2.sb("p0_QT", [128, NCB * 128])
        m_out = m2.sb("p0_m", [128, D])
        psA = m2.ps("p0_psA2", [128, 512])
        psB = m2.ps("p0_psB2", [128, 512])
        S.dma("sync", Cs.ten.ap().rearrange("p (n o) -> p n o", n=NCB),
              cst["dftC"].rearrange("(n p) o -> p n o", p=128),
              writes=[Cs], waits=t_qkv_done)
        S.dma("sync", Sn.ten.ap().rearrange("p (n o) -> p n o", n=NCB),
              cst["dftSneg"].rearrange("(n p) o -> p n o", p=128),
              writes=[Sn], waits=t_qkv_done)
        S.dma("sync", Crs.ten.ap().rearrange("p (n o) -> p n o", n=NCB),
              cst["dftC_r"].rearrange("(n p) o -> p n o", p=128),
              writes=[Crs], waits=t_qkv_done)
        S.dma("sync", Srs.ten.ap().rearrange("p (n o) -> p n o", n=NCB),
              cst["dftS_r"].rearrange("(n p) o -> p n o", p=128),
              writes=[Srs], waits=t_qkv_done)
        S.dma("sync", Snrs.ten.ap().rearrange("p (n o) -> p n o", n=NCB),
              cst["dftSn_r"].rearrange("(n p) o -> p n o", p=128),
              writes=[Snrs], waits=t_qkv_done)
        S.dma("gpsimd", wr_s.ten.ap().rearrange("p (n o) -> p n o", n=NCB),
              ios["w_freq_real"].rearrange("(n p) o -> p n o", p=128),
              writes=[wr_s], waits=t_qkv_done)
        S.dma("gpsimd", wi_s.ten.ap().rearrange("p (n o) -> p n o", n=NCB),
              ios["w_freq_imag"].rearrange("(n p) o -> p n o", p=128),
              writes=[wi_s], waits=t_qkv_done)
        for dstT, wa, wb, sb_b in ((PTs, wr_s, wi_s, Srs), (QTs, wi_s, wr_s, Snrs)):
            for kb in range(NCB):
                psum = psA if kb % 2 == 0 else psB
                items = []
                for db in range(NCB):
                    items.append((wa[:, db * D + kb * 128: db * D + (kb + 1) * 128],
                                  Crs[:, db * 128:(db + 1) * 128]))
                    items.append((wb[:, db * D + kb * 128: db * D + (kb + 1) * 128],
                                  sb_b[:, db * 128:(db + 1) * 128]))
                S.mm_group(items, psum, reads=[wa, wb, Crs, sb_b],
                           psum_slice=(slice(None), slice(0, 128)))
                S.step("scalar", lambda e, kb=kb, psum=psum, dstT=dstT: e.copy(
                    dstT[:, kb * 128:(kb + 1) * 128], psum.ten[:, 0:128]),
                    reads=[psum], writes=[dstT])
        for oc in range(2):
            psum = psA if oc == 0 else psB
            items = []
            for kb in range(NCB):
                items.append((PTs[:, kb * 128:(kb + 1) * 128],
                              Cs[:, kb * D + oc * 512: kb * D + (oc + 1) * 512]))
                items.append((QTs[:, kb * 128:(kb + 1) * 128],
                              Sn[:, kb * D + oc * 512: kb * D + (oc + 1) * 512]))
            S.mm_group(items, psum, reads=[PTs, QTs, Cs, Sn])
            S.step("scalar", lambda e, oc=oc, psum=psum: e.mul(
                m_out[:, oc * 512:(oc + 1) * 512], psum.ten[:, :], 1.0 / D),
                reads=[psum], writes=[m_out])
        S.dma("sync", ios["m_slice"], m_out[:, :], reads=[m_out])




def build_p0():
    nc = bass.Bass("TRN2", target_bir_lowering=False, debug=False)
    ios = {"const": {}}
    for name, shape, dt in CONST_SPECS:
        ios["const"][name] = nc.dram_tensor("c_" + name, shape, dt, kind="ExternalInput").ap()
    ios["x"] = nc.dram_tensor("x", [TOK, D], F32, kind="ExternalInput").ap()
    for w in ("wq", "wk", "wv"):
        ios[w] = nc.dram_tensor(w, [D, D], F32, kind="ExternalInput").ap()
    ios["w_freq_real"] = nc.dram_tensor("w_freq_real", [D, D], F32, kind="ExternalInput").ap()
    ios["w_freq_imag"] = nc.dram_tensor("w_freq_imag", [D, D], F32, kind="ExternalInput").ap()
    ios["q_a2a"] = nc.dram_tensor("q_a2a", [NCORES, 128, TOK], BF16, kind="ExternalOutput").ap()
    ios["k_a2a"] = nc.dram_tensor("k_a2a", [NCORES, 128, TOK], BF16, kind="ExternalOutput").ap()
    ios["v_a2a"] = nc.dram_tensor("v_a2a", [NCORES, TOK, 128], BF16, kind="ExternalOutput").ap()
    ios["m_slice"] = nc.dram_tensor("m_slice", [128, D], F32, kind="ExternalOutput").ap()
    with contextlib.ExitStack() as ctx:
        mem = Mem(nc, ctx)
        S = Sched(nc)
        emit_p0(S, mem, ios)
        end_tok = S.dma_barrier()
        S.step("gpsimd", lambda e: e.nop(), waits=end_tok, sem=None)
        block = ctx.enter_context(nc.Block())
        S.build(block)
    S.close()
    return nc





def emit_p1(S, mem, ios, units=4, qcs=16):
    cst = ios["const"]
    NKT = T_FULL // 128
    nbatch = max(1, units // 2)
    with mem.scope(S) as m1:
        qt_h = [m1.sb(f"p1_qt{h}", [64, T_FULL], BF16) for h in range(2)]
        kt_h = [m1.sb(f"p1_kt{h}", [64, T_FULL], BF16) for h in range(2)]
        vs_h = [m1.sb(f"p1_vs{h}", [128, NKT * 65], BF16) for h in range(2)]
        pt = [m1.sb(f"p1_pt{i}", [128, 512], BF16) for i in range(4)]
        masks = [m1.sb(f"p1_mask{j}", [128, 512], BF16) for j in range(4)]
        ones_s = m1.sb("p1_ones", [128, 64])
        den_s = m1.sb("p1_den", [128, 512])
        rec_sb = [m1.sb(f"p1_rec{h}", [64, 512]) for h in range(2)]
        outT = [m1.sb(f"p1_outT{h}", [64, 512], BF16) for h in range(2)]
        psS = [m1.ps(f"p1_psS{i}", [128, 512]) for i in range(4)]
        psO = [m1.ps(f"p1_psO{h}", [128, 512]) for h in range(2)]
        psR = [m1.ps(f"p1_psR{h}", [128, 512]) for h in range(2)]

        for j in range(4):
            S.dma("gpsimd", masks[j][:, :], cst[f"cmask{j}"], writes=[masks[j]])
        S.step("vector", lambda e: e.memset(ones_s.ten[:, :], 1.0), writes=[ones_s])

        for b in range(nbatch):
            for hl in range(2):
                qt = qt_h[hl]
                kt = kt_h[hl]
                S.dma("sync", qt.ten.ap().rearrange("p (s t) -> p s t", s=4),
                      ios["qg"][4 * b:4 * b + 4, 64 * hl:64 * hl + 64, :]
                          .rearrange("s p t -> p s t"),
                      writes=[qt])
                S.dma("sync", kt.ten.ap().rearrange("p (s t) -> p s t", s=4),
                      ios["kg"][4 * b:4 * b + 4, 64 * hl:64 * hl + 64, :]
                          .rearrange("s p t -> p s t"),
                      writes=[kt])
                vs = vs_h[hl]
                S.dma("sync",
                      vs.ten.ap().rearrange("p (n f) -> p n f", f=65)[:, :, 0:64]
                          .rearrange("p (s n) c -> p s n c", s=4),
                      ios["vg"][4 * b:4 * b + 4, :, 64 * hl:64 * hl + 64]
                          .rearrange("s (n p) c -> p s n c", p=128),
                      writes=[vs])
                S.step("vector", lambda e, vs=vs: e.memset(
                    vs.ten.ap().rearrange("p (n f) -> p n f", f=65)[:, :, 64:65], 1.0),
                    reads=[ones_s], writes=[vs])

            for qc in range(qcs):
                q0 = qc * 512
                nk = (q0 + 512) // 128
                first_pv = [True, True]

                def pvemit(kti):
                    for hl in range(2):
                        vs = vs_h[hl]
                        po = psO[hl]
                        p = pt[2 * hl + kti % 2]
                        wl = []
                        if first_pv[hl]:
                            if po.wr is not None:
                                wl.append(po.wr)
                            wl.extend(po.rds)
                            if vs.wr is not None:
                                wl.append(vs.wr)
                        if p.wr is not None:
                            wl.append(p.wr)
                        tokpv = S._record(
                            "tensor",
                            lambda e, kti=kti, p=p, po=po, vs=vs, s=first_pv[hl],
                                   last=(kti == nk - 1): e.matmul(
                                po.ten[0:65, :], vs.ten[:, kti * 65:kti * 65 + 65],
                                p.ten[:, :], start=s, stop=last, skip_group_check=True),
                            wl, "pe", 1)
                        if kti == nk - 1:
                            po.wr = tokpv
                            po.rds = []
                        p.rds.append(tokpv)
                        vs.rds.append(tokpv)
                        first_pv[hl] = False

                for kti in range(nk):
                    for hl in range(2):
                        qt = qt_h[hl]
                        kt = kt_h[hl]
                        ss = psS[2 * hl + kti % 2]
                        S.mm_group([(kt[:, kti * 128:(kti + 1) * 128],
                                     qt[:, q0:q0 + 512])],
                                   ss, reads=[kt, qt])
                    for hl in range(2):
                        ss = psS[2 * hl + kti % 2]
                        p = pt[2 * hl + kti % 2]
                        S.step("scalar", lambda e, ss=ss, p=p: e.activation(
                            p.ten[:, :], ss.ten[:, :], AF.Exp, scale=0.125),
                            reads=[ss], writes=[p])
                        j = kti - q0 // 128
                        if j >= 0:
                            S.step("vector", lambda e, p=p, j=j: e.tensor_mul(
                                p.ten[:, :], p.ten[:, :], masks[j].ten[:, :]),
                                reads=[p, masks[j]], writes=[p])
                    if kti > 0:
                        pvemit(kti - 1)
                pvemit(nk - 1)
                for hl in range(2):
                    po = psO[hl]
                    rec = rec_sb[hl]
                    ot = outT[hl]
                    dr = 64 * hl
                    S.step("scalar", lambda e, po=po, dr=dr: e.copy(
                        den_s.ten[dr + 0:dr + 1, :], po.ten[64:65, :]),
                        reads=[po], writes=[den_s])
                    S.step("vector", lambda e, dr=dr: e.reciprocal(
                        den_s.ten[dr + 0:dr + 1, :], den_s.ten[dr + 0:dr + 1, :]),
                        reads=[den_s], writes=[den_s])
                    S.mm_group([(ones_s[dr:dr + 1, 0:64], den_s[dr:dr + 1, :])],
                               psR[hl], reads=[ones_s, den_s],
                               psum_slice=(slice(0, 64), slice(None)))
                    S.step("scalar", lambda e, rec=rec, hl=hl: e.copy(
                        rec.ten[:, :], psR[hl].ten[0:64, :]),
                        reads=[psR[hl]], writes=[rec])
                    S.step("vector", lambda e, ot=ot, po=po, rec=rec: e.tensor_mul(
                        ot.ten[:, :], po.ten[0:64, :], rec.ten[:, :]),
                        reads=[po, rec], writes=[ot])
                    blk = 4 * b + q0 // TOK
                    S.dma("sync",
                          ios["ao_a2a"][blk, 64 * hl:64 * hl + 64,
                                        (q0 % TOK):(q0 % TOK) + 512],
                          ot[:, :], reads=[ot])


def build_p1(units=4, qcs=16):
    nc = bass.Bass("TRN2", target_bir_lowering=False, debug=False)
    ios = {"const": {}}
    for name, shape, dt in CONST_SPECS:
        ios["const"][name] = nc.dram_tensor("c_" + name, shape, dt, kind="ExternalInput").ap()
    ios["qg"] = nc.dram_tensor("qg", [NCORES, 128, TOK], BF16, kind="ExternalInput").ap()
    ios["kg"] = nc.dram_tensor("kg", [NCORES, 128, TOK], BF16, kind="ExternalInput").ap()
    ios["vg"] = nc.dram_tensor("vg", [NCORES, TOK, 128], BF16, kind="ExternalInput").ap()
    ios["ao_a2a"] = nc.dram_tensor("ao_a2a", [NCORES, 128, TOK], BF16, kind="ExternalOutput").ap()
    with contextlib.ExitStack() as ctx:
        mem = Mem(nc, ctx)
        S = Sched(nc)
        emit_p1(S, mem, ios, units, qcs)
        end_tok = S.dma_barrier()
        S.step("gpsimd", lambda e: e.nop(), waits=end_tok, sem=None)
        block = ctx.enter_context(nc.Block())
        S.build(block)
    S.close()
    return nc



def emit_p2(S, mem, ios):
    cst = ios["const"]
    with mem.scope(S) as m1:
        attnT = m1.sb("p2_attnT", [128, NCB * TOK], BF16)
        w_s = m1.sb("p2_w", [128, NCB * D], BF16)
        woT = m1.sb("p2_woT", [128, NCB * D], BF16)
        m_s = m1.sb("p2_m", [128, NCB * D], BF16)
        bias_s = m1.sb("p2_bias", [1, D], BF16)
        ones_s = m1.sb("p2_ones", [1, 128], BF16)
        ident_b = m1.sb("p2_identb", [128, 128], BF16)
        epscol = m1.sb("p2_eps", [128, 1])
        x_t = [m1.sb(f"p2_x{i}", [128, D]) for i in range(2)]
        x1_t = [m1.sb(f"p2_x1{i}", [128, D]) for i in range(2)]
        x2_t = [m1.sb(f"p2_x2{i}", [128, D]) for i in range(2)]
        xln_f = m1.sb("p2_xlnf", [128, D])
        xln_bf = m1.sb("p2_xlnbf", [128, D], BF16)
        xlnT_t = [m1.sb(f"p2_xlnT{i}", [128, NCB * 128], BF16) for i in range(2)]
        sq = m1.sb("p2_sq", [128, D])
        stat = m1.sb("p2_stat", [128, 8])
        psA = m1.ps("p2_psA", [128, 512])
        psB = m1.ps("p2_psB", [128, 512])
        psT = [m1.ps("p2_psT1", [128, 128], BF16), m1.ps("p2_psT2", [128, 128], BF16)]

        S.dma("gpsimd", ident_b[:, :], cst["ident"], writes=[ident_b])
        S.dma("gpsimd", epscol[:, :], cst["epscol"], writes=[epscol])
        S.dma("gpsimd", ones_s[:, :], cst["ones1x128"], writes=[ones_s])
        S.dma("sync", attnT.ten.ap().rearrange("p (j t) -> p j t", j=NCORES),
              ios["ao"].rearrange("j p t -> p j t"), writes=[attnT])
        S.dma("gpsimd", w_s.ten.ap().rearrange("p (n c) -> p n c", n=NCB),
              ios["wo"].rearrange("(n p) c -> p n c", p=128), writes=[w_s])
        transpose_tiles(S, m1, w_s, 0, woT, 0, NCB, NCB, D, D, ident_b, psT)
        S.dma("gpsimd", m_s.ten.ap().rearrange("p (n c) -> p n c", n=NCB),
              ios["M"].rearrange("(n p) c -> p n c", p=128), writes=[m_s])
        S.dma("gpsimd", bias_s[:, :], ios["ffn_bias"].rearrange("(x c) -> x c", x=1), writes=[bias_s])

        for ti in range(NT):
            xt = x_t[ti % 2]
            x1 = x1_t[ti % 2]
            x2 = x2_t[ti % 2]
            xlnT = xlnT_t[ti % 2]
            S.dma("sync", xt[:, :], ios["x"][ti * 128:(ti + 1) * 128, :], writes=[xt])
            for oc in range(2):
                psum = psA if oc == 0 else psB
                items = [(attnT[:, cb * TOK + ti * 128: cb * TOK + (ti + 1) * 128],
                          woT[:, cb * D + oc * 512: cb * D + (oc + 1) * 512])
                         for cb in range(NCB)]
                S.mm_group(items, psum, reads=[attnT, woT])
                S.step("vector", lambda e, psum=psum, oc=oc, x1=x1, xt=xt: e.tensor_add(
                    x1.ten[:, oc * 512:(oc + 1) * 512], psum.ten[:, :],
                    xt.ten[:, oc * 512:(oc + 1) * 512]),
                    reads=[psum, xt], writes=[x1])
            ln_emit(S, m1, x1[:, :], xln_f[:, :], stat, sq, epscol,
                    reads=[x1], writes=[xln_f])
            S.step("vector", lambda e: e.tensor_copy(xln_bf.ten[:, :], xln_f.ten[:, :]),
                   reads=[xln_f], writes=[xln_bf])
            for cb in range(NCB):
                p = psT[cb % 2]
                S.step("tensor", lambda e, cb=cb, p=p: e.transpose(
                    p.ten[:, :], xln_bf.ten[:, cb * 128:(cb + 1) * 128], ident_b.ten[:, :]),
                    reads=[xln_bf, ident_b], writes=[p])
                S.step("scalar", lambda e, cb=cb, p=p, xlnT=xlnT: e.copy(
                    xlnT[:, cb * 128:(cb + 1) * 128], p.ten[:, :]),
                    reads=[p], writes=[xlnT])
            for oc in range(2):
                psum = psA if oc == 0 else psB
                items = [(xlnT[:, cb * 128:(cb + 1) * 128],
                          m_s[:, cb * D + oc * 512: cb * D + (oc + 1) * 512])
                         for cb in range(NCB)]
                items.append((ones_s[0:1, :], bias_s[0:1, oc * 512:(oc + 1) * 512]))
                S.mm_group(items, psum, reads=[xlnT, m_s, ones_s, bias_s])
                S.step("vector", lambda e, psum=psum, oc=oc, x1=x1, x2=x2: e.tensor_add(
                    x2.ten[:, oc * 512:(oc + 1) * 512], psum.ten[:, :],
                    x1.ten[:, oc * 512:(oc + 1) * 512]),
                    reads=[psum, x1], writes=[x2])
            tx2 = S.dma("sync", ios["x2"][ti * 128:(ti + 1) * 128, :], x2[:, :], reads=[x2])
            ios.setdefault("_x2_tok", {})[ti] = [tx2]
            if ti == NT - 1 and "x2tail" in ios:
                S.dma("sync", ios["x2tail"], x2[125:128, :], reads=[x2])

        if "ao_h" in ios:
            aoh = m1.sb("p2_aoh", [128, NCB * 3], BF16)
            xh3 = m1.sb("p2_xh3", [3, D])
            x1h = m1.sb("p2_x1h", [3, D])
            x2h = m1.sb("p2_x2h", [3, D])
            xlnh = m1.sb("p2_xlnh", [3, D])
            xlnh_bf = m1.sb("p2_xlnhbf", [3, D], BF16)
            xlnT_h = m1.sb("p2_xlnTh", [128, NCB * 3], BF16)
            sqh = m1.sb("p2_sqh", [3, D])
            stath = m1.sb("p2_stath", [128, 8])
            hm = m1.sb("p2_hm", [3, 1])
            S.dma("sync", aoh.ten.ap().rearrange("p (j t) -> p j t", j=NCORES),
                  ios["ao_h"].rearrange("j p t -> p j t"), writes=[aoh])
            S.dma("sync", xh3[:, :], ios["x_h"], writes=[xh3])
            S.dma("sync", hm[:, :], ios["hmask"], writes=[hm])
            for oc in range(2):
                psum = psA if oc == 0 else psB
                items = [(aoh[:, cb * 3:(cb + 1) * 3],
                          woT[:, cb * D + oc * 512: cb * D + (oc + 1) * 512])
                         for cb in range(NCB)]
                S.mm_group(items, psum, reads=[aoh, woT],
                           psum_slice=(slice(0, 3), slice(None)))
                S.step("vector", lambda e, psum=psum, oc=oc: e.tensor_add(
                    x1h.ten[0:3, oc * 512:(oc + 1) * 512], psum.ten[0:3, :],
                    xh3.ten[0:3, oc * 512:(oc + 1) * 512]),
                    reads=[psum, xh3], writes=[x1h])
            ln_emit(S, m1, x1h[0:3, :], xlnh[0:3, :], stath, sqh, epscol,
                    reads=[x1h], writes=[xlnh], rows=3)
            S.step("vector", lambda e: e.tensor_copy(xlnh_bf.ten[0:3, :], xlnh.ten[0:3, :]),
                   reads=[xlnh], writes=[xlnh_bf])
            for cb in range(NCB):
                p = psT[cb % 2]
                S.step("tensor", lambda e, cb=cb, p=p: e.transpose(
                    p.ten[0:128, 0:3], xlnh_bf.ten[0:3, cb * 128:(cb + 1) * 128],
                    ident_b.ten[0:3, 0:3]),
                    reads=[xlnh_bf, ident_b], writes=[p])
                S.step("scalar", lambda e, cb=cb, p=p: e.copy(
                    xlnT_h[:, cb * 3:(cb + 1) * 3], p.ten[0:128, 0:3]),
                    reads=[p], writes=[xlnT_h])
            for oc in range(2):
                psum = psA if oc == 0 else psB
                items = [(xlnT_h[:, cb * 3:(cb + 1) * 3],
                          m_s[:, cb * D + oc * 512: cb * D + (oc + 1) * 512])
                         for cb in range(NCB)]
                items.append((ones_s[0:1, 0:3], bias_s[0:1, oc * 512:(oc + 1) * 512]))
                S.mm_group(items, psum, reads=[xlnT_h, m_s, ones_s, bias_s],
                           psum_slice=(slice(0, 3), slice(None)))
                S.step("vector", lambda e, psum=psum, oc=oc: e.tensor_add(
                    x2h.ten[0:3, oc * 512:(oc + 1) * 512], psum.ten[0:3, :],
                    x1h.ten[0:3, oc * 512:(oc + 1) * 512]),
                    reads=[psum, x1h], writes=[x2h])
            S.step("vector", lambda e: e.tensor_scalar(
                x2h.ten[0:3, :], x2h.ten[0:3, :], hm.ten[0:3, 0:1], None, ALU.mult),
                reads=[hm], writes=[x2h])
            th = S.dma("sync", ios["halo_scr"], x2h[0:3, :], reads=[x2h])
            ios["_halo_tok"] = [th]



def emit_p3(S, mem, ios):
    cst = ios["const"]
    NDB = D_IN // 128
    EXT = TOK + 3
    with mem.scope(S) as m0:
        ident_b = m0.sb("p3_identb", [128, 128], BF16)
        epscol = m0.sb("p3_eps", [128, 1])
        onecol = m0.sb("p3_one", [128, 1])
        u_preT = m0.sb("p3_upre", [128, NDB * EXT], BF16)
        S.dma("gpsimd", ident_b[:, :], cst["ident"], writes=[ident_b])
        S.dma("gpsimd", epscol[:, :], cst["epscol"], writes=[epscol])
        S.dma("gpsimd", onecol[:, :], cst["onecol"], writes=[onecol])

        with mem.scope(S) as m1:
            xh = m1.sb("p3_xh", [3, D])
            xah = m1.sb("p3_xah", [3, D])
            xah_bf = m1.sb("p3_xahbf", [3, D], BF16)
            xaT_halo = m1.sb("p3_xaTh", [128, NCB * 3], BF16)
            stat = m1.sb("p3_stat", [128, 8])
            sqh = m1.sb("p3_sqh", [3, D])
            psH = m1.ps("p3_psH", [128, 512])
            psT = [m1.ps("p3_psT1", [128, 128], BF16), m1.ps("p3_psT2", [128, 128], BF16)]
            if "_halo_tok" not in ios:
                halo_s = m1.sb("p3_halo", [8, 3 * D])
                oneh = m1.sb("p3_oneh", [8, 1])
                xhf = m1.sb("p3_xhf", [1, 3 * D])
                S.dma("sync", halo_s.ten.ap().rearrange("p (r c) -> p r c", r=3),
                      ios["halo_g"], writes=[halo_s])
                S.dma("sync", oneh[:, :], ios["onehot"], writes=[oneh])
                for k in range(6):
                    S.mm_group([(oneh[0:8, 0:1], halo_s[:, k * 512:(k + 1) * 512])], psH,
                               reads=[oneh, halo_s], psum_slice=(slice(0, 1), slice(None)))
                    S.step("scalar", lambda e, k=k: e.copy(xhf.ten[0:1, k * 512:(k + 1) * 512],
                                                           psH.ten[0:1, :]),
                           reads=[psH], writes=[xhf])
                tw = [S.dma("sync", ios["halo_scr"].rearrange("(x r) c -> x r c", x=1),
                            xhf[0:1, :].rearrange("p (r c) -> p r c", r=3), reads=[xhf])]
            else:
                tw = ios["_halo_tok"]
            S.dma("sync", xh[:, :], ios["halo_scr"], writes=[xh], waits=tw)
            ln_emit(S, m1, xh[0:3, :], xah[0:3, :], stat, sqh, epscol,
                    reads=[xh], writes=[xah], rows=3)
            S.step("vector", lambda e: e.tensor_copy(xah_bf.ten[0:3, :], xah.ten[0:3, :]),
                   reads=[xah], writes=[xah_bf])
            for cb in range(NCB):
                p = psT[cb % 2]
                S.step("tensor", lambda e, cb=cb, p=p: e.transpose(
                    p.ten[0:128, 0:3], xah_bf.ten[0:3, cb * 128:(cb + 1) * 128],
                    ident_b.ten[0:3, 0:3]),
                    reads=[xah_bf, ident_b], writes=[p])
                S.step("scalar", lambda e, cb=cb, p=p: e.copy(
                    xaT_halo[:, cb * 3:(cb + 1) * 3], p.ten[0:128, 0:3]),
                    reads=[p], writes=[xaT_halo])

            xaT = m1.sb("p3_xaT", [128, NCB * TOK], BF16)
            x2t = [m1.sb(f"p3_x2{i}", [128, D]) for i in range(2)]
            xln_f = m1.sb("p3_xlnf", [128, D])
            xln_bf = m1.sb("p3_xlnbf", [128, D], BF16)
            sq = m1.sb("p3_sq", [128, D])
            for ti in range(NT):
                xt = x2t[ti % 2]
                S.dma("sync", xt[:, :], ios["x2"][ti * 128:(ti + 1) * 128, :], writes=[xt],
                      waits=ios.get("_x2_tok", {}).get(ti, []))
                ln_emit(S, m1, xt[:, :], xln_f[:, :], stat, sq, epscol,
                        reads=[xt], writes=[xln_f])
                S.step("vector", lambda e: e.tensor_copy(xln_bf.ten[:, :], xln_f.ten[:, :]),
                       reads=[xln_f], writes=[xln_bf])
                for cb in range(NCB):
                    p = psT[cb % 2]
                    S.step("tensor", lambda e, cb=cb, p=p: e.transpose(
                        p.ten[:, :], xln_bf.ten[:, cb * 128:(cb + 1) * 128], ident_b.ten[:, :]),
                        reads=[xln_bf, ident_b], writes=[p])
                    S.step("scalar", lambda e, cb=cb, ti=ti, p=p: e.copy(
                        xaT[:, cb * TOK + ti * 128: cb * TOK + (ti + 1) * 128], p.ten[:, :]),
                        reads=[p], writes=[xaT])

            win_s = m1.sb("p3_win", [128, NCB * D_IN], BF16)
            res_t = [m1.sb(f"p3_res{i}", [128, TOK], BF16) for i in range(2)]
            S.dma("gpsimd", win_s.ten.ap().rearrange("p (n c) -> p n c", n=NCB),
                  ios["W_in"][:, 0:D_IN].rearrange("(n p) c -> p n c", p=128),
                  writes=[win_s])
            for db in range(NDB):
                items = [(win_s[:, cb * D_IN + db * 128:cb * D_IN + (db + 1) * 128],
                          xaT_halo[:, cb * 3:(cb + 1) * 3]) for cb in range(NCB)]
                S.mm_group(items, psH, reads=[win_s, xaT_halo],
                           psum_slice=(slice(None), slice(0, 3)))
                S.step("scalar", lambda e, db=db: e.copy(
                    u_preT[:, db * EXT: db * EXT + 3], psH.ten[:, 0:3]),
                    reads=[psH], writes=[u_preT])
                for tc in range(4):
                    items = [(win_s[:, cb * D_IN + db * 128:cb * D_IN + (db + 1) * 128],
                              xaT[:, cb * TOK + tc * 512: cb * TOK + (tc + 1) * 512])
                             for cb in range(NCB)]
                    S.mm_group(items, psH, reads=[win_s, xaT])
                    S.step("scalar", lambda e, db=db, tc=tc: e.copy(
                        u_preT[:, db * EXT + 3 + tc * 512: db * EXT + 3 + (tc + 1) * 512],
                        psH.ten[:, :]),
                        reads=[psH], writes=[u_preT])
            S.dma("gpsimd", win_s.ten.ap().rearrange("p (n c) -> p n c", n=NCB),
                  ios["W_in"][:, D_IN:2 * D_IN].rearrange("(n p) c -> p n c", p=128),
                  writes=[win_s])
            for db in range(NDB):
                rt = res_t[db % 2]
                for tc in range(4):
                    items = [(win_s[:, cb * D_IN + db * 128:cb * D_IN + (db + 1) * 128],
                              xaT[:, cb * TOK + tc * 512: cb * TOK + (tc + 1) * 512])
                             for cb in range(NCB)]
                    S.mm_group(items, psH, reads=[win_s, xaT])
                    S.step("scalar", lambda e, tc=tc, rt=rt: e.activation(
                        rt.ten[:, tc * 512:(tc + 1) * 512], psH.ten[:, :], AF.Silu),
                        reads=[psH], writes=[rt])
                S.dma("sync", ios["sres"][db * 128:(db + 1) * 128, :], rt[:, :], reads=[rt])

        with mem.scope(S) as m2:
            cw_t = [m2.sb(f"p3_cw{i}", [128, 4 * NDB * 128], BF16) for i in range(2)]
            u_t = [m2.sb(f"p3_u{i}", [128, TOK], BF16) for i in range(2)]
            wx_s = m2.sb("p3_wx", [128, NDB * 96], BF16)
            cb_col = m2.sb("p3_cb", [128, NDB])
            psC = [m2.ps(f"p3_psC{i}", [128, 512]) for i in range(2)]
            psX = [m2.ps(f"p3_psX{i}", [128, 512]) for i in range(4)]
            S.dma("gpsimd", wx_s.ten.ap().rearrange("p (n c) -> p n c", n=NDB),
                  ios["W_x"].rearrange("(n p) c -> p n c", p=128), writes=[wx_s])
            ident_f = m2.sb("p3_identf", [128, 128])
            colr = m2.sb("p3_colr", [16, 128])
            psF = m2.ps("p3_psF", [128, 128])
            S.dma("gpsimd", ident_f[:, :], cst["ident"], writes=[ident_f])
            S.dma("gpsimd", colr[:, :], ios["conv_b"].rearrange("(n p) -> n p", n=NDB),
                  writes=[colr])
            S.step("tensor", lambda e: e.transpose(
                psF.ten[0:128, 0:16], colr.ten[0:16, :], ident_f.ten[0:16, 0:16]),
                reads=[colr, ident_f], writes=[psF])
            S.step("scalar", lambda e: e.copy(cb_col.ten[:, :], psF.ten[0:128, 0:16]),
                   reads=[psF], writes=[cb_col])
            xdbl_started = [False] * 4
            for ob in range(NDB):
                cw = cw_t[ob % 2]
                S.dma("gpsimd", cw.ten.ap().rearrange("p (j it o) -> p j it o", j=4, it=NDB),
                      ios["conv_w"][:, :, ob * 128:(ob + 1) * 128]
                          .rearrange("j (it p) o -> p j it o", p=128),
                      writes=[cw])
                ut = u_t[ob % 2]
                for tc in range(4):
                    psum = psC[tc % 2]
                    items = []
                    for j in range(4):
                        for it in range(NDB):
                            items.append((cw[:, (j * NDB + it) * 128:(j * NDB + it + 1) * 128],
                                          u_preT[:, it * EXT + tc * 512 + j:
                                                 it * EXT + tc * 512 + j + 512]))
                    S.mm_group(items, psum, reads=[cw, u_preT])
                    S.step("scalar", lambda e, tc=tc, ob=ob, psum=psum, ut=ut: e.activation(
                        ut.ten[:, tc * 512:(tc + 1) * 512], psum.ten[:, :], AF.Silu,
                        bias=cb_col.ten[:, ob:ob + 1]),
                        reads=[psum, cb_col], writes=[ut])
                    px = psX[tc]
                    wl = []
                    if not xdbl_started[tc]:
                        if px.wr is not None:
                            wl.append(px.wr)
                        wl.extend(px.rds)
                    wl.append(ut.wr)
                    wl.append(wx_s.wr)
                    tokx = S._record(
                        "tensor",
                        lambda e, ob=ob, tc=tc, px=px, ut=ut, s=(ob == 0): e.matmul(
                            px.ten[0:96, :], wx_s.ten[:, ob * 96:(ob + 1) * 96],
                            ut.ten[:, tc * 512:(tc + 1) * 512],
                            start=s, stop=(ob == NDB - 1), skip_group_check=True),
                        wl, "pe", 1)
                    xdbl_started[tc] = True
                    ut.rds.append(tokx)
                    wx_s.rds.append(tokx)
                    if ob == NDB - 1:
                        px.wr = tokx
                        px.rds = []
                S.dma("sync", ios["u_scr"][ob * 128:(ob + 1) * 128, :], ut[:, :], reads=[ut])

            dpre = m2.sb("p3_dpre", [64, TOK], BF16)
            bc_s = m2.sb("p3_bc", [32, TOK], BF16)
            for tc in range(4):
                S.step("scalar", lambda e, tc=tc: e.copy(
                    dpre.ten[0:64, tc * 512:(tc + 1) * 512], psX[tc].ten[0:64, :]),
                    reads=[psX[tc]], writes=[dpre])
                S.step("scalar", lambda e, tc=tc: e.copy(
                    bc_s.ten[0:32, tc * 512:(tc + 1) * 512], psX[tc].ten[64:96, :]),
                    reads=[psX[tc]], writes=[bc_s])
            ios["_bc_tok"] = [S.dma("sync", ios["bc"], bc_s[0:32, :], reads=[bc_s])]

            wdt_s = m2.sb("p3_wdt", [64, D_IN], BF16)
            bdt_col = m2.sb("p3_bdt", [128, NDB])
            dtmp = m2.sb("p3_dtmp", [128, 512])
            dbf_t = [m2.sb(f"p3_dbf{i}", [128, TOK], BF16) for i in range(2)]
            du_t = [m2.sb(f"p3_du{i}", [128, TOK], BF16) for i in range(2)]
            ur_t = [m2.sb(f"p3_ur{i}", [128, TOK], BF16) for i in range(2)]
            u_scr_done = S.dma_barrier()
            S.dma("gpsimd", wdt_s[:, :], ios["W_dt"], writes=[wdt_s])
            S.dma("gpsimd", colr[:, :], ios["b_dt"].rearrange("(n p) -> n p", n=NDB),
                  writes=[colr])
            S.step("tensor", lambda e: e.transpose(
                psF.ten[0:128, 0:16], colr.ten[0:16, :], ident_f.ten[0:16, 0:16]),
                reads=[colr, ident_f], writes=[psF])
            S.step("scalar", lambda e: e.copy(bdt_col.ten[:, :], psF.ten[0:128, 0:16]),
                   reads=[psF], writes=[bdt_col])
            for db in range(NDB):
                dbf = dbf_t[db % 2]
                du = du_t[db % 2]
                ur = ur_t[db % 2]
                S.dma("sync", ur[:, :], ios["u_scr"][db * 128:(db + 1) * 128, :],
                      writes=[ur], waits=u_scr_done)
                for tc in range(4):
                    psum = psC[tc % 2]
                    S.mm_group([(wdt_s[0:64, db * 128:(db + 1) * 128],
                                 dpre[0:64, tc * 512:(tc + 1) * 512])], psum,
                               reads=[wdt_s, dpre])
                    S.step("scalar", lambda e, db=db, psum=psum: e.activation(
                        dtmp.ten[:, :], psum.ten[:, :], AF.Exp,
                        bias=bdt_col.ten[:, db:db + 1]),
                        reads=[psum, bdt_col], writes=[dtmp])
                    S.step("scalar", lambda e, tc=tc, dbf=dbf: e.activation(
                        dbf.ten[:, tc * 512:(tc + 1) * 512], dtmp.ten[:, :], AF.Ln,
                        bias=onecol.ten[:, 0:1]),
                        reads=[dtmp, onecol], writes=[dbf])
                S.step("vector", lambda e, dbf=dbf, du=du, ur=ur: e.tensor_mul(
                    du.ten[:, :], dbf.ten[:, :], ur.ten[:, :]),
                    reads=[dbf, ur], writes=[du])
                t1 = S.dma("sync", ios["dlt"][db * 128:(db + 1) * 128, :], dbf[:, :], reads=[dbf])
                t2 = S.dma("sync", ios["du"][db * 128:(db + 1) * 128, :], du[:, :], reads=[du])
                ios.setdefault("_dltdu_tok", {})[db] = [t1, t2]


def build_p2():
    nc = bass.Bass("TRN2", target_bir_lowering=False, debug=False)
    ios = {"const": {}}
    for name, shape, dt in CONST_SPECS:
        ios["const"][name] = nc.dram_tensor("c_" + name, shape, dt, kind="ExternalInput").ap()
    ios["x"] = nc.dram_tensor("x", [TOK, D], F32, kind="ExternalInput").ap()
    ios["ao"] = nc.dram_tensor("ao", [NCORES, 128, TOK], BF16, kind="ExternalInput").ap()
    ios["wo"] = nc.dram_tensor("wo", [D, D], F32, kind="ExternalInput").ap()
    ios["M"] = nc.dram_tensor("M", [D, D], F32, kind="ExternalInput").ap()
    ios["ffn_bias"] = nc.dram_tensor("ffn_bias", [D], F32, kind="ExternalInput").ap()
    ios["x2"] = nc.dram_tensor("x2", [TOK, D], F32, kind="ExternalOutput").ap()
    ios["x2tail"] = nc.dram_tensor("x2tail", [3, D], F32, kind="ExternalOutput").ap()
    with contextlib.ExitStack() as ctx:
        mem = Mem(nc, ctx)
        S = Sched(nc)
        emit_p2(S, mem, ios)
        S.step("gpsimd", lambda e: e.nop(), waits=S.dma_barrier(), sem=None)
        block = ctx.enter_context(nc.Block())
        S.build(block)
    S.close()
    return nc


def build_p3():
    nc = bass.Bass("TRN2", target_bir_lowering=False, debug=False)
    ios = {"const": {}}
    for name, shape, dt in CONST_SPECS:
        ios["const"][name] = nc.dram_tensor("c_" + name, shape, dt, kind="ExternalInput").ap()
    ios["x2"] = nc.dram_tensor("x2", [TOK, D], F32, kind="ExternalInput").ap()
    ios["halo_g"] = nc.dram_tensor("halo_g", [NCORES, 3, D], F32, kind="ExternalInput").ap()
    ios["onehot"] = nc.dram_tensor("onehot", [NCORES, 1], F32, kind="ExternalInput").ap()
    ios["W_in"] = nc.dram_tensor("W_in", [D, 2 * D_IN], F32, kind="ExternalInput").ap()
    ios["conv_w"] = nc.dram_tensor("conv_w", [4, D_IN, D_IN], F32, kind="ExternalInput").ap()
    ios["conv_b"] = nc.dram_tensor("conv_b", [D_IN], F32, kind="ExternalInput").ap()
    ios["W_x"] = nc.dram_tensor("W_x", [D_IN, 96], F32, kind="ExternalInput").ap()
    ios["W_dt"] = nc.dram_tensor("W_dt", [DT_RANK, D_IN], F32, kind="ExternalInput").ap()
    ios["b_dt"] = nc.dram_tensor("b_dt", [D_IN], F32, kind="ExternalInput").ap()
    ios["dlt"] = nc.dram_tensor("dlt", [D_IN, TOK], BF16, kind="ExternalOutput").ap()
    ios["du"] = nc.dram_tensor("du", [D_IN, TOK], BF16, kind="ExternalOutput").ap()
    ios["sres"] = nc.dram_tensor("sres", [D_IN, TOK], BF16, kind="ExternalOutput").ap()
    ios["bc"] = nc.dram_tensor("bc", [32, TOK], BF16, kind="ExternalOutput").ap()
    ios["halo_scr"] = nc.dram_tensor("halo_scr", [3, D], F32, kind="Internal").ap()
    ios["u_scr"] = nc.dram_tensor("u_scr", [D_IN, TOK], BF16, kind="Internal").ap()
    with contextlib.ExitStack() as ctx:
        mem = Mem(nc, ctx)
        S = Sched(nc)
        emit_p3(S, mem, ios)
        S.step("gpsimd", lambda e: e.nop(), waits=S.dma_barrier(), sem=None)
        block = ctx.enter_context(nc.Block())
        S.build(block)
    S.close()
    return nc



def emit_p4(S, mem, ios):
    NG = 16
    F = 2048
    with mem.scope(S) as m1:
        e8 = m1.sb("p4_e8", [128, NG * 128], BF16)
        ws = m1.sb("p4_ws", [128, NG * 128], BF16)
        e16 = m1.sb("p4_e16", [32, 256], BF16)
        alog = m1.sb("p4_alog", [128, 32])
        alan = m1.sb("p4_alan", [128, 32])
        dcol = m1.sb("p4_dcol", [128, 2])
        dlt_b = m1.sb("p4_dlt", [128, 2 * T_FULL], BF16)
        du_b = m1.sb("p4_du", [128, 2 * T_FULL], BF16)
        sres_b = m1.sb("p4_sres", [128, 2 * T_FULL], BF16)
        y_sb = m1.sb("p4_y", [128, 2 * T_FULL], BF16)
        bc_b = m1.sb("p4_bc", [32, T_FULL], BF16)
        brep = m1.sb("p4_brep", [128, F])
        crep = m1.sb("p4_crep", [128, F])
        a_sb = m1.sb("p4_a", [128, F])
        b_sb = m1.sb("p4_b", [128, F])
        s_sb = m1.sb("p4_s", [128, F])
        sc_sb = m1.sb("p4_sc", [128, F], BF16)
        carry = m1.sb("p4_carry", [128, 64])
        resid = m1.sb("p4_resid", [128, 2])
        carrybf = m1.sb("p4_carrybf", [128, 64], BF16)
        psE = [m1.ps(f"p4_psE{i}", [128, 512]) for i in range(2)]
        psU = [m1.ps(f"p4_psU{i}", [128, 512]) for i in range(2)]
        psY = m1.ps("p4_psY", [128, 2048])
        cst = ios["const"]

        S.dma("gpsimd", e8[:, :], ios["e8sel"], writes=[e8])
        S.dma("gpsimd", ws[:, :], ios["wsel"], writes=[ws])
        S.dma("gpsimd", e16[:, :], ios["e16sel"], writes=[e16])
        S.dma("gpsimd", alog[:, :], ios["alog_lanes"], writes=[alog])
        S.dma("gpsimd", dcol[:, :], ios["d_col"], writes=[dcol])
        S.step("scalar", lambda e: e.activation(alan.ten[:, :], alog.ten[:, :], AF.Exp),
               reads=[alog], writes=[alan])
        S.step("vector", lambda e: e.tensor_scalar(alan.ten[:, :], alan.ten[:, :],
                                                   -1.0, None, ALU.mult),
               reads=[alan], writes=[alan])
        S.step("vector", lambda e: e.memset(carry.ten[:, :], 0.0), writes=[carry])

        for b in range(2):
            for t3, io3 in ((dlt_b, "dlt_g"), (du_b, "du_g"), (sres_b, "sres_g")):
                for dh in range(2):
                    S.dma("sync",
                          t3.ten[:, dh * T_FULL:(dh + 1) * T_FULL]
                              .rearrange("p (s t) -> p s t", s=4),
                          ios[io3][4 * b:4 * b + 4, dh * 128:(dh + 1) * 128, :]
                              .rearrange("s p t -> p s t"),
                          writes=[t3])
            S.dma("sync", bc_b.ten.ap().rearrange("p (s t) -> p s t", s=4),
                  ios["bc_g"][4 * b:4 * b + 4, :, :].rearrange("s p t -> p s t"),
                  writes=[bc_b])
            if b == 0:
                S.step("vector", lambda e: e.memset(carry.ten[:, :], 0.0),
                       reads=[], writes=[carry])
            else:
                S.step("vector", lambda e: e.memset(carry.ten[:, :], 0.0),
                       writes=[carry])
            for tq in range(4):
                t0 = tq * F
                for which, rep, r0 in ((0, brep, 0), (1, crep, 16)):
                    for sub in range(4):
                        ps = psE[sub % 2]
                        S.mm_group([(e16[0:32, r0 * 8:r0 * 8 + 128] if False else
                                     e16[0:32, which * 128:(which + 1) * 128],
                                     bc_b[0:32, t0 + sub * 512: t0 + (sub + 1) * 512])],
                                   ps, reads=[e16, bc_b])
                        S.step("scalar", lambda e, rep=rep, sub=sub, ps=ps: e.copy(
                            rep.ten[:, sub * 512:(sub + 1) * 512], ps.ten[:, :]),
                            reads=[ps], writes=[rep])
                ystart = [False] * 4
                for dh in range(2):
                    for g16 in range(NG):
                        gidx = dh * NG + g16
                        dsl = slice(dh * T_FULL + t0, dh * T_FULL + t0 + F)
                        for sub in range(4):
                            ps = psE[sub % 2]
                            S.mm_group([(e8[:, g16 * 128:(g16 + 1) * 128],
                                         dlt_b[:, dh * T_FULL + t0 + sub * 512:
                                               dh * T_FULL + t0 + (sub + 1) * 512])],
                                       ps, reads=[e8, dlt_b])
                            S.step("scalar", lambda e, sub=sub, ps=ps, gidx=gidx: e.activation(
                                a_sb.ten[:, sub * 512:(sub + 1) * 512], ps.ten[:, :],
                                AF.Exp, scale=alan.ten[:, gidx:gidx + 1]),
                                reads=[ps, alan], writes=[a_sb])
                        for sub in range(4):
                            ps = psU[sub % 2]
                            S.mm_group([(e8[:, g16 * 128:(g16 + 1) * 128],
                                         du_b[:, dh * T_FULL + t0 + sub * 512:
                                              dh * T_FULL + t0 + (sub + 1) * 512])],
                                       ps, reads=[e8, du_b])
                            S.step("vector", lambda e, sub=sub, ps=ps: e.tensor_mul(
                                b_sb.ten[:, sub * 512:(sub + 1) * 512], ps.ten[:, :],
                                brep.ten[:, sub * 512:(sub + 1) * 512]),
                                reads=[ps, brep], writes=[b_sb])
                        ccol = carry.ten[:, b * 0 + gidx + (0 if b == 0 else 0):gidx + 1]
                        S.step("vector", lambda e, gidx=gidx: e.tensor_tensor_scan(
                            s_sb.ten[:, :], a_sb.ten[:, :], b_sb.ten[:, :],
                            carry.ten[:, gidx:gidx + 1], ALU.mult, ALU.add),
                            reads=[a_sb, b_sb, carry], writes=[s_sb])
                        S.step("vector", lambda e, gidx=gidx: e.tensor_copy(
                            carry.ten[:, gidx:gidx + 1], s_sb.ten[:, F - 1:F]),
                            reads=[s_sb], writes=[carry])
                        S.step("gpsimd", lambda e: e.tensor_mul(
                            sc_sb.ten[:, :], s_sb.ten[:, :], crep.ten[:, :]),
                            reads=[s_sb, crep], writes=[sc_sb])
                        for sub in range(4):
                            wl = []
                            if not ystart[sub]:
                                if psY.wr is not None:
                                    wl.append(psY.wr)
                                wl.extend(psY.rds)
                            if sc_sb.wr is not None:
                                wl.append(sc_sb.wr)
                            if ws.wr is not None:
                                wl.append(ws.wr)
                            toky = S._record(
                                "tensor",
                                lambda e, g16=g16, sub=sub, s=(not ystart[sub]),
                                       last=(g16 == NG - 1): e.matmul(
                                    psY.ten[:, sub * 512:(sub + 1) * 512],
                                    ws.ten[:, g16 * 128:(g16 + 1) * 128],
                                    sc_sb.ten[:, sub * 512:(sub + 1) * 512],
                                    start=s, stop=last, skip_group_check=True),
                                wl, "pe", 1)
                            ystart[sub] = True
                            sc_sb.rds.append(toky)
                            ws.rds.append(toky)
                            if g16 == NG - 1:
                                if sub == 3:
                                    psY.wr = toky
                                    psY.rds = []
                    S.step("scalar", lambda e, dh=dh, t0=t0: e.copy(
                        y_sb.ten[:, dh * T_FULL + t0: dh * T_FULL + t0 + F],
                        psY.ten[:, :]),
                        reads=[psY], writes=[y_sb])
                    ystart = [False] * 4
            for dh in range(2):
                S.step("vector", lambda e, dh=dh: e.tensor_copy(
                    carrybf.ten[:, dh * NG:(dh + 1) * NG],
                    carry.ten[:, dh * NG:(dh + 1) * NG]),
                    reads=[carry], writes=[carrybf])
                items = []
                for g16 in range(NG):
                    gidx = dh * NG + g16
                    items.append((ws[:, g16 * 128:(g16 + 1) * 128],
                                  carrybf[:, gidx:gidx + 1]))
                S.mm_group(items, psE[0], reads=[ws, carrybf],
                           psum_slice=(slice(None), slice(0, 1)))
                S.step("vector", lambda e, dh=dh: e.tensor_mul(
                    resid.ten[:, dh:dh + 1], psE[0].ten[:, 0:1], dcol.ten[:, dh:dh + 1]),
                    reads=[psE[0], dcol], writes=[resid])
                S.step("scalar", lambda e, dh=dh: e.add(
                    y_sb.ten[:, dh * T_FULL:(dh + 1) * T_FULL],
                    y_sb.ten[:, dh * T_FULL:(dh + 1) * T_FULL],
                    resid.ten[:, dh:dh + 1]),
                    reads=[resid], writes=[y_sb])
                S.step("vector", lambda e, dh=dh: e.tensor_mul(
                    y_sb.ten[:, dh * T_FULL:(dh + 1) * T_FULL],
                    y_sb.ten[:, dh * T_FULL:(dh + 1) * T_FULL],
                    sres_b.ten[:, dh * T_FULL:(dh + 1) * T_FULL]),
                    reads=[sres_b], writes=[y_sb])
                for tq in range(4):
                    S.dma("sync", ios["ya2a"][4 * b + tq, dh * 128:(dh + 1) * 128, :],
                          y_sb[:, dh * T_FULL + tq * F: dh * T_FULL + (tq + 1) * F],
                          reads=[y_sb])


def build_p4():
    nc = bass.Bass("TRN2", target_bir_lowering=False, debug=False)
    ios = {"const": {}}
    for name, shape, dt in CONST_SPECS:
        ios["const"][name] = nc.dram_tensor("c_" + name, shape, dt, kind="ExternalInput").ap()
    for nm in ("dlt_g", "du_g", "sres_g"):
        ios[nm] = nc.dram_tensor(nm, [NCORES, 256, TOK], BF16, kind="ExternalInput").ap()
    ios["bc_g"] = nc.dram_tensor("bc_g", [NCORES, 32, TOK], BF16, kind="ExternalInput").ap()
    ios["alog_lanes"] = nc.dram_tensor("alog_lanes", [128, 32], F32, kind="ExternalInput").ap()
    ios["d_col"] = nc.dram_tensor("d_col", [128, 2], F32, kind="ExternalInput").ap()
    ios["e8sel"] = nc.dram_tensor("e8sel", [128, 16 * 128], BF16, kind="ExternalInput").ap()
    ios["wsel"] = nc.dram_tensor("wsel", [128, 16 * 128], BF16, kind="ExternalInput").ap()
    ios["e16sel"] = nc.dram_tensor("e16sel", [32, 256], BF16, kind="ExternalInput").ap()
    ios["ya2a"] = nc.dram_tensor("ya2a", [NCORES, 256, TOK], BF16, kind="ExternalOutput").ap()
    with contextlib.ExitStack() as ctx:
        mem = Mem(nc, ctx)
        S = Sched(nc)
        emit_p4(S, mem, ios)
        S.step("gpsimd", lambda e: e.nop(), waits=S.dma_barrier(), sem=None)
        block = ctx.enter_context(nc.Block())
        S.build(block)
    S.close()
    return nc


def p4_host_consts():
    e8 = np.zeros((128, 16 * 128), np.float32)
    for g in range(16):
        for l in range(128):
            e8[g * 8 + l // 16, g * 128 + l] = 1.0
    ws = np.zeros((128, 16 * 128), np.float32)
    for g in range(16):
        for l in range(128):
            ws[l, g * 128 + g * 8 + l // 16] = 1.0
    e16 = np.zeros((32, 256), np.float32)
    for l in range(128):
        e16[l % 16, l] = 1.0
        e16[16 + l % 16, 128 + l] = 1.0
    import ml_dtypes
    bf = ml_dtypes.bfloat16
    return e8.astype(bf), ws.astype(bf), e16.astype(bf)



def emit_p5(S, mem, ios):
    cst = ios["const"]
    NDB = D_IN // 128
    with mem.scope(S) as m0:
        ident_b = m0.sb("p5_identb", [128, 128], BF16)
        epscol = m0.sb("p5_eps", [128, 1])
        acc = m0.sb("p5_acc", [128, NT * D])
        xfT = m0.sb("p5_xfT", [128, NCB * TOK], BF16)
        combT = m0.sb("p5_combT", [16, TOK], BF16)
        stat = m0.sb("p5_stat", [128, 8])
        psA = m0.ps("p5_psA", [128, 512])
        psB = m0.ps("p5_psB", [128, 512])
        psC = m0.ps("p5_psC", [128, 512])
        psT = [m0.ps("p5_psT1", [128, 128], BF16), m0.ps("p5_psT2", [128, 128], BF16)]
        S.dma("gpsimd", ident_b[:, :], cst["ident"], writes=[ident_b])
        S.dma("gpsimd", epscol[:, :], cst["epscol"], writes=[epscol])

        with mem.scope(S) as m1:
            ygt = m1.sb("p5_ygt", [128, (NDB // 2) * TOK], BF16)
            wout = m1.sb("p5_wout", [128, NDB * D], BF16)
            x2_t = [m1.sb(f"p5_x2{i}", [128, D]) for i in range(2)]
            xln_f = m1.sb("p5_xlnf", [128, D])
            xln_bf = m1.sb("p5_xlnbf", [128, D], BF16)
            sq = m1.sb("p5_sq", [128, D])
            S.dma("gpsimd", wout.ten.ap().rearrange("p (n c) -> p n c", n=NDB),
                  ios["W_out"].rearrange("(n p) c -> p n c", p=128), writes=[wout])
            for dh in range(2):
                S.dma("sync", ygt.ten.ap().rearrange("p (j t) -> p j t", j=NDB // 2),
                      ios["yg_g"].rearrange("j (g h p) t -> p (j g h) t", p=128, g=2)
                          [:, dh, :, :] if False else
                      ios["yg_g"][:, dh * 128:(dh + 1) * 128, :]
                          .rearrange("j p t -> p j t"),
                      writes=[ygt])
                for ti in range(NT):
                    xt = x2_t[ti % 2]
                    if dh == 0:
                        S.dma("sync", xt[:, :], ios["x2"][ti * 128:(ti + 1) * 128, :],
                              writes=[xt])
                    for oc in range(2):
                        psum = psA if oc == 0 else psB
                        items = [(ygt[:, it * TOK + ti * 128: it * TOK + (ti + 1) * 128],
                                  wout[:, (2 * it + dh) * D + oc * 512:
                                       (2 * it + dh) * D + (oc + 1) * 512])
                             for it in range(NDB // 2)]
                        S.mm_group(items, psum, reads=[ygt, wout])
                        if dh == 0:
                            S.step("vector", lambda e, psum=psum, oc=oc, ti=ti, xt=xt: e.tensor_add(
                                acc[:, ti * D + oc * 512: ti * D + (oc + 1) * 512],
                                psum.ten[:, :], xt.ten[:, oc * 512:(oc + 1) * 512]),
                                reads=[psum, xt], writes=[acc])
                        else:
                            S.step("vector", lambda e, psum=psum, oc=oc, ti=ti: e.tensor_add(
                                acc[:, ti * D + oc * 512: ti * D + (oc + 1) * 512],
                                psum.ten[:, :],
                                acc[:, ti * D + oc * 512: ti * D + (oc + 1) * 512]),
                                reads=[psum], writes=[acc])
            for ti in range(NT):
                ln_emit(S, m1, acc[:, ti * D:(ti + 1) * D], xln_f[:, :], stat, sq, epscol,
                        reads=[acc], writes=[xln_f])
                S.step("vector", lambda e: e.tensor_copy(xln_bf.ten[:, :], xln_f.ten[:, :]),
                       reads=[xln_f], writes=[xln_bf])
                for cb in range(NCB):
                    p = psT[cb % 2]
                    S.step("tensor", lambda e, cb=cb, p=p: e.transpose(
                        p.ten[:, :], xln_bf.ten[:, cb * 128:(cb + 1) * 128], ident_b.ten[:, :]),
                        reads=[xln_bf, ident_b], writes=[p])
                    S.step("scalar", lambda e, cb=cb, ti=ti, p=p: e.copy(
                        xfT[:, cb * TOK + ti * 128: cb * TOK + (ti + 1) * 128], p.ten[:, :]),
                        reads=[p], writes=[xfT])

            rwr = m1.sb("p5_rwr", [16, D], BF16)
            rwt = m1.sb("p5_rwt", [128, NCB * 16], BF16)
            scT = m1.sb("p5_scT", [16, TOK])
            scores_s = m1.sb("p5_scores", [128, NT * 16])
            S.dma("gpsimd", rwr[:, :], ios["router_w"], writes=[rwr])
            for cb in range(NCB):
                p = psT[cb % 2]
                S.step("tensor", lambda e, cb=cb, p=p: e.transpose(
                    p.ten[0:128, 0:16], rwr.ten[0:16, cb * 128:(cb + 1) * 128],
                    ident_b.ten[0:16, 0:16]),
                    reads=[rwr, ident_b], writes=[p])
                S.step("scalar", lambda e, cb=cb, p=p: e.copy(
                    rwt[:, cb * 16:(cb + 1) * 16], p.ten[0:128, 0:16]),
                    reads=[p], writes=[rwt])
            for tc in range(4):
                items = [(rwt[:, cb * 16:(cb + 1) * 16],
                          xfT[:, cb * TOK + tc * 512: cb * TOK + (tc + 1) * 512])
                         for cb in range(NCB)]
                S.mm_group(items, psA, reads=[rwt, xfT],
                           psum_slice=(slice(0, 16), slice(None)))
                S.step("scalar", lambda e, tc=tc: e.activation(
                    scT.ten[0:16, tc * 512:(tc + 1) * 512], psA.ten[0:16, :], AF.Sigmoid),
                    reads=[psA], writes=[scT])
            scT_bf = m1.sb("p5_scTbf", [16, TOK], BF16)
            S.step("vector", lambda e: e.tensor_copy(scT_bf.ten[0:16, :], scT.ten[0:16, :]),
                   reads=[scT], writes=[scT_bf])
            for ti in range(NT):
                p = psT[ti % 2]
                S.step("tensor", lambda e, ti=ti, p=p: e.transpose(
                    p.ten[0:128, 0:16], scT_bf.ten[0:16, ti * 128:(ti + 1) * 128],
                    ident_b.ten[0:16, 0:16]),
                    reads=[scT_bf, ident_b], writes=[p])
                S.step("scalar", lambda e, ti=ti, p=p: e.copy(
                    scores_s[:, ti * 16:(ti + 1) * 16], p.ten[0:128, 0:16]),
                    reads=[p], writes=[scores_s])

            cb_b = m1.sb("p5_cbb", [128, 16])
            cbr = m1.sb("p5_cbr", [1, 16])
            ones_f = m1.sb("p5_onesf", [1, 128])
            neginf = m1.sb("p5_neg", [128, 16])
            sb_t = m1.sb("p5_sbt", [128, 16])
            pstmp = m1.sb("p5_pstmp", [128, 24])
            gs_s = m1.sb("p5_gs", [128, 4])
            gmin = m1.sb("p5_gmin", [128, 6])
            red = m1.sb("p5_red", [128, 4])
            em_s = m1.sb("p5_em", [128, 16])
            msk = m1.sb("p5_msk", [128, 16])
            mm_s = m1.sb("p5_mm", [128, 16])
            mm2_s = m1.sb("p5_mm2", [128, 16])
            eq_s = m1.sb("p5_eq", [128, 16])
            wr_s2 = m1.sb("p5_wr", [128, 16])
            comb_bf = m1.sb("p5_combbf", [128, 16], BF16)
            em_u8 = m1.sb("p5_emu8", [128, 16], mybir.dt.uint8)
            eq_u8 = m1.sb("p5_equ8", [128, 16], mybir.dt.uint8)
            S.dma("sync", cbr[:, :], ios["corr_bias"].rearrange("(x e) -> x e", x=1),
                  writes=[cbr])
            S.dma("sync", ones_f[:, :], cst["ones1x128"], writes=[ones_f])
            S.step("vector", lambda e: e.memset(neginf.ten[:, :], -1e30), writes=[neginf])
            S.mm_group([(ones_f[0:1, :], cbr[0:1, :])], psA, reads=[ones_f, cbr],
                       psum_slice=(slice(None), slice(0, 16)))
            S.step("scalar", lambda e: e.copy(cb_b.ten[:, :], psA.ten[:, 0:16]),
                   reads=[psA], writes=[cb_b])
            for ti in range(NT):
                sc_ap = scores_s[:, ti * 16:(ti + 1) * 16]
                S.step("vector", lambda e, sc_ap=sc_ap: e.tensor_add(
                    sb_t.ten[:, :], sc_ap, cb_b.ten[:, :]),
                    reads=[scores_s, cb_b], writes=[sb_t])
                pk = 0
                for i in range(4):
                    for j in range(i + 1, 4):
                        S.step("vector", lambda e, i=i, j=j, pk=pk: e.tensor_add(
                            pstmp.ten[:, pk * 4:(pk + 1) * 4],
                            sb_t.ten.ap()[:, i:13 + i:4], sb_t.ten.ap()[:, j:13 + j:4]),
                            reads=[sb_t], writes=[pstmp])
                        pk += 1
                S.step("vector", lambda e: e.tensor_max(
                    gs_s.ten[:, :], pstmp.ten[:, 0:4], pstmp.ten[:, 4:8]),
                    reads=[pstmp], writes=[gs_s])
                for k in range(2, 6):
                    S.step("vector", lambda e, k=k: e.tensor_max(
                        gs_s.ten[:, :], gs_s.ten[:, :], pstmp.ten[:, k * 4:(k + 1) * 4]),
                        reads=[pstmp], writes=[gs_s])
                pk = 0
                for i in range(4):
                    for j in range(i + 1, 4):
                        S.step("vector", lambda e, i=i, j=j, pk=pk: e.tensor_tensor(
                            gmin.ten[:, pk:pk + 1], gs_s.ten[:, i:i + 1],
                            gs_s.ten[:, j:j + 1], ALU.min),
                            reads=[gs_s], writes=[gmin])
                        pk += 1
                S.step("vector", lambda e: e.reduce_max(red.ten[:, 0:1], gmin.ten[:, :], axis=AXX),
                       reads=[gmin], writes=[red])
                S.step("vector", lambda e: e.tensor_scalar(
                    gs_s.ten[:, :], gs_s.ten[:, :], red.ten[:, 0:1], None, ALU.is_ge),
                    reads=[red], writes=[gs_s])
                for j in range(4):
                    S.step("vector", lambda e, j=j: e.tensor_copy(
                        em_s.ten.ap()[:, j:13 + j:4], gs_s.ten[:, :]),
                        reads=[gs_s], writes=[em_s])
                S.step("vector", lambda e: e.tensor_copy(em_u8.ten[:, :], em_s.ten[:, :]),
                       reads=[em_s], writes=[em_u8])
                S.step("vector", lambda e: e.select(
                    msk.ten[:, :], em_u8.ten[:, :], sb_t.ten[:, :], neginf.ten[:, :], add_drain=True),
                    reads=[em_u8, sb_t, neginf], writes=[msk])
                S.step("vector", lambda e: e.tensor_copy(mm_s.ten[:, :], msk.ten[:, :]),
                       reads=[msk], writes=[mm_s])
                cur, nxt = mm_s, mm2_s
                for it in range(3):
                    S.step("vector", lambda e, cur=cur: e.reduce_max(
                        red.ten[:, 2:3], cur.ten[:, :], axis=AXX),
                        reads=[cur], writes=[red])
                    S.step("vector", lambda e, cur=cur: e.tensor_scalar(
                        eq_s.ten[:, :], cur.ten[:, :], red.ten[:, 2:3], None, ALU.is_equal),
                        reads=[cur, red], writes=[eq_s])
                    S.step("vector", lambda e: e.tensor_copy(eq_u8.ten[:, :], eq_s.ten[:, :]),
                           reads=[eq_s], writes=[eq_u8])
                    S.step("vector", lambda e, cur=cur, nxt=nxt: e.select(
                        nxt.ten[:, :], eq_u8.ten[:, :], neginf.ten[:, :], cur.ten[:, :], add_drain=True),
                        reads=[eq_u8, neginf, cur], writes=[nxt])
                    cur, nxt = nxt, cur
                S.step("vector", lambda e, cur=cur: e.reduce_max(
                    red.ten[:, 3:4], cur.ten[:, :], axis=AXX),
                    reads=[cur], writes=[red])
                S.step("vector", lambda e: e.tensor_scalar(
                    msk.ten[:, :], msk.ten[:, :], red.ten[:, 3:4], None, ALU.is_ge),
                    reads=[red], writes=[msk])
                S.step("vector", lambda e, sc_ap=sc_ap: e.tensor_mul(
                    wr_s2.ten[:, :], sc_ap, msk.ten[:, :]),
                    reads=[scores_s, msk], writes=[wr_s2])
                S.step("vector", lambda e: e.reduce_sum(
                    red.ten[:, 1:2], wr_s2.ten[:, :], axis=AXX),
                    reads=[wr_s2], writes=[red])
                S.step("vector", lambda e: e.tensor_scalar(
                    red.ten[:, 1:2], red.ten[:, 1:2], 1e-20, None, ALU.add),
                    reads=[red], writes=[red])
                S.step("vector", lambda e: e.reciprocal(
                    red.ten[:, 2:3], red.ten[:, 1:2]),
                    reads=[red], writes=[red])
                S.step("vector", lambda e, ti=ti: e.tensor_scalar(
                    comb_bf.ten[:, :], wr_s2.ten[:, :],
                    red.ten[:, 2:3], None, ALU.mult),
                    reads=[wr_s2, red], writes=[comb_bf])
                p = psT[ti % 2]
                S.step("tensor", lambda e, p=p: e.transpose(
                    p.ten[0:16, 0:128], comb_bf.ten[:, 0:16], ident_b.ten[:, :]),
                    reads=[comb_bf, ident_b], writes=[p])
                S.step("scalar", lambda e, ti=ti, p=p: e.copy(
                    combT.ten[0:16, ti * 128:(ti + 1) * 128], p.ten[0:16, 0:128]),
                    reads=[p], writes=[combT])

        with mem.scope(S) as m2:
            w_s = m2.sb("p5_ws", [128, NCB * D], BF16)
            swT = m2.sb("p5_swT", [128, NCB * (HSH // 2)], BF16)
            hwsh = m2.sb("p5_hwsh", [128, 8 * TOK], BF16)
            sdwT = m2.sb("p5_sdwT", [128, 8 * D], BF16)
            for hh in range(2):
                for wi_, wio in ((0, "shared_gate_w"), (1, "shared_up_w")):
                    S.dma("gpsimd", w_s.ten.ap().rearrange("p (n c) -> p n c", n=NCB),
                          ios[wio][hh * (HSH // 2):(hh + 1) * (HSH // 2), :]
                              .rearrange("(n p) c -> p n c", p=128),
                          writes=[w_s])
                    transpose_tiles(S, m2, w_s, 0, swT, 0, NCB, NCB, D, HSH // 2,
                                    ident_b, psT)
                    for ht in range(8):
                        for tc in range(4):
                            psum = psA if tc % 2 == 0 else psB
                            items = [(swT[:, cb * (HSH // 2) + ht * 128:
                                          cb * (HSH // 2) + (ht + 1) * 128],
                                      xfT[:, cb * TOK + tc * 512: cb * TOK + (tc + 1) * 512])
                                     for cb in range(NCB)]
                            S.mm_group(items, psum, reads=[swT, xfT])
                            if wi_ == 0:
                                S.step("scalar", lambda e, ht=ht, tc=tc, psum=psum: e.activation(
                                    hwsh.ten[:, ht * TOK + tc * 512: ht * TOK + (tc + 1) * 512],
                                    psum.ten[:, :], AF.Silu),
                                    reads=[psum], writes=[hwsh])
                            else:
                                S.step("vector", lambda e, ht=ht, tc=tc, psum=psum: e.tensor_mul(
                                    hwsh.ten[:, ht * TOK + tc * 512: ht * TOK + (tc + 1) * 512],
                                    psum.ten[:, :],
                                    hwsh.ten[:, ht * TOK + tc * 512: ht * TOK + (tc + 1) * 512]),
                                    reads=[psum], writes=[hwsh])
                S.dma("gpsimd", w_s.ten.ap().rearrange("p (n c) -> p n c", n=NCB),
                      ios["shared_down_w"][:, hh * (HSH // 2):(hh + 1) * (HSH // 2)]
                          .rearrange("(n p) c -> p n c", p=128),
                      writes=[w_s])
                transpose_tiles(S, m2, w_s, 0, sdwT, 0, NCB, 8, D, D, ident_b, psT)
                for ti in range(NT):
                    for oc in range(2):
                        psum = psA if oc == 0 else psB
                        items = [(hwsh[:, ht * TOK + ti * 128: ht * TOK + (ti + 1) * 128],
                                  sdwT[:, ht * D + oc * 512: ht * D + (oc + 1) * 512])
                                 for ht in range(8)]
                        S.mm_group(items, psum, reads=[hwsh, sdwT])
                        S.step("vector", lambda e, psum=psum, ti=ti, oc=oc: e.tensor_add(
                            acc[:, ti * D + oc * 512: ti * D + (oc + 1) * 512],
                            psum.ten[:, :],
                            acc[:, ti * D + oc * 512: ti * D + (oc + 1) * 512]),
                            reads=[psum], writes=[acc])

        with mem.scope(S) as m3:
            gw_t = [m3.sb(f"p5_gw{i}", [128, NCB * H_EXP], BF16) for i in range(2)]
            uw_t = [m3.sb(f"p5_uw{i}", [128, NCB * H_EXP], BF16) for i in range(2)]
            dw_t = [m3.sb(f"p5_dw{i}", [128, 2 * D], BF16) for i in range(2)]
            esel = m3.sb("p5_esel", [16, 16 * 128], BF16)
            ge_s = m3.sb("p5_ge", [128, 2 * TOK], BF16)
            hw_t = [m3.sb(f"p5_hw{i}", [128, 2 * TOK], BF16) for i in range(2)]
            psD = [m3.ps(f"p5_psD{i}", [128, 512]) for i in range(2)]
            S.dma("gpsimd", esel[:, :], ios["esel"], writes=[esel])

            def down_emit(ex, hw_s, dw_s):
                for ti in range(NT):
                    for oc in range(2):
                        psum = psD[oc]
                        items = [(hw_s[:, hb * TOK + ti * 128: hb * TOK + (ti + 1) * 128],
                                  dw_s[:, hb * D + oc * 512: hb * D + (oc + 1) * 512])
                                 for hb in range(2)]
                        S.mm_group(items, psum, reads=[hw_s, dw_s])
                        S.step("vector", lambda e, psum=psum, ti=ti, oc=oc: e.tensor_add(
                            acc[:, ti * D + oc * 512: ti * D + (oc + 1) * 512],
                            psum.ten[:, :],
                            acc[:, ti * D + oc * 512: ti * D + (oc + 1) * 512]),
                            reads=[psum], writes=[acc])

            pend = None
            for ex in range(NEXP):
                gw_s = gw_t[ex % 2]
                uw_s = uw_t[ex % 2]
                dw_s = dw_t[ex % 2]
                hw_s = hw_t[ex % 2]
                S.dma("gpsimd", gw_s.ten.ap().rearrange("p (n c) -> p n c", n=NCB),
                      ios["gate_w"][ex].rearrange("(n p) c -> p n c", p=128), writes=[gw_s])
                S.dma("gpsimd", uw_s.ten.ap().rearrange("p (n c) -> p n c", n=NCB),
                      ios["up_w"][ex].rearrange("(n p) c -> p n c", p=128), writes=[uw_s])
                S.dma("gpsimd", dw_s.ten.ap().rearrange("p (n c) -> p n c", n=2),
                      ios["down_w"][ex].rearrange("(n p) c -> p n c", p=128), writes=[dw_s])
                for tc in range(4):
                    S.mm_group([(esel[0:16, ex * 128:(ex + 1) * 128],
                                 combT[0:16, tc * 512:(tc + 1) * 512])], psC,
                               reads=[esel, combT])
                    for hb in range(2):
                        items = [(gw_s[:, cb * H_EXP + hb * 128: cb * H_EXP + (hb + 1) * 128],
                                  xfT[:, cb * TOK + tc * 512: cb * TOK + (tc + 1) * 512])
                                 for cb in range(NCB)]
                        S.mm_group(items, psA, reads=[gw_s, xfT])
                        S.step("scalar", lambda e, hb=hb, tc=tc: e.activation(
                            ge_s.ten[:, hb * TOK + tc * 512: hb * TOK + (tc + 1) * 512],
                            psA.ten[:, :], AF.Silu),
                            reads=[psA], writes=[ge_s])
                        items = [(uw_s[:, cb * H_EXP + hb * 128: cb * H_EXP + (hb + 1) * 128],
                                  xfT[:, cb * TOK + tc * 512: cb * TOK + (tc + 1) * 512])
                                 for cb in range(NCB)]
                        S.mm_group(items, psB, reads=[uw_s, xfT])
                        S.step("vector", lambda e, hb=hb, tc=tc: e.tensor_mul(
                            ge_s.ten[:, hb * TOK + tc * 512: hb * TOK + (tc + 1) * 512],
                            psB.ten[:, :],
                            ge_s.ten[:, hb * TOK + tc * 512: hb * TOK + (tc + 1) * 512]),
                            reads=[psB], writes=[ge_s])
                        S.step("vector", lambda e, hb=hb, tc=tc, hw_s=hw_s: e.tensor_mul(
                            hw_s.ten[:, hb * TOK + tc * 512: hb * TOK + (tc + 1) * 512],
                            psC.ten[:, :],
                            ge_s.ten[:, hb * TOK + tc * 512: hb * TOK + (tc + 1) * 512]),
                            reads=[psC, ge_s], writes=[hw_s])
                if pend is not None:
                    down_emit(*pend)
                pend = (ex, hw_s, dw_s)
            down_emit(*pend)

        S.dma("sync", ios["out"].rearrange("(n p) c -> p n c", p=128),
              acc.ten.ap().rearrange("p (n c) -> p n c", n=NT), reads=[acc])


def build_p5():
    nc = bass.Bass("TRN2", target_bir_lowering=False, debug=False)
    ios = {"const": {}}
    for name, shape, dt in CONST_SPECS:
        ios["const"][name] = nc.dram_tensor("c_" + name, shape, dt, kind="ExternalInput").ap()
    ios["x2"] = nc.dram_tensor("x2", [TOK, D], F32, kind="ExternalInput").ap()
    ios["yg_g"] = nc.dram_tensor("yg_g", [NCORES, 256, TOK], BF16, kind="ExternalInput").ap()
    ios["W_out"] = nc.dram_tensor("W_out", [D_IN, D], F32, kind="ExternalInput").ap()
    ios["router_w"] = nc.dram_tensor("router_w", [16, D], F32, kind="ExternalInput").ap()
    ios["corr_bias"] = nc.dram_tensor("corr_bias", [16], F32, kind="ExternalInput").ap()
    ios["gate_w"] = nc.dram_tensor("gate_w", [NEXP, D, H_EXP], F32, kind="ExternalInput").ap()
    ios["up_w"] = nc.dram_tensor("up_w", [NEXP, D, H_EXP], F32, kind="ExternalInput").ap()
    ios["down_w"] = nc.dram_tensor("down_w", [NEXP, H_EXP, D], F32, kind="ExternalInput").ap()
    ios["shared_gate_w"] = nc.dram_tensor("shared_gate_w", [HSH, D], F32, kind="ExternalInput").ap()
    ios["shared_up_w"] = nc.dram_tensor("shared_up_w", [HSH, D], F32, kind="ExternalInput").ap()
    ios["shared_down_w"] = nc.dram_tensor("shared_down_w", [D, HSH], F32, kind="ExternalInput").ap()
    ios["esel"] = nc.dram_tensor("esel", [16, 16 * 128], BF16, kind="ExternalInput").ap()
    ios["out"] = nc.dram_tensor("out", [TOK, D], F32, kind="ExternalOutput").ap()
    with contextlib.ExitStack() as ctx:
        mem = Mem(nc, ctx)
        S = Sched(nc)
        emit_p5(S, mem, ios)
        S.step("gpsimd", lambda e: e.nop(), waits=S.dma_barrier(), sem=None)
        block = ctx.enter_context(nc.Block())
        S.build(block)
    S.close()
    return nc


def p5_host_consts():
    import ml_dtypes
    esel = np.zeros((16, 16 * 128), np.float32)
    for ex in range(16):
        esel[ex, ex * 128:(ex + 1) * 128] = 1.0
    return esel.astype(ml_dtypes.bfloat16)



NTILE = D_IN // 128


def emit_p4a(S, mem, ios):
    NG = 16
    F = TOK
    dd_tok = ios.get("_dltdu_tok", {})
    bc_tok = ios.get("_bc_tok", S.dma_barrier())
    with mem.scope(S) as m1:
        e8 = m1.sb("a_e8", [128, NG * 128], BF16)
        ws = m1.sb("a_ws", [128, NG * 128], BF16)
        e16 = m1.sb("a_e16", [32, 256], BF16)
        alog = m1.sb("a_alog", [128, 256])
        alan = m1.sb("a_alan", [128, 256])
        bc_b = m1.sb("a_bc", [32, F], BF16)
        brep = m1.sb("a_brep", [128, F])
        crep = m1.sb("a_crep", [128, F])
        dlt_t = [m1.sb(f"a_dlt{i}", [128, F], BF16) for i in range(2)]
        du_t = [m1.sb(f"a_du{i}", [128, F], BF16) for i in range(2)]
        ones_b = m1.sb("a_ones", [128, F])
        dltf = m1.sb("a_dltf", [128, F])
        cdf = m1.sb("a_cdf", [128, F])
        cdbf_t = [m1.sb(f"a_cdbf{i}", [128, F], BF16) for i in range(2)]
        a_t = [m1.sb(f"a_a{i}", [128, F]) for i in range(2)]
        bu_t = [m1.sb(f"a_bu{i}", [128, F]) for i in range(2)]
        b_t = [m1.sb(f"a_b{i}", [128, F]) for i in range(2)]
        s_t = [m1.sb(f"a_s{i}", [128, F]) for i in range(3)]
        sc_t = [m1.sb(f"a_sc{i}", [128, F], BF16) for i in range(3)]
        y_t = [m1.sb(f"a_y{i}", [128, F], BF16) for i in range(2)]
        carry = m1.sb("a_carry", [128, NG])
        zcol = m1.sb("a_zcol", [128, 1])
        f_sb = m1.sb("a_f", [128, 256])
        q_sb = m1.sb("a_q", [128, 256])
        qtmp = m1.sb("a_qtmp", [128, NG])
        psE = [m1.ps(f"a_psE{i}", [128, 512]) for i in range(4)]
        psY = m1.ps("a_psY", [128, 2048])
        psU = psE

        S.dma("gpsimd", e8[:, :], ios["e8sel"], writes=[e8])
        S.dma("gpsimd", ws[:, :], ios["wsel"], writes=[ws])
        S.dma("gpsimd", e16[:, :], ios["e16sel"], writes=[e16])
        S.dma("gpsimd", alog[:, :], ios["alog_lanes"], writes=[alog])
        S.dma("gpsimd", bc_b[:, :], ios["bc"], writes=[bc_b], waits=bc_tok)
        S.step("scalar", lambda e: e.activation(alan.ten[:, :], alog.ten[:, :], AF.Exp),
               reads=[alog], writes=[alan])
        S.step("vector", lambda e: e.tensor_scalar(alan.ten[:, :], alan.ten[:, :],
                                                   -1.0, None, ALU.mult),
               reads=[alan], writes=[alan])
        S.step("vector", lambda e: e.memset(ones_b.ten[:, :], 1.0), writes=[ones_b])
        S.step("vector", lambda e: e.memset(zcol.ten[:, :], 0.0), writes=[zcol])

        for which, rep in ((0, brep), (1, crep)):
            for sub in range(4):
                ps = psE[sub]
                S.mm_group([(e16[0:32, which * 128:(which + 1) * 128],
                             bc_b[0:32, sub * 512:(sub + 1) * 512])],
                           ps, reads=[e16, bc_b])
                S.step("scalar", lambda e, rep=rep, sub=sub, ps=ps: e.copy(
                    rep.ten[:, sub * 512:(sub + 1) * 512], ps.ten[:, :]),
                    reads=[ps], writes=[rep])

        for T in range(NTILE):
            dlt = dlt_t[T % 2]
            du = du_t[T % 2]
            cdbf = cdbf_t[T % 2]
            yb = y_t[T % 2]
            tw = dd_tok.get(T, [])
            S.dma("sync", dlt[:, :], ios["dlt"][T * 128:(T + 1) * 128, :],
                  writes=[dlt], waits=tw)
            S.dma("sync", du[:, :], ios["du"][T * 128:(T + 1) * 128, :],
                  writes=[du], waits=tw)
            S.step("scalar", lambda e, dlt=dlt: e.copy(
                dltf.ten[:, :], dlt.ten[:, :]),
                reads=[dlt], writes=[dltf])
            S.step("vector", lambda e: e.tensor_tensor_scan(
                cdf.ten[:, :], ones_b.ten[:, :], dltf.ten[:, :],
                zcol.ten[:, 0:1], ALU.mult, ALU.add),
                reads=[ones_b, dltf, zcol], writes=[cdf])
            S.step("gpsimd", lambda e, cdbf=cdbf: e.tensor_copy(
                cdbf.ten[:, :], cdf.ten[:, :]),
                reads=[cdf], writes=[cdbf])
            S.dma("scalar", ios["cd"][T * 128:(T + 1) * 128, :], cdbf[:, :],
                  reads=[cdbf])
            for g in range(NG):
                S.mm_group([(e8[:, g * 128:(g + 1) * 128],
                             cdbf[:, F - 1:F])], psU[0],
                           reads=[e8, cdbf],
                           psum_slice=(slice(None), slice(g, g + 1)))
            S.step("vector", lambda e, T=T: e.tensor_mul(
                qtmp.ten[:, :], psU[0].ten[:, 0:NG],
                alan.ten[:, T * NG:(T + 1) * NG]),
                reads=[psU[0], alan], writes=[qtmp])
            S.step("scalar", lambda e, T=T: e.activation(
                q_sb.ten[:, T * NG:(T + 1) * NG], qtmp.ten[:, :], AF.Exp),
                reads=[qtmp], writes=[q_sb])
            ystart = [False] * 4

            def yemit(g, sc_sb):
                for sub in range(4):
                    wl = []
                    if not ystart[sub]:
                        if psY.wr is not None:
                            wl.append(psY.wr)
                        wl.extend(psY.rds)
                    if sc_sb.wr is not None:
                        wl.append(sc_sb.wr)
                    if ws.wr is not None:
                        wl.append(ws.wr)
                    toky = S._record(
                        "tensor",
                        lambda e, g=g, sub=sub, s=(not ystart[sub]), sc_sb=sc_sb,
                               last=(g == NG - 1): e.matmul(
                            psY.ten[:, sub * 512:(sub + 1) * 512],
                            ws.ten[:, g * 128:(g + 1) * 128],
                            sc_sb.ten[:, sub * 512:(sub + 1) * 512],
                            start=s, stop=last, skip_group_check=True),
                        wl, "pe", 1)
                    ystart[sub] = True
                    sc_sb.rds.append(toky)
                    ws.rds.append(toky)
                    if g == NG - 1 and sub == 3:
                        psY.wr = toky
                        psY.rds = []

            pend = []
            for g in range(NG):
                col = T * NG + g
                a_sb = a_t[g % 2]
                bu_sb = bu_t[g % 2]
                b_sb = b_t[g % 2]
                s_sb = s_t[g % 3]
                sc_sb = sc_t[g % 3]
                for sub in range(4):
                    ps = psE[sub]
                    S.mm_group([(e8[:, g * 128:(g + 1) * 128],
                                 dlt[:, sub * 512:(sub + 1) * 512])],
                               ps, reads=[e8, dlt])
                    S.step("scalar", lambda e, sub=sub, ps=ps, col=col, a_sb=a_sb: e.activation(
                        a_sb.ten[:, sub * 512:(sub + 1) * 512], ps.ten[:, :],
                        AF.Exp, scale=alan.ten[:, col:col + 1]),
                        reads=[ps, alan], writes=[a_sb])
                for sub in range(4):
                    ps = psE[sub]
                    S.mm_group([(e8[:, g * 128:(g + 1) * 128],
                                 du[:, sub * 512:(sub + 1) * 512])],
                               ps, reads=[e8, du])
                    S.step("vector", lambda e, sub=sub, ps=ps, b_sb=b_sb: e.tensor_mul(
                        b_sb.ten[:, sub * 512:(sub + 1) * 512], ps.ten[:, :],
                        brep.ten[:, sub * 512:(sub + 1) * 512]),
                        reads=[ps, brep], writes=[b_sb])
                if len(pend) >= 2:
                    yemit(*pend.pop(0))
                S.step("vector", lambda e, a_sb=a_sb, b_sb=b_sb, s_sb=s_sb: e.tensor_tensor_scan(
                    s_sb.ten[:, :], a_sb.ten[:, :], b_sb.ten[:, :],
                    zcol.ten[:, 0:1], ALU.mult, ALU.add),
                    reads=[a_sb, b_sb, zcol], writes=[s_sb])
                S.step("scalar", lambda e, col=col, s_sb=s_sb: e.copy(
                    f_sb.ten[:, col:col + 1], s_sb.ten[:, F - 1:F]),
                    reads=[s_sb], writes=[f_sb])
                S.step("gpsimd", lambda e, s_sb=s_sb, sc_sb=sc_sb: e.tensor_mul(
                    sc_sb.ten[:, :], s_sb.ten[:, :], crep.ten[:, :]),
                    reads=[s_sb, crep], writes=[sc_sb])
                pend.append((g, sc_sb))
            for pp in pend:
                yemit(*pp)
            S.step("scalar", lambda e, yb=yb: e.copy(yb.ten[:, :], psY.ten[:, :]),
                   reads=[psY], writes=[yb])
            S.dma("sync", ios["y_local"][T * 128:(T + 1) * 128, :], yb[:, :],
                  reads=[yb])
        S.dma("sync", ios["fq"][0], f_sb[:, :], reads=[f_sb])
        S.dma("sync", ios["fq"][1], q_sb[:, :], reads=[q_sb])



def emit_p4b(S, mem, ios):
    NG = 16
    F = TOK
    with mem.scope(S) as m1:
        e8 = m1.sb("b_e8", [128, NG * 128], BF16)
        ws = m1.sb("b_ws", [128, NG * 128], BF16)
        e16 = m1.sb("b_e16", [32, 256], BF16)
        alog = m1.sb("b_alog", [128, 256])
        alan = m1.sb("b_alan", [128, 256])
        dcol = m1.sb("b_dcol", [128, NTILE])
        oh4 = m1.sb("b_oh4", [128, 4])
        fqg = m1.sb("b_fqg", [128, 4 * 2 * 256])
        bc_b = m1.sb("b_bc", [32, F], BF16)
        crep = m1.sb("b_crep", [128, F])
        acc = m1.sb("b_acc", [128, 256])
        sel = m1.sb("b_sel", [128, 256])
        sfin = m1.sb("b_sfin", [128, 256])
        sfbf = m1.sb("b_sfbf", [128, 256], BF16)
        tmp = m1.sb("b_tmp", [128, 256])
        cd_t = [m1.sb(f"b_cd{i}", [128, F], BF16) for i in range(2)]
        y_t = [m1.sb(f"b_y{i}", [128, F], BF16) for i in range(2)]
        sres_t = [m1.sb(f"b_sres{i}", [128, F], BF16) for i in range(2)]
        p_t = [m1.sb(f"b_p{i}", [128, F]) for i in range(3)]
        sc_t = [m1.sb(f"b_sc{i}", [128, F], BF16) for i in range(3)]
        yg_t = [m1.sb(f"b_yg{i}", [128, F], BF16) for i in range(2)]
        yf = m1.sb("b_yf", [128, F])
        rcol = m1.sb("b_rcol", [128, 1])
        psE = [m1.ps(f"b_psE{i}", [128, 512]) for i in range(3)]
        psY = m1.ps("b_psY", [128, 2048])
        psR = m1.ps("b_psR", [128, 512])

        S.dma("gpsimd", e8[:, :], ios["e8sel"], writes=[e8])
        S.dma("gpsimd", ws[:, :], ios["wsel"], writes=[ws])
        S.dma("gpsimd", e16[:, :], ios["e16sel"], writes=[e16])
        S.dma("gpsimd", alog[:, :], ios["alog_lanes"], writes=[alog])
        S.dma("gpsimd", dcol[:, :], ios["d_col"], writes=[dcol])
        S.dma("gpsimd", oh4[:, :], ios["oh4"], writes=[oh4])
        S.dma("sync", fqg.ten.ap().rearrange("p (j two c) -> p j two c", j=4, two=2),
              ios["fq_g"].rearrange("j two p c -> p j two c"), writes=[fqg])
        S.dma("sync", bc_b[:, :], ios["bc"], writes=[bc_b])
        S.step("scalar", lambda e: e.activation(alan.ten[:, :], alog.ten[:, :], AF.Exp),
               reads=[alog], writes=[alan])
        S.step("vector", lambda e: e.tensor_scalar(alan.ten[:, :], alan.ten[:, :],
                                                   -1.0, None, ALU.mult),
               reads=[alan], writes=[alan])
        for sub in range(4):
            ps = psE[sub % 3]
            S.mm_group([(e16[0:32, 128:256],
                         bc_b[0:32, sub * 512:(sub + 1) * 512])],
                       ps, reads=[e16, bc_b])
            S.step("scalar", lambda e, sub=sub, ps=ps: e.copy(
                crep.ten[:, sub * 512:(sub + 1) * 512], ps.ten[:, :]),
                reads=[ps], writes=[crep])
        S.step("vector", lambda e: e.memset(acc.ten[:, :], 0.0), writes=[acc])
        S.step("vector", lambda e: e.memset(sel.ten[:, :], 0.0), writes=[sel])
        for k in range(4):
            if k > 0:
                fj = fqg[:, (k - 1) * 512:(k - 1) * 512 + 256]
                qj = fqg[:, (k - 1) * 512 + 256:(k - 1) * 512 + 512]
                S.step("vector", lambda e, qj=qj: e.tensor_mul(
                    acc.ten[:, :], acc.ten[:, :], qj),
                    reads=[fqg], writes=[acc])
                S.step("vector", lambda e, fj=fj: e.tensor_add(
                    acc.ten[:, :], acc.ten[:, :], fj),
                    reads=[fqg], writes=[acc])
            S.step("vector", lambda e, k=k: e.tensor_scalar(
                tmp.ten[:, :], acc.ten[:, :], oh4.ten[:, k:k + 1], None, ALU.mult),
                reads=[acc, oh4], writes=[tmp])
            S.step("vector", lambda e: e.tensor_add(
                sel.ten[:, :], sel.ten[:, :], tmp.ten[:, :]),
                reads=[tmp], writes=[sel])
        fj = fqg[:, 3 * 512:3 * 512 + 256]
        qj = fqg[:, 3 * 512 + 256:4 * 512]
        S.step("vector", lambda e, qj=qj: e.tensor_mul(
            sfin.ten[:, :], acc.ten[:, :], qj),
            reads=[acc, fqg], writes=[sfin])
        S.step("vector", lambda e, fj=fj: e.tensor_add(
            sfin.ten[:, :], sfin.ten[:, :], fj),
            reads=[fqg], writes=[sfin])
        S.step("vector", lambda e: e.tensor_copy(sfbf.ten[:, :], sfin.ten[:, :]),
               reads=[sfin], writes=[sfbf])

        for T in range(NTILE):
            cd = cd_t[T % 2]
            yb = y_t[T % 2]
            sres = sres_t[T % 2]
            yg = yg_t[T % 2]
            S.dma("sync", cd[:, :], ios["cd"][T * 128:(T + 1) * 128, :], writes=[cd])
            S.dma("sync", yb[:, :], ios["y_local"][T * 128:(T + 1) * 128, :], writes=[yb])
            S.dma("sync", sres[:, :], ios["sres"][T * 128:(T + 1) * 128, :], writes=[sres])
            ystart = [False] * 4

            def yemit(g, sc_sb):
                for sub in range(4):
                    wl = []
                    if not ystart[sub]:
                        if psY.wr is not None:
                            wl.append(psY.wr)
                        wl.extend(psY.rds)
                    if sc_sb.wr is not None:
                        wl.append(sc_sb.wr)
                    if ws.wr is not None:
                        wl.append(ws.wr)
                    toky = S._record(
                        "tensor",
                        lambda e, g=g, sub=sub, s=(not ystart[sub]), sc_sb=sc_sb,
                               last=(g == NG - 1): e.matmul(
                            psY.ten[:, sub * 512:(sub + 1) * 512],
                            ws.ten[:, g * 128:(g + 1) * 128],
                            sc_sb.ten[:, sub * 512:(sub + 1) * 512],
                            start=s, stop=last, skip_group_check=True),
                        wl, "pe", 1)
                    ystart[sub] = True
                    sc_sb.rds.append(toky)
                    ws.rds.append(toky)
                    if g == NG - 1 and sub == 3:
                        psY.wr = toky
                        psY.rds = []

            pend = []
            for g in range(NG):
                col = T * NG + g
                p_sb = p_t[g % 3]
                sc_sb = sc_t[g % 3]
                for sub in range(4):
                    ps = psE[sub % 3]
                    S.mm_group([(e8[:, g * 128:(g + 1) * 128],
                                 cd[:, sub * 512:(sub + 1) * 512])],
                               ps, reads=[e8, cd])
                    S.step("scalar", lambda e, sub=sub, ps=ps, col=col, p_sb=p_sb: e.activation(
                        p_sb.ten[:, sub * 512:(sub + 1) * 512], ps.ten[:, :],
                        AF.Exp, scale=alan.ten[:, col:col + 1]),
                        reads=[ps, alan], writes=[p_sb])
                if len(pend) >= 2:
                    yemit(*pend.pop(0))
                S.step("vector", lambda e, col=col, p_sb=p_sb: e.tensor_scalar(
                    p_sb.ten[:, :], p_sb.ten[:, :], sel.ten[:, col:col + 1],
                    None, ALU.mult),
                    reads=[sel], writes=[p_sb])
                S.step("vector", lambda e, p_sb=p_sb, sc_sb=sc_sb: e.tensor_mul(
                    sc_sb.ten[:, :], p_sb.ten[:, :], crep.ten[:, :]),
                    reads=[p_sb, crep], writes=[sc_sb])
                pend.append((g, sc_sb))
            for pp in pend:
                yemit(*pp)
            items = [(ws[:, g * 128:(g + 1) * 128],
                      sfbf[:, (T * NG + g):(T * NG + g) + 1]) for g in range(NG)]
            S.mm_group(items, psR, reads=[ws, sfbf],
                       psum_slice=(slice(None), slice(0, 1)))
            S.step("vector", lambda e, T=T: e.tensor_mul(
                rcol.ten[:, 0:1], psR.ten[:, 0:1], dcol.ten[:, T:T + 1]),
                reads=[psR, dcol], writes=[rcol])
            S.step("vector", lambda e, yb=yb: e.tensor_add(
                yf.ten[:, :], psY.ten[:, :], yb.ten[:, :]),
                reads=[psY, yb], writes=[yf])
            S.step("scalar", lambda e: e.add(
                yf.ten[:, :], yf.ten[:, :], rcol.ten[:, 0:1]),
                reads=[rcol], writes=[yf])
            S.step("vector", lambda e, yg=yg, sres=sres: e.tensor_mul(
                yg.ten[:, :], yf.ten[:, :], sres.ten[:, :]),
                reads=[yf, sres], writes=[yg])
            S.dma("sync", ios["ygate"][T * 128:(T + 1) * 128, :], yg[:, :],
                  reads=[yg])


def build_p234():
    nc = bass.Bass("TRN2", target_bir_lowering=False, debug=False)
    ios = {"const": {}}
    for name, shape, dt in CONST_SPECS:
        ios["const"][name] = nc.dram_tensor("c_" + name, shape, dt, kind="ExternalInput").ap()
    ios["x"] = nc.dram_tensor("x", [TOK, D], F32, kind="ExternalInput").ap()
    ios["ao"] = nc.dram_tensor("ao", [NCORES, 128, TOK], BF16, kind="ExternalInput").ap()
    ios["ao_h"] = nc.dram_tensor("ao_h", [NCORES, 128, 3], BF16, kind="ExternalInput").ap()
    ios["x_h"] = nc.dram_tensor("x_h", [3, D], F32, kind="ExternalInput").ap()
    ios["hmask"] = nc.dram_tensor("hmask", [3, 1], F32, kind="ExternalInput").ap()
    ios["wo"] = nc.dram_tensor("wo", [D, D], F32, kind="ExternalInput").ap()
    ios["M"] = nc.dram_tensor("M", [D, D], F32, kind="ExternalInput").ap()
    ios["ffn_bias"] = nc.dram_tensor("ffn_bias", [D], F32, kind="ExternalInput").ap()
    ios["x2"] = nc.dram_tensor("x2", [TOK, D], F32, kind="ExternalOutput").ap()
    ios["W_in"] = nc.dram_tensor("W_in", [D, 2 * D_IN], F32, kind="ExternalInput").ap()
    ios["conv_w"] = nc.dram_tensor("conv_w", [4, D_IN, D_IN], F32, kind="ExternalInput").ap()
    ios["conv_b"] = nc.dram_tensor("conv_b", [D_IN], F32, kind="ExternalInput").ap()
    ios["W_x"] = nc.dram_tensor("W_x", [D_IN, 96], F32, kind="ExternalInput").ap()
    ios["W_dt"] = nc.dram_tensor("W_dt", [DT_RANK, D_IN], F32, kind="ExternalInput").ap()
    ios["b_dt"] = nc.dram_tensor("b_dt", [D_IN], F32, kind="ExternalInput").ap()
    ios["e8sel"] = nc.dram_tensor("e8sel", [128, 16 * 128], BF16, kind="ExternalInput").ap()
    ios["wsel"] = nc.dram_tensor("wsel", [128, 16 * 128], BF16, kind="ExternalInput").ap()
    ios["e16sel"] = nc.dram_tensor("e16sel", [32, 256], BF16, kind="ExternalInput").ap()
    ios["alog_lanes"] = nc.dram_tensor("alog_lanes", [128, 256], F32, kind="ExternalInput").ap()
    ios["sres"] = nc.dram_tensor("sres", [D_IN, TOK], BF16, kind="ExternalOutput").ap()
    ios["bc"] = nc.dram_tensor("bc", [32, TOK], BF16, kind="ExternalOutput").ap()
    ios["y_local"] = nc.dram_tensor("y_local", [D_IN, TOK], BF16, kind="ExternalOutput").ap()
    ios["cd"] = nc.dram_tensor("cd", [D_IN, TOK], BF16, kind="ExternalOutput").ap()
    ios["fq"] = nc.dram_tensor("fq", [2, 128, 256], F32, kind="ExternalOutput").ap()
    ios["dlt"] = nc.dram_tensor("dlt", [D_IN, TOK], BF16, kind="Internal").ap()
    ios["du"] = nc.dram_tensor("du", [D_IN, TOK], BF16, kind="Internal").ap()
    ios["halo_scr"] = nc.dram_tensor("halo_scr", [3, D], F32, kind="Internal").ap()
    ios["u_scr"] = nc.dram_tensor("u_scr", [D_IN, TOK], BF16, kind="Internal").ap()
    with contextlib.ExitStack() as ctx:
        mem = Mem(nc, ctx)
        S = Sched(nc)
        emit_p2(S, mem, ios)
        S.set_fence()
        emit_p3(S, mem, ios)
        emit_p4a(S, mem, ios)
        S.step("gpsimd", lambda e: e.nop(), waits=S.dma_barrier(), sem=None)
        block = ctx.enter_context(nc.Block())
        S.build(block)
    S.close()
    return nc


def build_p45():
    nc = bass.Bass("TRN2", target_bir_lowering=False, debug=False)
    ios = {"const": {}}
    for name, shape, dt in CONST_SPECS:
        ios["const"][name] = nc.dram_tensor("c_" + name, shape, dt, kind="ExternalInput").ap()
    ios["x2"] = nc.dram_tensor("x2", [TOK, D], F32, kind="ExternalInput").ap()
    ios["fq_g"] = nc.dram_tensor("fq_g", [4, 2, 128, 256], F32, kind="ExternalInput").ap()
    ios["oh4"] = nc.dram_tensor("oh4", [128, 4], F32, kind="ExternalInput").ap()
    ios["cd"] = nc.dram_tensor("cd", [D_IN, TOK], BF16, kind="ExternalInput").ap()
    ios["y_local"] = nc.dram_tensor("y_local", [D_IN, TOK], BF16, kind="ExternalInput").ap()
    ios["sres"] = nc.dram_tensor("sres", [D_IN, TOK], BF16, kind="ExternalInput").ap()
    ios["bc"] = nc.dram_tensor("bc", [32, TOK], BF16, kind="ExternalInput").ap()
    ios["e8sel"] = nc.dram_tensor("e8sel", [128, 16 * 128], BF16, kind="ExternalInput").ap()
    ios["wsel"] = nc.dram_tensor("wsel", [128, 16 * 128], BF16, kind="ExternalInput").ap()
    ios["e16sel"] = nc.dram_tensor("e16sel", [32, 256], BF16, kind="ExternalInput").ap()
    ios["alog_lanes"] = nc.dram_tensor("alog_lanes", [128, 256], F32, kind="ExternalInput").ap()
    ios["d_col"] = nc.dram_tensor("d_col", [128, NTILE], F32, kind="ExternalInput").ap()
    ios["W_out"] = nc.dram_tensor("W_out", [D_IN, D], F32, kind="ExternalInput").ap()
    ios["router_w"] = nc.dram_tensor("router_w", [16, D], F32, kind="ExternalInput").ap()
    ios["corr_bias"] = nc.dram_tensor("corr_bias", [16], F32, kind="ExternalInput").ap()
    ios["gate_w"] = nc.dram_tensor("gate_w", [NEXP, D, H_EXP], F32, kind="ExternalInput").ap()
    ios["up_w"] = nc.dram_tensor("up_w", [NEXP, D, H_EXP], F32, kind="ExternalInput").ap()
    ios["down_w"] = nc.dram_tensor("down_w", [NEXP, H_EXP, D], F32, kind="ExternalInput").ap()
    ios["shared_gate_w"] = nc.dram_tensor("shared_gate_w", [HSH, D], F32, kind="ExternalInput").ap()
    ios["shared_up_w"] = nc.dram_tensor("shared_up_w", [HSH, D], F32, kind="ExternalInput").ap()
    ios["shared_down_w"] = nc.dram_tensor("shared_down_w", [D, HSH], F32, kind="ExternalInput").ap()
    ios["esel"] = nc.dram_tensor("esel", [16, 16 * 128], BF16, kind="ExternalInput").ap()
    ios["out"] = nc.dram_tensor("out", [TOK, D], F32, kind="ExternalOutput").ap()
    ygate_t = nc.dram_tensor("ygate", [D_IN, TOK], BF16, kind="Internal")
    ios["ygate"] = ygate_t.ap()
    with contextlib.ExitStack() as ctx:
        mem = Mem(nc, ctx)
        S = Sched(nc)
        emit_p4b(S, mem, ios)
        ios["yg_g"] = ygate_t.ap().rearrange("(j p) t -> j p t", j=NCORES)
        S.set_fence()
        emit_p5(S, mem, ios)
        S.step("gpsimd", lambda e: e.nop(), waits=S.dma_barrier(), sem=None)
        block = ctx.enter_context(nc.Block())
        S.build(block)
    S.close()
    return nc



def kernel(**inputs):
    import ml_dtypes
    x_full = np.ascontiguousarray(np.asarray(inputs["L0_attn_x"], np.float32)
                                  .reshape(BATCH * T_FULL, D))
    consts = [build_constants(c) for c in range(NCORES)]
    e8, ws, e16 = p4_host_consts()
    esel = p5_host_consts()

    def cmaps(extra_fn):
        maps = []
        for c in range(NCORES):
            m = {"c_" + k: v for k, v in consts[c].items()}
            m.update(extra_fn(c))
            maps.append(m)
        return maps

    nc = build_p0()
    r0 = run_bass_kernel_spmd(nc, cmaps(lambda c: {
        "x": np.ascontiguousarray(x_full[c * TOK:(c + 1) * TOK]),
        "wq": inputs["L0_attn_wq"], "wk": inputs["L0_attn_wk"],
        "wv": inputs["L0_attn_wv"],
        "w_freq_real": inputs["L0_ffn_w_freq_real"],
        "w_freq_imag": inputs["L0_ffn_w_freq_imag"]}),
        core_ids=list(range(NCORES))).results
    M = np.concatenate([np.asarray(r0[c]["m_slice"]) for c in range(NCORES)], 0)

    nc = build_p1()
    def p1_in(c):
        qg = np.stack([np.asarray(r0[j]["q_a2a"]).reshape(NCORES, 128, TOK)[c]
                       for j in range(NCORES)])
        kg = np.stack([np.asarray(r0[j]["k_a2a"]).reshape(NCORES, 128, TOK)[c]
                       for j in range(NCORES)])
        vg = np.stack([np.asarray(r0[j]["v_a2a"]).reshape(NCORES, TOK, 128)[c]
                       for j in range(NCORES)])
        return {"qg": qg, "kg": kg, "vg": vg}
    r1 = run_bass_kernel_spmd(nc, cmaps(p1_in), core_ids=list(range(NCORES))).results

    A_log = np.asarray(inputs["L1_attn_A_log"], np.float32)
    alog_lanes = np.ascontiguousarray(
        A_log.reshape(16, 16, 8, 16).transpose(2, 3, 0, 1).reshape(128, 256))
    d_col = np.ascontiguousarray(
        np.asarray(inputs["L1_attn_D"], np.float32).reshape(NTILE, 128).T)
    nc = build_p234()
    def p234_in(c):
        ao = np.stack([np.asarray(r1[j]["ao_a2a"]).reshape(NCORES, 128, TOK)[c]
                       for j in range(NCORES)])
        if c % 4 != 0:
            ao_h = np.stack([np.asarray(r1[j]["ao_a2a"])
                             .reshape(NCORES, 128, TOK)[c - 1][:, -3:]
                             for j in range(NCORES)])
            x_h = np.ascontiguousarray(x_full[c * TOK - 3:c * TOK])
            hmask = np.ones((3, 1), np.float32)
        else:
            ao_h = np.zeros((NCORES, 128, 3), r1[0]["ao_a2a"].dtype)
            x_h = np.zeros((3, D), np.float32)
            hmask = np.zeros((3, 1), np.float32)
        return {"x": np.ascontiguousarray(x_full[c * TOK:(c + 1) * TOK]),
                "ao": ao, "ao_h": np.ascontiguousarray(ao_h), "x_h": x_h,
                "hmask": hmask, "wo": inputs["L0_attn_wo"], "M": M,
                "ffn_bias": inputs["L0_ffn_bias"],
                "W_in": inputs["L1_attn_W_in"], "conv_w": inputs["L1_attn_conv_w"],
                "conv_b": inputs["L1_attn_conv_b"], "W_x": inputs["L1_attn_W_x"],
                "W_dt": inputs["L1_attn_W_dt"], "b_dt": inputs["L1_attn_b_dt"],
                "e8sel": e8, "wsel": ws, "e16sel": e16,
                "alog_lanes": alog_lanes}
    r34 = run_bass_kernel_spmd(nc, cmaps(p234_in), core_ids=list(range(NCORES))).results
    r2 = r34

    fq_all = [np.asarray(r34[j]["fq"]) for j in range(NCORES)]
    nc = build_p45()
    def p45_in(c):
        g0 = 4 * (c // 4)
        oh4 = np.zeros((128, 4), np.float32)
        oh4[:, c % 4] = 1.0
        return {"x2": np.asarray(r2[c]["x2"]),
                "fq_g": np.stack(fq_all[g0:g0 + 4]),
                "oh4": oh4,
                "cd": np.asarray(r34[c]["cd"]),
                "y_local": np.asarray(r34[c]["y_local"]),
                "sres": np.asarray(r34[c]["sres"]),
                "bc": np.asarray(r34[c]["bc"]),
                "e8sel": e8, "wsel": ws, "e16sel": e16,
                "alog_lanes": alog_lanes, "d_col": d_col,
                "W_out": inputs["L1_attn_W_out"],
                "router_w": inputs["L1_ffn_router_w"],
                "corr_bias": inputs["L1_ffn_correction_bias"],
                "gate_w": inputs["L1_ffn_gate_w"], "up_w": inputs["L1_ffn_up_w"],
                "down_w": inputs["L1_ffn_down_w"],
                "shared_gate_w": inputs["L1_ffn_shared_gate_w"],
                "shared_up_w": inputs["L1_ffn_shared_up_w"],
                "shared_down_w": inputs["L1_ffn_shared_down_w"], "esel": esel}
    r5 = run_bass_kernel_spmd(nc, cmaps(p45_in), core_ids=list(range(NCORES))).results

    out = np.concatenate([np.asarray(r5[c]["out"]) for c in range(NCORES)], 0)
    return out.reshape(BATCH, T_FULL, D).astype(np.float32)

